# version 02
# baseline (speedup 1.0000x reference)
import numpy as np
import ml_dtypes
from contextlib import ExitStack

import concourse.bass as bass
import concourse.mybir as mybir
from concourse.bass_utils import run_bass_kernel_spmd

F32 = mybir.dt.float32
BF16 = mybir.dt.bfloat16
U8 = mybir.dt.uint8
I8 = mybir.dt.int8
AF = mybir.ActivationFunctionType
ALU = mybir.AluOpType
AX = mybir.AxisListType

B, S, D = 4, 2048, 2048
II = 8192
NH = 16
TPC = 1024
T = 512
NT = TPC // T
DC = D // 128
IC = II // 128
SCALE = 1.0 / float(np.sqrt(128.0))
RG = [[0, 1], [2, 3], [4, 5], [6, 7]]
TCH = 1024


class Sem:
    def __init__(self, es, nc, name):
        self.h = es.enter_context(nc.semaphore(name))
        self.n = 0

    def inc(self, ins, k=1):
        ins.then_inc(self.h, k)
        self.n += k
        return self.n


def build(n_layers=2, l0_out=False, stage="full"):
    nc = bass.Bass("TRN2", target_bir_lowering=False, debug=False,
                   detect_race_conditions=False)

    def din(name, shape, dt):
        return nc.dram_tensor(name, list(shape), dt, kind="ExternalInput").ap()

    def dint(name, shape, dt):
        return nc.dram_tensor(name, list(shape), dt, kind="Internal").ap()

    x0 = din("x0", [D, TPC], F32)
    w0qk = din("w0qk", [32, 128, D], F32)
    w0v = din("w0v", [D, D], F32)
    w0o = din("w0o", [16, 128, D], F32)
    w0g = din("w0g", [64, 128, D], U8)
    w0u = din("w0u", [64, 128, D], U8)
    w0d = din("w0d", [16, 128, II], U8)
    gs_col = din("gs_col", [128, 65], F32)
    us_col = din("us_col", [128, 65], F32)
    ds_col = din("ds_col", [128, 17], F32)
    w1qk = din("w1qk", [32, 128, D], F32)
    w1v = din("w1v", [D, D], F32)
    w1o = din("w1o", [16, 128, D], F32)
    w1g = din("w1g", [64, 128, D], F32)
    w1u = din("w1u", [64, 128, D], F32)
    w1d = din("w1d", [16, 128, II], F32)
    qkvb_col = din("qkvb_col", [128, 33], F32)
    bv_row = din("bv_row", [1, D], F32)
    ob_col = din("ob_col", [128, 17], F32)
    gam_col = din("gam_col", [128, 65], F32)
    mask_d = din("mask_d", [32, 128, T], BF16)
    ident_in = din("ident_in", [128, 128], F32)

    out = nc.dram_tensor("out", [D, TPC], F32, kind="ExternalOutput").ap()
    xres = dint("xres", [D, TPC], F32)
    cck_i = dint("cck_i", [D + 16, TPC], BF16)
    ccv_i = dint("ccv_i", [TPC, D], BF16)
    cck_o = dint("cck_o", [2 * (D + 16), TPC], BF16)
    ccv_o = dint("ccv_o", [2 * TPC, D], BF16)
    tg_d = dint("tg_d", [64, 128, D], I8)
    tu_d = dint("tu_d", [64, 128, D], I8)
    td_d = dint("td_d", [16, 128, II], I8)
    rowspill = dint("rowspill", [64, T], F32)
    dummy_d = dint("dummy_d", [1, 64], F32)

    es = ExitStack()
    with es:
        def sbt(name, shape, dt):
            return es.enter_context(nc.sbuf_tensor(name, list(shape), dt))

        xn = [sbt(f"xn{t}", [128, DC * T], BF16) for t in range(NT)]
        qg = sbt("qg", [128, DC * T], BF16)
        har = sbt("har", [128, IC * T], BF16)
        war = sbt("war", [128, 4 * D], BF16)
        wvb = sbt("wvb", [128, 2 * T], BF16)
        kbuf = sbt("kbuf", [128, 2 * TPC], BF16)
        vbuf = sbt("vbuf", [128, 2 * TPC], BF16)
        pbuf = sbt("pbuf", [128, 3 * T], BF16)
        qhb = sbt("qhb", [128, 2 * T], BF16)
        bct = sbt("bct", [128, 4 * T], F32)
        xck = sbt("xck", [128, 2 * T], F32)
        ftmp = sbt("ftmp", [128, 2 * T], F32)
        mrun = sbt("mrun", [128, 2 * T], F32)
        mcols = sbt("mcols", [128, 48], F32)
        bfs = sbt("bfs", [128, 3 * T], BF16)
        kvst = sbt("kvst", [128, 2 * T], BF16)
        terni = sbt("terni", [128, TCH], F32)
        ternm = sbt("ternm", [128, TCH], F32)
        terno = sbt("terno", [128, TCH], I8)
        acc = sbt("acc", [128, 256], F32)
        qi8 = sbt("qi8", [128, 2 * T], I8)
        mbuf = sbt("mbuf", [128, 2 * T], BF16)
        ident = sbt("ident", [128, 128], F32)
        gsc = sbt("gsc", [128, 65], F32)
        usc = sbt("usc", [128, 65], F32)
        dsc = sbt("dsc", [128, 17], F32)
        gam = sbt("gam", [128, 65], F32)
        qkvb = sbt("qkvb", [128, 33], F32)
        obc = sbt("obc", [128, 17], F32)
        bvr = sbt("bvr", [1, D], BF16)
        onesrow_f = sbt("onesrow_f", [1, 128], F32)
        onesrow_bf = sbt("onesrow_bf", [1, 128], BF16)
        m128row = sbt("m128row", [1, 128], BF16)
        onescol_f = sbt("onescol_f", [128, 1], F32)
        onescol_bf = sbt("onescol_bf", [128, 1], BF16)
        biascol = sbt("biascol", [128, 1], F32)
        sccol = sbt("sccol", [128, 1], F32)
        swsp = [sbt(f"swsp{i}", [128, 1], F32) for i in range(3)]
        rows = [sbt(f"row{i}", [1, T], F32) for i in range(7)]
        srow_bf = [sbt(f"rbf{i}", [1, T], BF16) for i in range(2)]
        ccm_sb = sbt("ccm_sb", [1, 64], BF16)
        knm = sbt("knm", [1, 32], F32)
        sc1 = sbt("sc1", [1, 8], F32)
        wmean = sbt("wmean", [1, 8], F32)
        wrecip = [sbt(f"wrecip{i}", [1, 1], F32) for i in range(3)]
        dummy_sb = sbt("dummy_sb", [1, 64], F32)

        sbuf_left = nc.sbuf_bytes_remaining

        def psm(name, shape):
            return es.enter_context(nc.psum_tensor(name, list(shape), F32))

        pw = [psm(f"pw{i}", [128, T]) for i in range(6)]
        pr0 = psm("pr0", [1, T])
        pr1 = psm("pr1", [1, T])

        s_cc = Sem(es, nc, "s_cc")
        s_t = Sem(es, nc, "s_t")
        s_v = Sem(es, nc, "s_v")
        s_s = Sem(es, nc, "s_s")
        s_mk = Sem(es, nc, "s_mk")
        sP = Sem(es, nc, "sP")
        sP2 = Sem(es, nc, "sP2")
        sX = [Sem(es, nc, f"sX{i}") for i in range(2)]
        sT = Sem(es, nc, "sT")
        sM = [Sem(es, nc, f"sM{i}") for i in range(2)]
        sKB = [Sem(es, nc, f"sKB{i}") for i in range(2)]
        sVB = [Sem(es, nc, f"sVB{i}") for i in range(2)]
        sCM = Sem(es, nc, "sCM")
        sU = Sem(es, nc, "sU")
        sKS = [Sem(es, nc, f"sKS{i}") for i in range(2)]
        sVS = [Sem(es, nc, f"sVS{i}") for i in range(2)]
        sUP = [Sem(es, nc, f"sUP{i}") for i in range(2)]
        sSP = Sem(es, nc, "sSP")
        sTS = Sem(es, nc, "sTS")
        sKN = Sem(es, nc, "sKN")
        sW = [Sem(es, nc, f"sW{i}") for i in range(4)]
        sWV = [Sem(es, nc, f"sWV{i}") for i in range(2)]

        blk = es.enter_context(nc.Block())

        def TEN(f): blk.tensor(f)
        def VEC(f): blk.vector(f)
        def SCA(f): blk.scalar(f)
        def SYN(f): blk.sync(f)
        def GPS(f): blk.gpsimd(f)

        def marker(deps):
            vals = [(d.h, d.n) for d in deps]

            def _(sy):
                for h, v in vals:
                    sy.wait_ge(h, v)
                s_mk.inc(sy.dma_start(out=dummy_sb[:], in_=dummy_d), 16)
            SYN(_)
            return s_mk.n

        def barrier(allow_safe=True):
            tv, vv, sv = s_t.n, s_v.n, s_s.n

            def _(tn):
                tn.wait_ge(s_v.h, vv)
                tn.wait_ge(s_s.h, sv)
            TEN(_)

            def _(v):
                v.wait_ge(s_t.h, tv)
                v.wait_ge(s_s.h, sv)
            VEC(_)

            def _(sc):
                sc.wait_ge(s_t.h, tv)
                sc.wait_ge(s_v.h, vv)
            SCA(_)
            if allow_safe and tern_pos[0] < len(tern_steps) \
                    and tern_steps[tern_pos[0]][0] == "safe":
                pump(4, safe=True)

        def _(sy):
            sP.inc(sy.dma_start(out=gsc[:], in_=gs_col), 16)
            sP.inc(sy.dma_start(out=usc[:], in_=us_col), 16)
            sP.inc(sy.dma_start(out=dsc[:], in_=ds_col), 16)
            sP.inc(sy.dma_start(out=gam[:], in_=gam_col), 16)
            sP.inc(sy.dma_start(out=qkvb[:], in_=qkvb_col), 16)
            sP.inc(sy.dma_start(out=obc[:], in_=ob_col), 16)
            sP.inc(sy.dma_start(out=ident[:], in_=ident_in), 16)
        SYN(_)
        setup_sy = sP.n

        def _(g):
            sP2.inc(g.dma_start(out=bvr[:], in_=bv_row), 16)
        GPS(_)
        setup_gp = sP2.n

        def _(v):
            s_v.inc(v.memset(kvst[:], 0.0))
        VEC(_)
        zv = s_v.n

        def _(sy):
            sy.wait_ge(s_v.h, zv)
            for rr in range(16):
                sP.inc(sy.dma_start(out=cck_i[D + rr:D + rr + 1, :],
                                    in_=kvst[0:1, :]), 16)
        SYN(_)

        def _(v):
            v.wait_ge(sP.h, setup_sy)
            v.memset(onesrow_f[:], 1.0)
            v.memset(onesrow_bf[:], 1.0)
            v.memset(m128row[:], -128.0)
            v.memset(onescol_f[:], 1.0)
            v.memset(sccol[:], SCALE)
            s_v.inc(v.memset(onescol_bf[:], 1.0))
        VEC(_)

        tern_steps = []
        tern_abs_cnt = {}
        tern_absred = {}
        tern_p2_cnt = {}
        tern_st_cnt = {}
        tern_marker = {}
        tern_state = {}

        def build_tern():
            if n_layers < 2:
                return
            TC2 = TCH // 2
            specs = [(0, w1g, tg_d), (1, w1u, tu_d), (2, w1d, td_d)]
            for wi, wsrc, wdst in specs:
                Mo, K = wsrc.shape[0], wsrc.shape[2]
                ncols = K // TC2
                nch = Mo * ncols
                NEL = float(Mo * 128 * K)

                def mean_chunk(ci, wi=wi, wsrc=wsrc, ncols=ncols, nch=nch):
                    mo, co = divmod(ci, ncols)
                    hf = ci % 2
                    my = [0]

                    def _(sy):
                        if ci >= 2:
                            sy.wait_ge(s_v.h, tern_absred[(wi, ci - 2)])
                        elif wi >= 1:
                            sy.wait_ge(s_v.h, tern_p2_cnt[(wi - 1, 256 - 1)])
                        my[0] = sT.inc(sy.dma_start(
                            out=terni[:, hf * TC2:(hf + 1) * TC2],
                            in_=wsrc[mo, :, co * TC2:(co + 1) * TC2]), 16)
                    SYN(_)

                    def _(v):
                        v.wait_ge(sT.h, my[0])
                        tern_absred[(wi, ci)] = s_v.inc(v.reduce_sum(
                            out=acc[:, ci:ci + 1],
                            in_=terni[:, hf * TC2:(hf + 1) * TC2],
                            axis=AX.X, apply_absolute_value=True))
                    VEC(_)
                for ci in range(nch):
                    tern_steps.append(("any", lambda ci=ci, f=mean_chunk: f(ci)))

                def finish(wi=wi, nch=nch, NEL=NEL):
                    need_v = tern_absred[(wi, nch - 1)]
                    vpre, tpre = s_v.n, s_t.n

                    def _(v):
                        v.wait_ge(s_v.h, need_v)
                        v.wait_ge(s_t.h, tpre)
                        c = s_v.inc(v.reduce_sum(out=mcols[:, 6:7], in_=acc[:, 0:nch],
                                                 axis=AX.X))
                        v.wait_ge(s_v.h, c)
                        s_v.inc(v.tensor_copy(out=mcols[:, 7:8], in_=mcols[:, 6:7]))
                    VEC(_)
                    v1 = s_v.n

                    def _(tn):
                        tn.wait_ge(s_v.h, v1)
                        s_t.inc(tn.matmul(pr0[0:1, 0:1], onescol_f[:], mcols[:, 7:8],
                                          start=True, stop=True))
                    TEN(_)
                    t1 = s_t.n

                    def _(v):
                        v.wait_ge(s_t.h, t1)
                        v.tensor_scalar(out=wmean[0:1, wi + 1:wi + 2],
                                        in0=pr0[0:1, 0:1],
                                        scalar1=1.0 / (NEL * 127.0),
                                        scalar2=1e-5 / 127.0,
                                        op0=ALU.mult, op1=ALU.max)
                        c = s_v.inc(v.tensor_scalar(out=sc1[0:1, 5:6],
                                                    in0=pr0[0:1, 0:1],
                                                    scalar1=1.0 / NEL, scalar2=1e-5,
                                                    op0=ALU.mult, op1=ALU.max))
                        v.wait_ge(s_v.h, c)
                        s_v.inc(v.reciprocal(out=wrecip[wi][:], in_=sc1[0:1, 5:6]))
                    VEC(_)
                    v2 = s_v.n

                    def _(tn):
                        tn.wait_ge(s_v.h, v2)
                        s_t.inc(tn.matmul(pw[5][:, 0:1], onesrow_f[:], wrecip[wi][:],
                                          start=True, stop=True))
                    TEN(_)
                    t2 = s_t.n

                    def _(v):
                        v.wait_ge(s_t.h, t2)
                        s_v.inc(v.tensor_copy(out=swsp[wi][:], in_=pw[5][:, 0:1]))
                    VEC(_)
                tern_steps.append(("safe", finish))

                def tern_chunk(ci, wi=wi, wsrc=wsrc, wdst=wdst, ncols=ncols, nch=nch):
                    mo, co = divmod(ci, ncols)
                    hf = ci % 2
                    sl = slice(hf * TC2, (hf + 1) * TC2)
                    my = [0]

                    def _(sy):
                        if ci >= 2:
                            sy.wait_ge(s_v.h, tern_p2_cnt[(wi, ci - 2)])
                        else:
                            sy.wait_ge(s_v.h, tern_absred[(wi, nch - 1)])
                        my[0] = sT.inc(sy.dma_start(
                            out=terni[:, sl],
                            in_=wsrc[mo, :, co * TC2:(co + 1) * TC2]), 16)
                    SYN(_)

                    def _(v):
                        v.wait_ge(sT.h, my[0])
                        if ci >= 2:
                            v.wait_ge(sTS.h, tern_st_cnt[(wi, ci - 2)])
                        v.tensor_scalar(out=ternm[:, sl],
                                        in0=terni[:, sl],
                                        scalar1=swsp[wi][:], scalar2=None, op0=ALU.mult)
                        tern_p2_cnt[(wi, ci)] = s_v.inc(v.tensor_scalar(
                            out=terno[:, sl], in0=ternm[:, sl],
                            scalar1=-1.4375, scalar2=1.4375, op0=ALU.max, op1=ALU.min))
                    VEC(_)
                    v1 = s_v.n

                    def _(sy):
                        sy.wait_ge(s_v.h, v1)
                        tern_st_cnt[(wi, ci)] = sTS.inc(sy.dma_start(
                            out=wdst[mo, :, co * TC2:(co + 1) * TC2],
                            in_=terno[:, sl]), 16)
                    SYN(_)
                for ci in range(nch):
                    tern_steps.append(("any", lambda ci=ci, f=tern_chunk: f(ci)))

                def endmark(wi=wi):
                    tern_marker[wi] = marker([sTS])
                tern_steps.append(("any", endmark))

        build_tern()
        tern_pos = [0]

        def pump(n, safe=False):
            k = 0
            while k < n and tern_pos[0] < len(tern_steps):
                tag, fn = tern_steps[tern_pos[0]]
                if tag == "safe" and not safe:
                    break
                fn()
                tern_pos[0] += 1
                k += 1

        def pump_flush():
            pump(len(tern_steps), safe=True)

        spill_idx = {}
        spill_mk = {}

        def spill(row_ap, key, v_cnt):
            idx = spill_idx.setdefault(key, len(spill_idx))
            assert idx < 64

            def _(sy):
                sy.wait_ge(s_v.h, v_cnt)
                sSP.inc(sy.dma_start(out=rowspill[idx:idx + 1, :], in_=row_ap), 16)
            SYN(_)
            spill_mk[key] = None

        def unspill(row_ap, key):
            idx = spill_idx[key]
            pre = sSP.n
            res = [0]

            def _(sy):
                sy.wait_ge(sSP.h, pre)
                res[0] = sU.inc(sy.dma_start(out=row_ap, in_=rowspill[idx:idx + 1, :]), 16)
            SYN(_)
            return res[0]

        war_free = [0, 0, 0, 0]
        warslot = [0]
        kst_gate = [0, 0]
        vst_gate = [0, 0]
        wv_free = [0, 0]
        qh_free = [0, 0]
        exp_free = [0, 0]
        pv_free = [0, 0, 0]
        mask_free = [0, 0]
        kv_free = [0, 0]
        upd_free = [0, 0]
        layer_e_marker = [0, 0]

        def load_war(src_ap):
            slot = warslot[0]
            warslot[0] = (slot + 1) % 4
            free = war_free[slot]
            cnt = [0]

            def _(g):
                if free:
                    g.wait_ge(s_t.h, free)
                cnt[0] = sW[slot].inc(g.dma_start(out=war[:, slot * D:(slot + 1) * D],
                                                  in_=src_ap), 16)
            GPS(_)
            return slot, cnt[0]

        def wsl(slot, dm):
            return war[:, slot * D + dm * 128: slot * D + (dm + 1) * 128]

        xsrcs = [x0, xres]
        xdsts = [xres, out]
        if n_layers == 1 or l0_out:
            xdsts[0] = out

        def emit_layer(l):
            L1 = (l == 1)
            xsrc, xdst = xsrcs[l], xdsts[l]
            wqk = w1qk if L1 else w0qk
            wvp = w1v if L1 else w0v
            wop = w1o if L1 else w0o
            wg_t = tg_d if L1 else w0g
            wu_t = tu_d if L1 else w0u
            wd_t = td_d if L1 else w0d

            if L1:
                pump_flush()
                mask_free[0] = max(mask_free[0], s_v.n)
                mask_free[1] = max(mask_free[1], s_v.n)
                barrier()
                xmark = layer_e_marker[0]

                def _(sy):
                    sy.wait_ge(s_mk.h, xmark)
                SYN(_)

            def _(v):
                s_v.inc(v.memset(knm[0:1, 0:32], 0.0))
            VEC(_)

            for t in range(NT):
                barrier()
                for f in range(DC):
                    slot = f % 2
                    my = [0]
                    tpre = s_t.n

                    def _(sy, f=f, slot=slot, my=my, tpre=tpre):
                        sy.wait_ge(s_t.h, tpre)
                        my[0] = sX[slot].inc(sy.dma_start(
                            out=xck[:, slot * T:(slot + 1) * T],
                            in_=xsrc[f * 128:(f + 1) * 128, t * T:(t + 1) * T]), 16)
                    SYN(_)

                    def _(sc, slot=slot, my=my):
                        sc.wait_ge(sX[slot].h, my[0])
                        s_s.inc(sc.activation(out=ftmp[:, slot * T:(slot + 1) * T],
                                              in_=xck[:, slot * T:(slot + 1) * T],
                                              func=AF.Square))
                    SCA(_)
                    mys = s_s.n

                    def _(tn, f=f, slot=slot, my=my, mys=mys):
                        tn.wait_ge(sX[slot].h, my[0])
                        tn.matmul(pr0[:], onescol_f[:], xck[:, slot * T:(slot + 1) * T],
                                  start=(f == 0), stop=(f == DC - 1),
                                  skip_group_check=True)
                        tn.wait_ge(s_s.h, mys)
                        s_t.inc(tn.matmul(pr1[:], onescol_f[:],
                                          ftmp[:, slot * T:(slot + 1) * T],
                                          start=(f == 0), stop=(f == DC - 1),
                                          skip_group_check=True))
                    TEN(_)
                stats_t = s_t.n

                def _(v, stats_t=stats_t):
                    v.wait_ge(s_t.h, stats_t)
                    v.tensor_scalar(out=rows[0][:], in0=pr0[:], scalar1=1.0 / D,
                                    scalar2=None, op0=ALU.mult)
                    v.tensor_mul(out=rows[2][:], in0=rows[0][:], in1=rows[0][:])
                    s_v.inc(v.scalar_tensor_tensor(out=rows[2][:], in0=pr1[:],
                                                   scalar=1.0 / D, in1=rows[2][:],
                                                   op0=ALU.mult, op1=ALU.subtract))
                VEC(_)
                v1 = s_v.n

                def _(v):
                    s_v.inc(v.tensor_scalar(out=rows[3][:], in0=rows[2][:],
                                            scalar1=1e-5, scalar2=None, op0=ALU.add))
                VEC(_)
                v1b = s_v.n

                def _(sc, v1b=v1b):
                    sc.wait_ge(s_v.h, v1b)
                    s_s.inc(sc.activation(out=rows[2][:], in_=rows[3][:], func=AF.Sqrt))
                SCA(_)
                s1 = s_s.n

                def _(v, s1=s1):
                    v.wait_ge(s_s.h, s1)
                    s_v.inc(v.reciprocal(out=rows[1][:], in_=rows[2][:]))
                VEC(_)
                vr = s_v.n

                def _(tn, vr=vr):
                    tn.wait_ge(s_v.h, vr)
                    tn.matmul(pw[4][:], onesrow_f[:], rows[0][:], start=True, stop=True)
                    s_t.inc(tn.matmul(pw[5][:], onesrow_f[:], rows[1][:],
                                      start=True, stop=True))
                TEN(_)
                t1 = s_t.n

                def _(v, t1=t1):
                    v.wait_ge(s_t.h, t1)
                    v.tensor_copy(out=bct[:, 0:T], in_=pw[4][:])
                    s_v.inc(v.tensor_copy(out=bct[:, T:2 * T], in_=pw[5][:]))
                VEC(_)
                bc_ready = s_v.n
                spill(rows[0][:], ("m", l, t), bc_ready)

                if L1:
                    def _(v, bc_ready=bc_ready):
                        v.wait_ge(s_v.h, bc_ready)
                        v.memset(mrun[:, 0:T], -1e30)
                        s_v.inc(v.memset(mrun[:, T:2 * T], 1e30))
                    VEC(_)
                for f in range(DC):
                    slot = f % 2
                    my = [0]
                    vpre = s_v.n

                    def _(sy, f=f, slot=slot, my=my, vpre=vpre):
                        sy.wait_ge(s_v.h, vpre)
                        my[0] = sX[slot].inc(sy.dma_start(
                            out=xck[:, slot * T:(slot + 1) * T],
                            in_=xsrc[f * 128:(f + 1) * 128, t * T:(t + 1) * T]), 16)
                    SYN(_)

                    def _(v, f=f, slot=slot, my=my, t=t, bc_ready=bc_ready):
                        v.wait_ge(sX[slot].h, my[0])
                        v.wait_ge(s_v.h, bc_ready)
                        v.tensor_sub(out=ftmp[:, slot * T:(slot + 1) * T],
                                     in0=xck[:, slot * T:(slot + 1) * T],
                                     in1=bct[:, 0:T])
                        ins = v.tensor_mul(out=xn[t][:, f * T:(f + 1) * T],
                                           in0=ftmp[:, slot * T:(slot + 1) * T],
                                           in1=bct[:, T:2 * T])
                        if L1:
                            v.tensor_tensor(out=mrun[:, 0:T],
                                            in0=ftmp[:, slot * T:(slot + 1) * T],
                                            in1=mrun[:, 0:T], op=ALU.max)
                            ins = v.tensor_tensor(out=mrun[:, T:2 * T],
                                                  in0=ftmp[:, slot * T:(slot + 1) * T],
                                                  in1=mrun[:, T:2 * T], op=ALU.min)
                        s_v.inc(ins)
                    VEC(_)
                xn_ready = s_v.n
                if stage == "LN" + str(t):
                    return

                if L1:
                    def _(v, xn_ready=xn_ready):
                        v.wait_ge(s_v.h, xn_ready)
                        v.tensor_scalar(out=mrun[:, T:2 * T], in0=mrun[:, T:2 * T],
                                        scalar1=-1.0, scalar2=None, op0=ALU.mult)
                        s_v.inc(v.tensor_tensor(out=mrun[:, 0:T], in0=mrun[:, 0:T],
                                                in1=mrun[:, T:2 * T], op=ALU.max))
                    VEC(_)
                    mv = s_v.n
                    for tb in range(4):
                        def _(tn, tb=tb, mv=mv):
                            tn.wait_ge(s_v.h, mv)
                            s_t.inc(tn.transpose(pw[tb][:, 0:128],
                                                 mrun[:, tb * 128:(tb + 1) * 128],
                                                 ident[:]))
                        TEN(_)
                    t2 = s_t.n
                    mcbase = mcol_ctr[0]
                    mcol_ctr[0] += 4

                    def _(v, t2=t2, mcbase=mcbase):
                        v.wait_ge(s_t.h, t2)
                        for tb in range(4):
                            ins = v.reduce_max(out=mcols[:, mcbase + tb:mcbase + tb + 1],
                                               in_=pw[tb][:, 0:128], axis=AX.X)
                        s_v.inc(ins)
                    VEC(_)
                    v2 = s_v.n
                    for tb in range(4):
                        def _(tn, tb=tb, v2=v2, mcbase=mcbase):
                            tn.wait_ge(s_v.h, v2)
                            s_t.inc(tn.matmul(pr0[0:1, tb * 128:(tb + 1) * 128],
                                              mcols[:, mcbase + tb:mcbase + tb + 1],
                                              ident[:], start=True, stop=True,
                                              skip_group_check=True))
                        TEN(_)
                    t3 = s_t.n

                    def _(v, t3=t3):
                        v.wait_ge(s_t.h, t3)
                        v.tensor_copy(out=rows[3][:], in_=pr0[:])
                        v.tensor_mul(out=rows[4][:], in0=rows[3][:], in1=rows[1][:])
                        v.tensor_scalar(out=rows[4][:], in0=rows[4][:], scalar1=1e-5,
                                        scalar2=None, op0=ALU.max)
                        v.reciprocal(out=rows[5][:], in_=rows[4][:])
                        s_v.inc(v.scalar_tensor_tensor(out=rows[5][:], in0=rows[5][:],
                                                       scalar=127.0, in1=rows[1][:],
                                                       op0=ALU.mult, op1=ALU.mult))
                    VEC(_)
                    v3 = s_v.n
                    spill(rows[4][:], ("mx", l, t), v3)
                    spill(rows[5][:], ("qscT", l, t), v3)
                else:
                    for f in range(DC):
                        def _(tn, f=f, t=t, xn_ready=xn_ready):
                            tn.wait_ge(s_v.h, xn_ready)
                            s_t.inc(tn.matmul(pr1[:], onescol_bf[:],
                                              xn[t][:, f * T:(f + 1) * T],
                                              start=(f == 0), stop=(f == DC - 1),
                                              skip_group_check=True))
                        TEN(_)
                    t2 = s_t.n

                    def _(v, t2=t2, t_=t):
                        v.wait_ge(s_t.h, t2)
                        s_v.inc(v.tensor_copy(out=srow_bf[t_][:], in_=pr1[:]))
                    VEC(_)

                for hmo in range(16):
                    slot, gcnt = load_war(wqk[16 + hmo])
                    vpre = s_v.n

                    def _(tn, slot=slot, gcnt=gcnt, t=t, xn_ready=xn_ready, vpre=vpre):
                        tn.wait_ge(sW[slot].h, gcnt)
                        tn.wait_ge(s_v.h, vpre)
                        for dm in range(DC):
                            ins = tn.matmul(pw[0][:], wsl(slot, dm),
                                            xn[t][:, dm * T:(dm + 1) * T],
                                            start=(dm == 0), stop=(dm == DC - 1),
                                            skip_group_check=True)
                        s_t.inc(ins)
                    TEN(_)
                    war_free[slot] = s_t.n
                    t1 = s_t.n
                    ks = hmo % 2

                    def _(v, hmo=hmo, t1=t1, ks=ks):
                        v.wait_ge(s_t.h, t1)
                        v.wait_ge(sKS[ks].h, kst_gate[ks])
                        v.wait_ge(sVS[ks].h, vst_gate[ks])
                        b = qkvb[:, 17 + hmo:18 + hmo] if L1 else 0.0
                        v.tensor_scalar(out=kvst[:, ks * T:(ks + 1) * T],
                                        in0=pw[0][:], scalar1=b, scalar2=None,
                                        op0=ALU.add)
                        s_v.inc(v.tensor_mul(out=bfs[:, 0:T],
                                             in0=kvst[:, ks * T:(ks + 1) * T],
                                             in1=kvst[:, ks * T:(ks + 1) * T]))
                    VEC(_)
                    s2 = s_v.n
                    v2 = s_v.n

                    def _(tn, v2=v2):
                        tn.wait_ge(s_v.h, v2)
                        s_t.inc(tn.matmul(pr1[:], onescol_bf[:], bfs[:, 0:T],
                                          start=True, stop=True, skip_group_check=True))
                    TEN(_)
                    t3 = s_t.n

                    def _(v, hmo=hmo, t3=t3):
                        v.wait_ge(s_t.h, t3)
                        c = s_v.inc(v.reduce_max(out=sc1[0:1, 4:5], in_=pr1[:],
                                                 axis=AX.X))
                        v.wait_ge(s_v.h, c)
                        s_v.inc(v.tensor_tensor(out=knm[0:1, hmo:hmo + 1],
                                                in0=knm[0:1, hmo:hmo + 1],
                                                in1=sc1[0:1, 4:5], op=ALU.max))
                    VEC(_)

                    def _(sy, hmo=hmo, s2=s2, ks=ks, t=t):
                        sy.wait_ge(s_v.h, s2)
                        kst_gate[ks] = sKS[ks].inc(sy.dma_start(
                            out=cck_i[hmo * 128:(hmo + 1) * 128, t * T:(t + 1) * T],
                            in_=kvst[:, ks * T:(ks + 1) * T]), 16)
                    SYN(_)
                    pump(6)
                if stage == "K" + str(t):
                    return

                prev_vevac = [s_v.n]
                for cg in range(4):
                    for dm in range(DC):
                        slot = dm % 2
                        my = [0]
                        free = wv_free[slot]

                        def _(g, dm=dm, cg=cg, slot=slot, my=my, free=free):
                            if free:
                                g.wait_ge(s_t.h, free)
                            my[0] = sWV[slot].inc(g.dma_start(
                                out=wvb[:, slot * T:(slot + 1) * T],
                                in_=wvp[dm * 128:(dm + 1) * 128, cg * T:(cg + 1) * T]), 16)
                        GPS(_)

                        def _(tn, dm=dm, slot=slot, my=my, t=t, xn_ready=xn_ready,
                              pe=prev_vevac[0]):
                            tn.wait_ge(sWV[slot].h, my[0])
                            if dm == 0:
                                tn.wait_ge(s_v.h, max(xn_ready, pe))
                            last = (dm == DC - 1) and (not L1)
                            for tb in range(4):
                                ins = tn.matmul(
                                    pw[tb][:],
                                    xn[t][:, dm * T + tb * 128: dm * T + (tb + 1) * 128],
                                    wvb[:, slot * T:(slot + 1) * T],
                                    start=(dm == 0), stop=last, skip_group_check=True)
                            s_t.inc(ins)
                        TEN(_)
                        wv_free[slot] = s_t.n
                    if L1:
                        def _(tn, cg=cg):
                            tn.wait_ge(sP2.h, setup_gp)
                            for tb in range(4):
                                ins = tn.matmul(pw[tb][:], onesrow_bf[:],
                                                bvr[0:1, cg * T:(cg + 1) * T],
                                                start=False, stop=True,
                                                skip_group_check=True)
                            s_t.inc(ins)
                        TEN(_)
                    t4 = s_t.n
                    for tb in range(4):
                        vs = tb % 2
                        if "x" in stage:
                            continue

                        def _(v, tb=tb, t4=t4, vs=vs):
                            v.wait_ge(s_t.h, t4)
                            v.wait_ge(sVS[vs].h, vst_gate[vs])
                            v.wait_ge(sKS[vs].h, kst_gate[vs])
                            s_v.inc(v.tensor_copy(out=kvst[:, vs * T:(vs + 1) * T],
                                                  in_=pw[tb][:]))
                        VEC(_)
                        v4 = s_v.n
                        if "y" in stage:
                            continue

                        def _(sy, tb=tb, cg=cg, v4=v4, vs=vs, t=t):
                            sy.wait_ge(s_v.h, v4)
                            vst_gate[vs] = sVS[vs].inc(sy.dma_start(
                                out=ccv_i[t * T + tb * 128: t * T + (tb + 1) * 128,
                                          cg * T:(cg + 1) * T],
                                in_=kvst[:, vs * T:(vs + 1) * T]), 16)
                        SYN(_)
                    prev_vevac[0] = s_v.n
                    pump(8)
                if stage.startswith("V") and stage.endswith(str(t)):
                    return

            vpre = s_v.n

            def _(g, vpre=vpre):
                g.wait_ge(s_v.h, vpre)
                sKN.inc(g.dma_start(out=cck_i[D:D + 1, 0:32], in_=knm[0:1, 0:32]), 16)
            GPS(_)
            ccmark = marker([sKS[0], sKS[1], sVS[0], sVS[1], sKN, sP])

            def _(g, ccmark=ccmark, l=l):
                g.wait_ge(s_mk.h, ccmark)
                if l == 1:
                    for sdep in (sKB[0], sKB[1], sVB[0], sVB[1]):
                        g.wait_ge(sdep.h, sdep.n)
                s_cc.inc(g.collective_compute("AllGather", ALU.bypass, ins=[cck_i],
                                              outs=[cck_o], replica_groups=RG), 1)
                s_cc.inc(g.collective_compute("AllGather", ALU.bypass, ins=[ccv_i],
                                              outs=[ccv_o], replica_groups=RG), 1)
                g.wait_ge(s_cc.h, s_cc.n)
            GPS(_)
            cc_done = s_cc.n
            if stage == "A":
                return

            for t in range(NT):
                barrier()
                if L1:
                    u1 = unspill(rows[0][:], ("m", l, t))
                    u2 = unspill(rows[5][:], ("qscT", l, t))
                    u3 = unspill(rows[4][:], ("mx", l, t))
                    vpre = s_v.n

                    def _(tn, u1=u1, u2=u2, u3=u3, vpre=vpre):
                        tn.wait_ge(sU.h, max(u1, u2, u3))
                        tn.wait_ge(s_v.h, vpre)
                        tn.matmul(pw[4][:], onesrow_f[:], rows[0][:], start=True,
                                  stop=True)
                        s_t.inc(tn.matmul(pw[5][:], onesrow_f[:], rows[5][:],
                                          start=True, stop=True))
                    TEN(_)
                    t1 = s_t.n

                    def _(v, t1=t1):
                        v.wait_ge(s_t.h, t1)
                        v.tensor_copy(out=bct[:, 0:T], in_=pw[4][:])
                        s_v.inc(v.tensor_copy(out=bct[:, 2 * T:3 * T], in_=pw[5][:]))
                    VEC(_)
                    bcq = s_v.n
                    for f in range(DC):
                        slot = f % 2
                        my = [0]
                        vpre = s_v.n

                        def _(sy, f=f, slot=slot, my=my, vpre=vpre):
                            sy.wait_ge(s_v.h, vpre)
                            my[0] = sX[slot].inc(sy.dma_start(
                                out=xck[:, slot * T:(slot + 1) * T],
                                in_=xsrc[f * 128:(f + 1) * 128, t * T:(t + 1) * T]), 16)
                        SYN(_)

                        def _(v, f=f, slot=slot, my=my, bcq=bcq):
                            v.wait_ge(sX[slot].h, my[0])
                            v.wait_ge(s_v.h, bcq)
                            v.tensor_sub(out=ftmp[:, slot * T:(slot + 1) * T],
                                         in0=xck[:, slot * T:(slot + 1) * T],
                                         in1=bct[:, 0:T])
                            v.tensor_mul(out=qi8[:, slot * T:(slot + 1) * T],
                                         in0=ftmp[:, slot * T:(slot + 1) * T],
                                         in1=bct[:, 2 * T:3 * T])
                            s_v.inc(v.tensor_copy(out=qg[:, f * T:(f + 1) * T],
                                                  in_=qi8[:, slot * T:(slot + 1) * T]))
                        VEC(_)
                    rhs = qg
                    def _(v):
                        v.tensor_scalar(out=rows[0][:], in0=rows[4][:],
                                        scalar1=wmean[0:1, 1:2], scalar2=None,
                                        op0=ALU.mult)
                        s_v.inc(v.tensor_scalar(out=rows[2][:], in0=rows[4][:],
                                                scalar1=wmean[0:1, 2:3], scalar2=None,
                                                op0=ALU.mult))
                    VEC(_)
                    v2 = s_v.n

                    def _(tn, v2=v2):
                        tn.wait_ge(s_v.h, v2)
                        tn.matmul(pw[4][:], onesrow_f[:], rows[0][:], start=True,
                                  stop=True)
                        s_t.inc(tn.matmul(pw[5][:], onesrow_f[:], rows[2][:],
                                          start=True, stop=True))
                    TEN(_)
                    t2 = s_t.n

                    def _(v, t2=t2):
                        v.wait_ge(s_t.h, t2)
                        v.tensor_copy(out=bct[:, 0:T], in_=pw[4][:])
                        v.tensor_copy(out=bct[:, T:2 * T], in_=pw[5][:])
                        v.memset(mrun[:, 0:T], -1e30)
                        s_v.inc(v.memset(mrun[:, T:2 * T], 1e30))
                    VEC(_)
                    cgu = s_v.n

                    def _(g):
                        g.wait_ge(s_mk.h, max(tern_marker.get(0, 0),
                                              tern_marker.get(1, 0)))
                    GPS(_)
                else:
                    rhs = xn[t]
                    cgu = s_v.n

                gu_free = [0, 0, 0, 0]
                for sl in range(4):
                    gu_free[sl] = war_free[sl]
                for ic in range(IC):
                    gsl = ic % 2
                    usl = 2 + (ic % 2)
                    mg = [0]
                    mu = [0]

                    def _(g, ic=ic, gsl=gsl, usl=usl, mg=mg, mu=mu,
                          f1=gu_free[gsl], f2=gu_free[usl]):
                        if f1:
                            g.wait_ge(s_t.h, f1)
                        mg[0] = sW[gsl].inc(g.dma_start(out=war[:, gsl * D:(gsl + 1) * D],
                                                        in_=wg_t[ic]), 16)
                        if f2:
                            g.wait_ge(s_t.h, f2)
                        mu[0] = sW[usl].inc(g.dma_start(out=war[:, usl * D:(usl + 1) * D],
                                                        in_=wu_t[ic]), 16)
                    GPS(_)

                    pg = ic % 2
                    pu = ic % 2
                    gb = pw[0] if pg == 0 else pw[2]
                    ub = pw[1] if pu == 0 else pw[3]

                    def _(tn, ic=ic, gsl=gsl, usl=usl, mg=mg, mu=mu, gb=gb, ub=ub,
                          cgu=cgu, t=t, ef=evac_free):
                        tn.wait_ge(sW[gsl].h, mg[0])
                        if ic == 0:
                            tn.wait_ge(s_v.h, cgu)
                        if ic >= 2:
                            tn.wait_ge(s_v.h, ef[ic % 2])
                            tn.wait_ge(s_s.h, ef_s[ic % 2])
                        for dm in range(DC):
                            ins = tn.matmul(gb[:], wsl(gsl, dm), rhs[:, dm * T:(dm + 1) * T],
                                            start=(dm == 0),
                                            stop=(L1 and dm == DC - 1),
                                            skip_group_check=True)
                        if not L1:
                            ins = tn.matmul(gb[:], m128row[:], srow_bf[t][:],
                                            start=False, stop=True, skip_group_check=True)
                        s_t.inc(ins)
                        gu_free[gsl] = s_t.n
                        tn.wait_ge(sW[usl].h, mu[0])
                        for dm in range(DC):
                            ins = tn.matmul(ub[:], wsl(usl, dm), rhs[:, dm * T:(dm + 1) * T],
                                            start=(dm == 0),
                                            stop=(L1 and dm == DC - 1),
                                            skip_group_check=True)
                        if not L1:
                            ins = tn.matmul(ub[:], m128row[:], srow_bf[t][:],
                                            start=False, stop=True, skip_group_check=True)
                        s_t.inc(ins)
                        gu_free[usl] = s_t.n
                    TEN(_)
                    t1 = s_t.n

                    if not L1:
                        def _(sc, ic=ic, t1=t1, gb=gb):
                            sc.wait_ge(s_t.h, t1)
                            s_s.inc(sc.activation(out=bfs[:, 0:T], in_=gb[:],
                                                  func=AF.Silu, bias=0.0,
                                                  scale=gsc[:, ic + 1:ic + 2]))
                        SCA(_)
                        ef_s[ic % 2] = s_s.n
                        s2 = s_s.n

                        def _(v, ic=ic, t1=t1, s2=s2, ub=ub):
                            v.wait_ge(s_t.h, t1)
                            v.tensor_scalar(out=bfs[:, T:2 * T], in0=ub[:],
                                            scalar1=usc[:, ic + 1:ic + 2], scalar2=None,
                                            op0=ALU.mult)
                            v.wait_ge(s_s.h, s2)
                            s_v.inc(v.tensor_mul(out=har[:, ic * T:(ic + 1) * T],
                                                 in0=bfs[:, 0:T], in1=bfs[:, T:2 * T]))
                        VEC(_)
                        evac_free[ic % 2] = s_v.n
                        v2 = s_v.n

                        def _(tn, ic=ic, v2=v2):
                            tn.wait_ge(s_v.h, v2)
                            s_t.inc(tn.matmul(pr1[:], onescol_bf[:],
                                              har[:, ic * T:(ic + 1) * T],
                                              start=(ic == 0), stop=(ic == IC - 1),
                                              skip_group_check=True))
                        TEN(_)
                    else:
                        def _(v, ic=ic, t1=t1, gb=gb):
                            v.wait_ge(s_t.h, t1)
                            s_v.inc(v.tensor_mul(out=ftmp[:, 0:T], in0=gb[:],
                                                 in1=bct[:, 0:T]))
                        VEC(_)
                        v1 = s_v.n

                        def _(sc, v1=v1):
                            sc.wait_ge(s_v.h, v1)
                            s_s.inc(sc.activation(out=bfs[:, 0:T], in_=ftmp[:, 0:T],
                                                  func=AF.Silu))
                        SCA(_)
                        ef_s[ic % 2] = s_s.n
                        s2 = s_s.n

                        def _(v, ic=ic, s2=s2, ub=ub):
                            v.tensor_mul(out=ftmp[:, T:2 * T], in0=ub[:],
                                         in1=bct[:, T:2 * T])
                            v.wait_ge(s_s.h, s2)
                            v.tensor_mul(out=har[:, ic * T:(ic + 1) * T],
                                         in0=ftmp[:, T:2 * T], in1=bfs[:, 0:T])
                            v.tensor_mul(out=bfs[:, 2 * T:3 * T],
                                         in0=har[:, ic * T:(ic + 1) * T],
                                         in1=har[:, ic * T:(ic + 1) * T])
                            v.scalar_tensor_tensor(out=ftmp[:, 0:T],
                                                   in0=har[:, ic * T:(ic + 1) * T],
                                                   scalar=gam[:, ic + 1:ic + 2],
                                                   in1=mrun[:, 0:T],
                                                   op0=ALU.mult, op1=ALU.max)
                            v.tensor_copy(out=mrun[:, 0:T], in_=ftmp[:, 0:T])
                            v.scalar_tensor_tensor(out=ftmp[:, 0:T],
                                                   in0=har[:, ic * T:(ic + 1) * T],
                                                   scalar=gam[:, ic + 1:ic + 2],
                                                   in1=mrun[:, T:2 * T],
                                                   op0=ALU.mult, op1=ALU.min)
                            s_v.inc(v.tensor_copy(out=mrun[:, T:2 * T], in_=ftmp[:, 0:T]))
                        VEC(_)
                        evac_free[ic % 2] = s_v.n
                        v2 = s_v.n

                        def _(tn, ic=ic, v2=v2):
                            tn.wait_ge(s_v.h, v2)
                            s_t.inc(tn.matmul(pr1[:], onescol_bf[:], bfs[:, 2 * T:3 * T],
                                              start=(ic == 0), stop=(ic == IC - 1),
                                              skip_group_check=True))
                        TEN(_)
                    pump(8)
                hs_t = s_t.n
                hs_v = s_v.n

                if not L1:
                    def _(v, hs_t=hs_t, t_=t):
                        v.wait_ge(s_t.h, hs_t)
                        s_v.inc(v.tensor_copy(out=srow_bf[t_][:], in_=pr1[:]))
                    VEC(_)
                else:
                    def _(v, hs_t=hs_t):
                        v.wait_ge(s_t.h, hs_t)
                        s_v.inc(v.tensor_scalar(out=rows[3][:], in0=pr1[:],
                                                scalar1=1.0 / II, scalar2=1e-6,
                                                op0=ALU.mult, op1=ALU.add))
                    VEC(_)
                    rss_v = s_v.n

                    def _(sc, rss_v=rss_v):
                        sc.wait_ge(s_v.h, rss_v)
                        s_s.inc(sc.activation(out=rows[6][:], in_=rows[3][:],
                                              func=AF.Sqrt))
                    SCA(_)
                    rstd_ss = s_s.n

                    def _(v, rstd_ss=rstd_ss):
                        v.wait_ge(s_s.h, rstd_ss)
                        s_v.inc(v.reciprocal(out=rows[2][:], in_=rows[6][:]))
                    VEC(_)
                    rstd_s = s_s.n

                    def _(v, hs_v=hs_v):
                        v.wait_ge(s_v.h, hs_v)
                        v.tensor_scalar(out=mrun[:, T:2 * T], in0=mrun[:, T:2 * T],
                                        scalar1=-1.0, scalar2=None, op0=ALU.mult)
                        s_v.inc(v.tensor_tensor(out=mrun[:, 0:T], in0=mrun[:, 0:T],
                                                in1=mrun[:, T:2 * T], op=ALU.max))
                    VEC(_)
                    mv = s_v.n
                    for tb in range(4):
                        def _(tn, tb=tb, mv=mv):
                            tn.wait_ge(s_v.h, mv)
                            s_t.inc(tn.transpose(pw[tb][:, 0:128],
                                                 mrun[:, tb * 128:(tb + 1) * 128],
                                                 ident[:]))
                        TEN(_)
                    t2 = s_t.n
                    mcbase = mcol_ctr[0]
                    mcol_ctr[0] += 4

                    def _(v, t2=t2, mcbase=mcbase):
                        v.wait_ge(s_t.h, t2)
                        for tb in range(4):
                            ins = v.reduce_max(out=mcols[:, mcbase + tb:mcbase + tb + 1],
                                               in_=pw[tb][:, 0:128], axis=AX.X)
                        s_v.inc(ins)
                    VEC(_)
                    v2 = s_v.n
                    for tb in range(4):
                        def _(tn, tb=tb, v2=v2, mcbase=mcbase):
                            tn.wait_ge(s_v.h, v2)
                            s_t.inc(tn.matmul(pr0[0:1, tb * 128:(tb + 1) * 128],
                                              mcols[:, mcbase + tb:mcbase + tb + 1],
                                              ident[:], start=True, stop=True,
                                              skip_group_check=True))
                        TEN(_)
                    t3 = s_t.n

                    def _(v, t3=t3, rstd_s=rstd_s):
                        v.wait_ge(s_t.h, t3)
                        v.wait_ge(s_s.h, rstd_s)
                        v.tensor_copy(out=rows[3][:], in_=pr0[:])
                        v.tensor_mul(out=rows[4][:], in0=rows[3][:], in1=rows[2][:])
                        v.tensor_scalar(out=rows[4][:], in0=rows[4][:], scalar1=1e-5,
                                        scalar2=None, op0=ALU.max)
                        v.reciprocal(out=rows[5][:], in_=rows[4][:])
                        v.scalar_tensor_tensor(out=rows[5][:], in0=rows[5][:],
                                               scalar=127.0, in1=rows[2][:],
                                               op0=ALU.mult, op1=ALU.mult)
                        s_v.inc(v.tensor_scalar(out=rows[6][:], in0=rows[4][:],
                                                scalar1=wmean[0:1, 3:4], scalar2=None,
                                                op0=ALU.mult))
                    VEC(_)
                    v3 = s_v.n

                    def _(tn, v3=v3):
                        tn.wait_ge(s_v.h, v3)
                        tn.matmul(pw[4][:], onesrow_f[:], rows[5][:], start=True,
                                  stop=True)
                        s_t.inc(tn.matmul(pw[5][:], onesrow_f[:], rows[6][:],
                                          start=True, stop=True))
                    TEN(_)
                    t4 = s_t.n

                    def _(v, t4=t4):
                        v.wait_ge(s_t.h, t4)
                        v.tensor_copy(out=bct[:, 2 * T:3 * T], in_=pw[4][:])
                        s_v.inc(v.tensor_copy(out=bct[:, 3 * T:4 * T], in_=pw[5][:]))
                    VEC(_)
                    qsch_v = s_v.n

                    for ic2 in range(IC):
                        slot = ic2 % 2

                        def _(v, ic2=ic2, slot=slot, qsch_v=qsch_v):
                            if ic2 == 0:
                                v.wait_ge(s_v.h, qsch_v)
                            v.scalar_tensor_tensor(out=qi8[:, slot * T:(slot + 1) * T],
                                                   in0=har[:, ic2 * T:(ic2 + 1) * T],
                                                   scalar=gam[:, ic2 + 1:ic2 + 2],
                                                   in1=bct[:, 2 * T:3 * T],
                                                   op0=ALU.mult, op1=ALU.mult)
                            s_v.inc(v.tensor_copy(out=har[:, ic2 * T:(ic2 + 1) * T],
                                                  in_=qi8[:, slot * T:(slot + 1) * T]))
                        VEC(_)
                    def _(g):
                        g.wait_ge(s_mk.h, tern_marker.get(2, 0))
                    GPS(_)
                qdone_v = s_v.n

                dn_free = list(gu_free)
                for hf in range(DC):
                    pd = pw[4] if hf % 2 == 0 else pw[5]
                    for sub in range(4):
                        my = [0]
                        free = dn_free[sub]

                        def _(g, hf=hf, sub=sub, my=my, free=free):
                            if free:
                                g.wait_ge(s_t.h, free)
                            my[0] = sW[sub].inc(g.dma_start(
                                out=war[:, sub * D:(sub + 1) * D],
                                in_=wd_t[hf, :, sub * D:(sub + 1) * D]), 16)
                        GPS(_)

                        def _(tn, hf=hf, sub=sub, my=my, pd=pd, qdone_v=qdone_v, t=t,
                              dse=down_evac):
                            tn.wait_ge(sW[sub].h, my[0])
                            if sub == 0:
                                tn.wait_ge(s_v.h, qdone_v)
                                if hf >= 2 and dse[hf % 2] is not None:
                                    tn.wait_ge(dse[hf % 2][0].h, dse[hf % 2][1])
                                    tn.wait_ge(s_s.h, dse_s[hf % 2])
                            for k in range(DC):
                                ic2 = sub * DC + k
                                ins = tn.matmul(pd[:], wsl(sub, k),
                                                har[:, ic2 * T:(ic2 + 1) * T],
                                                start=(ic2 == 0),
                                                stop=(L1 and ic2 == IC - 1),
                                                skip_group_check=True)
                            if sub == 3 and not L1:
                                ins = tn.matmul(pd[:], m128row[:], srow_bf[t][:],
                                                start=False, stop=True,
                                                skip_group_check=True)
                            s_t.inc(ins)
                            dn_free[sub] = s_t.n
                        TEN(_)
                    t1 = s_t.n

                    if not L1:
                        def _(sc, hf=hf, t1=t1, pd=pd):
                            sc.wait_ge(s_t.h, t1)
                            s_s.inc(sc.activation(out=qg[:, hf * T:(hf + 1) * T],
                                                  in_=pd[:], func=AF.Silu, bias=0.0,
                                                  scale=dsc[:, hf + 1:hf + 2]))
                        SCA(_)
                        down_evac[hf % 2] = (s_s, s_s.n)
                        dse_s[hf % 2] = s_s.n
                    else:
                        def _(v, hf=hf, t1=t1, pd=pd):
                            v.wait_ge(s_t.h, t1)
                            s_v.inc(v.tensor_mul(out=ftmp[:, 0:T], in0=pd[:],
                                                 in1=bct[:, 3 * T:4 * T]))
                        VEC(_)
                        down_evac[hf % 2] = (s_v, s_v.n)
                        v1 = s_v.n

                        def _(sc, hf=hf, v1=v1):
                            sc.wait_ge(s_v.h, v1)
                            s_s.inc(sc.activation(out=qg[:, hf * T:(hf + 1) * T],
                                                  in_=ftmp[:, 0:T], func=AF.Silu))
                        SCA(_)
                        dse_s[hf % 2] = s_s.n
                    pump(8)
                for sl in range(4):
                    war_free[sl] = dn_free[sl]
                g_done = s_s.n
                if stage == "C":
                    return

                barrier()
                my = [0]

                def _(sy, cc_done=cc_done, my=my):
                    sy.wait_ge(s_cc.h, cc_done)
                    sCM.inc(sy.dma_start(out=ccm_sb[0:1, 0:16],
                                         in_=cck_o[D:D + 1, 0:16]), 16)
                    my[0] = sCM.inc(sy.dma_start(out=ccm_sb[0:1, 16:32],
                                                 in_=cck_o[2 * D + 16:2 * D + 17, 0:16]), 16)
                SYN(_)
                sy1 = sCM.n

                def _(v, sy1=sy1):
                    v.wait_ge(sCM.h, sy1)
                    s_v.inc(v.tensor_tensor(out=knm[0:1, 16:32], in0=ccm_sb[0:1, 0:16],
                                            in1=ccm_sb[0:1, 16:32], op=ALU.max))
                VEC(_)
                knall_v = s_v.n

                for h in range(NH):
                    vpre, spre = s_v.n, s_s.n

                    def _(tn, vpre=vpre, spre=spre):
                        tn.wait_ge(s_v.h, vpre)
                        tn.wait_ge(s_s.h, spre)
                    TEN(_)

                    slot, gcnt = load_war(wqk[h])
                    qs = h % 2

                    def _(tn, slot=slot, gcnt=gcnt, t=t):
                        tn.wait_ge(sW[slot].h, gcnt)
                        for dm in range(DC):
                            ins = tn.matmul(pw[3][:], wsl(slot, dm),
                                            xn[t][:, dm * T:(dm + 1) * T],
                                            start=(dm == 0), stop=(dm == DC - 1),
                                            skip_group_check=True)
                        s_t.inc(ins)
                    TEN(_)
                    war_free[slot] = s_t.n
                    t1 = s_t.n

                    def _(v, h=h, t1=t1, qs=qs, qf=qh_free[qs]):
                        v.wait_ge(s_t.h, max(t1, qf))
                        b = qkvb[:, h + 1:h + 2] if L1 else 0.0
                        v.tensor_scalar(out=qhb[:, qs * T:(qs + 1) * T],
                                        in0=pw[3][:], scalar1=b, scalar2=None,
                                        op0=ALU.add)
                        s_v.inc(v.tensor_mul(out=bfs[:, 0:T],
                                             in0=qhb[:, qs * T:(qs + 1) * T],
                                             in1=qhb[:, qs * T:(qs + 1) * T]))
                    VEC(_)
                    v1 = s_v.n

                    def _(tn, v1=v1):
                        tn.wait_ge(s_v.h, v1)
                        s_t.inc(tn.matmul(pr0[:], onescol_bf[:], bfs[:, 0:T],
                                          start=True, stop=True, skip_group_check=True))
                    TEN(_)
                    t2 = s_t.n

                    def _(v, h=h, t2=t2, knall_v=knall_v):
                        v.wait_ge(s_t.h, t2)
                        v.wait_ge(s_v.h, knall_v)
                        c = s_v.inc(v.reduce_max(out=sc1[0:1, 1:2], in_=pr0[:],
                                                 axis=AX.X))
                        v.wait_ge(s_v.h, c)
                        s_v.inc(v.tensor_tensor(out=sc1[0:1, 2:3], in0=sc1[0:1, 1:2],
                                                in1=knm[0:1, 16 + h:17 + h],
                                                op=ALU.mult))
                    VEC(_)
                    v2 = s_v.n

                    def _(sc, v2=v2):
                        sc.wait_ge(s_v.h, v2)
                        s_s.inc(sc.activation(out=sc1[0:1, 3:4], in_=sc1[0:1, 2:3],
                                              func=AF.Sqrt))
                    SCA(_)
                    s2 = s_s.n

                    def _(v, s2=s2):
                        v.wait_ge(s_s.h, s2)
                        s_v.inc(v.tensor_scalar(out=sc1[0:1, 4:5], in0=sc1[0:1, 3:4],
                                                scalar1=-SCALE, scalar2=None,
                                                op0=ALU.mult))
                    VEC(_)
                    v3 = s_v.n

                    def _(tn, v3=v3):
                        tn.wait_ge(s_v.h, v3)
                        s_t.inc(tn.matmul(pw[5][:, 0:1], onesrow_f[:], sc1[0:1, 4:5],
                                          start=True, stop=True))
                    TEN(_)
                    t3 = s_t.n

                    def _(v, t3=t3):
                        v.wait_ge(s_t.h, t3)
                        s_v.inc(v.tensor_copy(out=biascol[:], in_=pw[5][:, 0:1]))
                    VEC(_)
                    bias_v = s_v.n

                    first = [True]
                    for r in range(2):
                        myk = [0]
                        myv_ = [0]

                        def _(sy, h=h, r=r, myk=myk, myv_=myv_, cc_done=cc_done,
                              kf=kv_free[r]):
                            sy.wait_ge(s_cc.h, cc_done)
                            if kf:
                                sy.wait_ge(s_t.h, kf)
                            myk[0] = sKB[r].inc(sy.dma_start(
                                out=kbuf[:, r * TPC:(r + 1) * TPC],
                                in_=cck_o[r * (D + 16) + h * 128:
                                          r * (D + 16) + (h + 1) * 128, :]), 16)
                            inap = ccv_o[r * TPC:(r + 1) * TPC,
                                         h * 128:(h + 1) * 128].rearrange(
                                "(b p) d -> p b d", p=128)
                            outap = vbuf[:, r * TPC:(r + 1) * TPC].rearrange(
                                "p (b d) -> p b d", d=128)
                            myv_[0] = sVB[r].inc(sy.dma_start(out=outap, in_=inap), 16)
                        SYN(_)

                        for kb in range(8):
                            gkb = r * 8 + kb
                            psl = gkb % 2
                            pslot = gkb % 3
                            sb = pw[0] if psl == 0 else pw[1]
                            mym = [0]
                            if L1:
                                def _(sy, t=t, gkb=gkb, mym=mym, mf=mask_free[gkb % 2]):
                                    if mf:
                                        sy.wait_ge(s_v.h, mf)
                                    mym[0] = sM[gkb % 2].inc(sy.dma_start(
                                        out=mbuf[:, (gkb % 2) * T:(gkb % 2 + 1) * T],
                                        in_=mask_d[t * 16 + gkb]), 16)
                                SYN(_)

                            def _(tn, kb=kb, r=r, sb=sb, myk=myk, qs=qs,
                                  ef=exp_free[psl]):
                                tn.wait_ge(sKB[r].h, myk[0])
                                if ef:
                                    tn.wait_ge(s_s.h, ef)
                                s_t.inc(tn.matmul(
                                    sb[:],
                                    kbuf[:, r * TPC + kb * 128: r * TPC + (kb + 1) * 128],
                                    qhb[:, qs * T:(qs + 1) * T],
                                    start=True, stop=True, skip_group_check=True))
                            TEN(_)
                            t4 = s_t.n

                            def _(sc, sb=sb, psl=psl, pslot=pslot, t4=t4,
                                  bias_v=bias_v, pf=pv_free[pslot]):
                                sc.wait_ge(s_t.h, t4)
                                sc.wait_ge(s_v.h, bias_v)
                                if pf:
                                    sc.wait_ge(s_t.h, pf)
                                ins = sc.activation(out=pbuf[:, pslot * T:(pslot + 1) * T],
                                                    in_=sb[:], func=AF.Exp,
                                                    bias=biascol[:], scale=sccol[:])
                                exp_free[psl] = s_s.inc(ins)
                            SCA(_)
                            s3 = s_s.n

                            if L1:
                                def _(v, pslot=pslot, gkb=gkb, s3=s3, mym=mym):
                                    v.wait_ge(s_s.h, s3)
                                    v.wait_ge(sM[gkb % 2].h, mym[0])
                                    ins = v.tensor_tensor(
                                        out=pbuf[:, pslot * T:(pslot + 1) * T],
                                        in0=pbuf[:, pslot * T:(pslot + 1) * T],
                                        in1=mbuf[:, (gkb % 2) * T:(gkb % 2 + 1) * T],
                                        op=ALU.mult)
                                    mask_free[gkb % 2] = s_v.inc(ins)
                                VEC(_)
                                pvw = ("v", s_v.n)
                            else:
                                pvw = ("s", s3)

                            def _(tn, kb=kb, r=r, pslot=pslot, myv_=myv_, pvw=pvw,
                                  first=first, gkb=gkb):
                                if pvw[0] == "v":
                                    tn.wait_ge(s_v.h, pvw[1])
                                else:
                                    tn.wait_ge(s_s.h, pvw[1])
                                tn.wait_ge(sVB[r].h, myv_[0])
                                tn.matmul(pr1[:], onescol_bf[:],
                                          pbuf[:, pslot * T:(pslot + 1) * T],
                                          start=first[0], stop=(gkb == 15),
                                          skip_group_check=True)
                                ins = tn.matmul(
                                    pw[4][:],
                                    vbuf[:, r * TPC + kb * 128: r * TPC + (kb + 1) * 128],
                                    pbuf[:, pslot * T:(pslot + 1) * T],
                                    start=first[0], stop=(gkb == 15),
                                    skip_group_check=True)
                                pv_free[pslot] = s_t.inc(ins)
                                first[0] = False
                            TEN(_)
                        kv_free[r] = s_t.n
                    qh_free[qs] = s_t.n
                    pvdone = s_t.n

                    def _(v, pvdone=pvdone):
                        v.wait_ge(s_t.h, pvdone)
                        s_v.inc(v.reciprocal(out=rows[2][:], in_=pr1[:]))
                    VEC(_)
                    v4 = s_v.n

                    def _(tn, v4=v4):
                        tn.wait_ge(s_v.h, v4)
                        s_t.inc(tn.matmul(pw[5][:], onesrow_f[:], rows[2][:],
                                          start=True, stop=True))
                    TEN(_)
                    t5 = s_t.n

                    def _(v, h=h, t5=t5):
                        v.wait_ge(s_t.h, t5)
                        v.tensor_copy(out=bct[:, 3 * T:4 * T], in_=pw[5][:])
                        s_v.inc(v.tensor_mul(out=har[:, (16 + h) * T:(17 + h) * T],
                                             in0=pw[4][:], in1=bct[:, 3 * T:4 * T]))
                    VEC(_)
                    pump(12)
                ao_done = s_v.n

                for mo in range(DC):
                    slot, gcnt = load_war(wop[mo])

                    def _(tn, slot=slot, gcnt=gcnt, ao_done=ao_done, mo=mo,
                          spre=s_v.n):
                        tn.wait_ge(sW[slot].h, gcnt)
                        if mo == 0:
                            tn.wait_ge(s_v.h, ao_done)
                        else:
                            tn.wait_ge(s_v.h, spre)
                        for dm in range(DC):
                            ins = tn.matmul(pw[3][:], wsl(slot, dm),
                                            har[:, (16 + dm) * T:(17 + dm) * T],
                                            start=(dm == 0), stop=(dm == DC - 1),
                                            skip_group_check=True)
                        s_t.inc(ins)
                    TEN(_)
                    war_free[slot] = s_t.n
                    t1 = s_t.n

                    def _(v, mo=mo, t1=t1):
                        v.wait_ge(s_t.h, t1)
                        b = obc[:, mo + 1:mo + 2] if L1 else 0.0
                        s_v.inc(v.tensor_scalar(out=har[:, (32 + mo) * T:(33 + mo) * T],
                                                in0=pw[3][:], scalar1=b, scalar2=None,
                                                op0=ALU.add))
                    VEC(_)
                    pump(8)
                s_done = s_v.n
                if stage == "D":
                    return

                for f in range(DC):
                    slot = f % 2
                    my = [0]

                    def _(sy, f=f, slot=slot, my=my, uf=upd_free[slot]):
                        if uf:
                            sy.wait_ge(sUP[slot].h, uf)
                        my[0] = sX[slot].inc(sy.dma_start(
                            out=xck[:, slot * T:(slot + 1) * T],
                            in_=xsrc[f * 128:(f + 1) * 128, t * T:(t + 1) * T]), 16)
                    SYN(_)

                    def _(v, f=f, slot=slot, my=my, s_done=s_done, g_done=g_done):
                        v.wait_ge(s_s.h, g_done)
                        v.wait_ge(s_v.h, s_done)
                        v.wait_ge(sX[slot].h, my[0])
                        if stage == "dumps" and l == n_layers - 1:
                            s_v.inc(v.tensor_copy(out=xck[:, slot * T:(slot + 1) * T],
                                                  in_=har[:, (32 + f) * T:(33 + f) * T]))
                        elif stage == "dumpg" and l == n_layers - 1:
                            s_v.inc(v.tensor_copy(out=xck[:, slot * T:(slot + 1) * T],
                                                  in_=qg[:, f * T:(f + 1) * T]))
                        else:
                            v.tensor_mul(out=ftmp[:, slot * T:(slot + 1) * T],
                                         in0=har[:, (32 + f) * T:(33 + f) * T],
                                         in1=qg[:, f * T:(f + 1) * T])
                            s_v.inc(v.tensor_add(out=xck[:, slot * T:(slot + 1) * T],
                                                 in0=xck[:, slot * T:(slot + 1) * T],
                                                 in1=ftmp[:, slot * T:(slot + 1) * T]))
                    VEC(_)
                    v1 = s_v.n

                    def _(sy, f=f, slot=slot, v1=v1, t=t):
                        sy.wait_ge(s_v.h, v1)
                        upd_free[slot] = sUP[slot].inc(sy.dma_start(
                            out=xdst[f * 128:(f + 1) * 128, t * T:(t + 1) * T],
                            in_=xck[:, slot * T:(slot + 1) * T]), 16)
                    SYN(_)
                    pump(10)
            layer_e_marker[l] = marker([sUP[0], sUP[1]])

        mcol_ctr = [8]
        evac_free = [0, 0]
        ef_s = [0, 0]
        down_evac = [None, None]
        dse_s = [0, 0]

        for l in range(n_layers):
            emit_layer(l)
        pump_flush()

        if stage == "dumpt":
            pre = (s_mk.n, sUP[0].n, sUP[1].n)

            def _(g, pre=pre):
                g.wait_ge(s_mk.h, pre[0])
                g.wait_ge(sUP[0].h, pre[1])
                g.wait_ge(sUP[1].h, pre[2])
                for mo in range(16):
                    sKN.inc(g.dma_start(out=out[mo * 128:(mo + 1) * 128, :],
                                        in_=tg_d[mo, :, 0:TPC]), 16)
                for i in range(3):
                    sKN.inc(g.dma_start(out=out[0:128, 1020 + i:1021 + i],
                                        in_=swsp[i][:],
                                        allow_slow_non_contiguous=True), 16)
                sKN.inc(g.dma_start(out=out[128:129, 1008:1016],
                                    in_=wmean[0:1, :]), 16)
                sKN.inc(g.dma_start(out=out[129:130, 1008:1016],
                                    in_=sc1[0:1, :]), 16)
                sKN.inc(g.dma_start(out=out[130:258, 1016:1017],
                                    in_=acc[:, 0:1],
                                    allow_slow_non_contiguous=True), 16)
            GPS(_)
            marker([sKN])

        marker([sKS[0], sKS[1], sVS[0], sVS[1], sUP[0], sUP[1], sSP, sTS, sKN])
        mk_fin = s_mk.n

        def _(sy):
            sy.wait_ge(s_mk.h, mk_fin)
        SYN(_)

    return nc, sbuf_left


def _tile_lhsT(w):
    K, M = w.shape
    Ko, Mo = K // 128, M // 128
    return np.ascontiguousarray(
        w.reshape(Ko, 128, Mo, 128).transpose(2, 1, 0, 3).reshape(Mo, 128, K))


def _u8(wi32):
    return np.ascontiguousarray(wi32.view(np.uint8)[..., 0::4])


def _padcol(a):
    return np.ascontiguousarray(
        np.concatenate([np.zeros((a.shape[0], 1), a.dtype), a], axis=1))


def make_inputs(kw):
    f32 = np.float32
    x = np.asarray(kw["L0_attn_x"], f32)
    w0qkv = np.asarray(kw["L0_attn_w_qkv"], f32)
    w0o = np.asarray(kw["L0_attn_w_out"], f32)
    g0 = _u8(np.ascontiguousarray(np.asarray(kw["L0_ffn_gate_w"], np.int32)))
    u0 = _u8(np.ascontiguousarray(np.asarray(kw["L0_ffn_up_w"], np.int32)))
    d0 = _u8(np.ascontiguousarray(np.asarray(kw["L0_ffn_down_w"], np.int32)))
    gs = np.asarray(kw["L0_ffn_gate_scale"], f32)
    us = np.asarray(kw["L0_ffn_up_scale"], f32)
    ds = np.asarray(kw["L0_ffn_down_scale"], f32)
    w1qkv = np.asarray(kw["L1_attn_attn_qkvw"], f32)
    qkvb = np.asarray(kw["L1_attn_attn_qkvb"], f32)
    w1o = np.asarray(kw["L1_attn_attn_ow"], f32)
    ob = np.asarray(kw["L1_attn_attn_ob"], f32)
    wg1 = np.asarray(kw["L1_ffn_w_gate"], f32)
    wu1 = np.asarray(kw["L1_ffn_w_up"], f32)
    wd1 = np.asarray(kw["L1_ffn_w_down"], f32)
    gam = np.asarray(kw["L1_ffn_ln_weight"], f32)

    shared = {
        "w0qk": _tile_lhsT(w0qkv[:, 0:4096]),
        "w0v": np.ascontiguousarray(w0qkv[:, 4096:6144]),
        "w0o": _tile_lhsT(w0o),
        "w0g": _tile_lhsT(np.ascontiguousarray(g0.T)),
        "w0u": _tile_lhsT(np.ascontiguousarray(u0.T)),
        "w0d": _tile_lhsT(np.ascontiguousarray(d0.T)),
        "gs_col": _padcol(np.ascontiguousarray(gs.reshape(64, 128).T)),
        "us_col": _padcol(np.ascontiguousarray(us.reshape(64, 128).T)),
        "ds_col": _padcol(np.ascontiguousarray(ds.reshape(16, 128).T)),
        "w1qk": _tile_lhsT(w1qkv[:, 0:4096]),
        "w1v": np.ascontiguousarray(w1qkv[:, 4096:6144]),
        "w1o": _tile_lhsT(w1o),
        "w1g": _tile_lhsT(np.ascontiguousarray(wg1.T)),
        "w1u": _tile_lhsT(np.ascontiguousarray(wu1.T)),
        "w1d": _tile_lhsT(np.ascontiguousarray(wd1.T)),
        "qkvb_col": _padcol(np.ascontiguousarray(qkvb[0:4096].reshape(32, 128).T)),
        "bv_row": np.ascontiguousarray(qkvb[4096:6144].reshape(1, 2048)),
        "ob_col": _padcol(np.ascontiguousarray(ob.reshape(16, 128).T)),
        "gam_col": _padcol(np.ascontiguousarray(gam.reshape(64, 128).T)),
        "ident_in": np.eye(128, dtype=f32),
        "dummy_d": np.zeros((1, 64), f32),
    }
    in_maps = []
    for c in range(8):
        b, half = c // 2, c % 2
        qbase = half * TPC
        m = dict(shared)
        m["x0"] = np.ascontiguousarray(x[b, qbase:qbase + TPC, :].T)
        msk = np.zeros((32, 128, T), dtype=ml_dtypes.bfloat16)
        for t in range(NT):
            for gkb in range(16):
                kpos = gkb * 128 + np.arange(128)[:, None]
                qpos = qbase + t * T + np.arange(T)[None, :]
                msk[t * 16 + gkb] = (kpos <= qpos).astype(ml_dtypes.bfloat16)
        m["mask_d"] = msk
        in_maps.append(m)
    return in_maps


_CACHE = {}


def kernel(**kw):
    if "nc" not in _CACHE:
        _CACHE["nc"] = build()[0]
    nc = _CACHE["nc"]
    in_maps = make_inputs(kw)
    res = run_bass_kernel_spmd(nc, in_maps, core_ids=list(range(8)))
    outp = np.zeros((B, S, D), np.float32)
    for c in range(8):
        b, half = c // 2, c % 2
        outp[b, half * TPC:(half + 1) * TPC, :] = res.results[c]["out"].T
    return outp


# version 03
# speedup vs baseline: 1.0123x; 1.0123x over previous
import numpy as np
import ml_dtypes
from contextlib import ExitStack

import concourse.bass as bass
import concourse.mybir as mybir
from concourse.bass_utils import run_bass_kernel_spmd

F32 = mybir.dt.float32
BF16 = mybir.dt.bfloat16
U8 = mybir.dt.uint8
I8 = mybir.dt.int8
AF = mybir.ActivationFunctionType
ALU = mybir.AluOpType
AX = mybir.AxisListType

B, S, D = 4, 2048, 2048
II = 8192
NH = 16
TPC = 1024
T = 512
NT = TPC // T
DC = D // 128
IC = II // 128
SCALE = 1.0 / float(np.sqrt(128.0))
RG = [[0, 1], [2, 3], [4, 5], [6, 7]]
TCH = 1024


class Sem:
    def __init__(self, es, nc, name):
        self.h = es.enter_context(nc.semaphore(name))
        self.n = 0

    def inc(self, ins, k=1):
        ins.then_inc(self.h, k)
        self.n += k
        return self.n


def build(n_layers=2, l0_out=False, stage="full"):
    nc = bass.Bass("TRN2", target_bir_lowering=False, debug=False,
                   detect_race_conditions=False)

    def din(name, shape, dt):
        return nc.dram_tensor(name, list(shape), dt, kind="ExternalInput").ap()

    def dint(name, shape, dt):
        return nc.dram_tensor(name, list(shape), dt, kind="Internal").ap()

    x0 = din("x0", [D, TPC], F32)
    w0qk = din("w0qk", [32, 128, D], F32)
    w0v = din("w0v", [D, D], F32)
    w0o = din("w0o", [16, 128, D], F32)
    w0g = din("w0g", [64, 128, D], U8)
    w0u = din("w0u", [64, 128, D], U8)
    w0d = din("w0d", [16, 128, II], U8)
    gs_col = din("gs_col", [128, 65], F32)
    us_col = din("us_col", [128, 65], F32)
    ds_col = din("ds_col", [128, 17], F32)
    w1qk = din("w1qk", [32, 128, D], F32)
    w1v = din("w1v", [D, D], F32)
    w1o = din("w1o", [16, 128, D], F32)
    w1g = din("w1g", [64, 128, D], F32)
    w1u = din("w1u", [64, 128, D], F32)
    w1d = din("w1d", [16, 128, II], F32)
    qkvb_col = din("qkvb_col", [128, 33], F32)
    bv_row = din("bv_row", [1, D], F32)
    ob_col = din("ob_col", [128, 17], F32)
    gam_col = din("gam_col", [128, 65], F32)
    mask_d = din("mask_d", [32, 128, T], BF16)
    ident_in = din("ident_in", [128, 128], F32)

    out = nc.dram_tensor("out", [D, TPC], F32, kind="ExternalOutput").ap()
    xres = dint("xres", [D, TPC], F32)
    cck_i = dint("cck_i", [D + 16, TPC], BF16)
    ccv_i = dint("ccv_i", [TPC, D], BF16)
    cck_o = dint("cck_o", [2 * (D + 16), TPC], BF16)
    ccv_o = dint("ccv_o", [2 * TPC, D], BF16)
    tg_d = dint("tg_d", [64, 128, D], I8)
    tu_d = dint("tu_d", [64, 128, D], I8)
    td_d = dint("td_d", [16, 128, II], I8)
    rowspill = dint("rowspill", [64, T], F32)
    dummy_d = dint("dummy_d", [1, 64], F32)

    es = ExitStack()
    with es:
        def sbt(name, shape, dt):
            return es.enter_context(nc.sbuf_tensor(name, list(shape), dt))

        xn = [sbt(f"xn{t}", [128, DC * T], BF16) for t in range(NT)]
        qg = sbt("qg", [128, DC * T], BF16)
        har = sbt("har", [128, IC * T], BF16)
        war = sbt("war", [128, 4 * D], BF16)
        wvb = sbt("wvb", [128, 2 * T], BF16)
        kbuf = sbt("kbuf", [128, 2 * TPC], BF16)
        vbuf = sbt("vbuf", [128, 2 * TPC], BF16)
        pbuf = sbt("pbuf", [128, 3 * T], BF16)
        qhb = sbt("qhb", [128, 2 * T], BF16)
        bct = sbt("bct", [128, 4 * T], F32)
        xck = sbt("xck", [128, 2 * T], F32)
        ftmp = sbt("ftmp", [128, 2 * T], F32)
        mrun = sbt("mrun", [128, 2 * T], F32)
        mcols = sbt("mcols", [128, 48], F32)
        bfs = sbt("bfs", [128, 3 * T], BF16)
        kvst = sbt("kvst", [128, 2 * T], BF16)
        terni = sbt("terni", [128, TCH], F32)
        ternm = sbt("ternm", [128, TCH], F32)
        terno = sbt("terno", [128, TCH], I8)
        acc = sbt("acc", [128, 256], F32)
        qi8 = sbt("qi8", [128, 2 * T], I8)
        mbuf = sbt("mbuf", [128, 2 * T], BF16)
        ident = sbt("ident", [128, 128], F32)
        gsc = sbt("gsc", [128, 65], F32)
        usc = sbt("usc", [128, 65], F32)
        dsc = sbt("dsc", [128, 17], F32)
        gam = sbt("gam", [128, 65], F32)
        qkvb = sbt("qkvb", [128, 33], F32)
        obc = sbt("obc", [128, 17], F32)
        bvr = sbt("bvr", [1, D], BF16)
        onesrow_f = sbt("onesrow_f", [1, 128], F32)
        onesrow_bf = sbt("onesrow_bf", [1, 128], BF16)
        m128row = sbt("m128row", [1, 128], BF16)
        onescol_f = sbt("onescol_f", [128, 1], F32)
        onescol_bf = sbt("onescol_bf", [128, 1], BF16)
        biascol = sbt("biascol", [128, 1], F32)
        sccol = sbt("sccol", [128, 1], F32)
        swsp = [sbt(f"swsp{i}", [128, 1], F32) for i in range(3)]
        rows = [sbt(f"row{i}", [1, T], F32) for i in range(7)]
        srow_bf = [sbt(f"rbf{i}", [1, T], BF16) for i in range(2)]
        ccm_sb = sbt("ccm_sb", [1, 64], BF16)
        knm = sbt("knm", [1, 32], F32)
        sc1 = sbt("sc1", [1, 8], F32)
        wmean = sbt("wmean", [1, 8], F32)
        wrecip = [sbt(f"wrecip{i}", [1, 1], F32) for i in range(3)]
        dummy_sb = sbt("dummy_sb", [1, 64], F32)

        sbuf_left = nc.sbuf_bytes_remaining

        def psm(name, shape):
            return es.enter_context(nc.psum_tensor(name, list(shape), F32))

        pw = [psm(f"pw{i}", [128, T]) for i in range(6)]
        pr0 = psm("pr0", [1, T])
        pr1 = psm("pr1", [1, T])

        s_cc = Sem(es, nc, "s_cc")
        s_t = Sem(es, nc, "s_t")
        s_v = Sem(es, nc, "s_v")
        s_s = Sem(es, nc, "s_s")
        s_mk = Sem(es, nc, "s_mk")
        sP = Sem(es, nc, "sP")
        sP2 = Sem(es, nc, "sP2")
        sX = [Sem(es, nc, f"sX{i}") for i in range(2)]
        sT = Sem(es, nc, "sT")
        sM = [Sem(es, nc, f"sM{i}") for i in range(2)]
        sKB = [Sem(es, nc, f"sKB{i}") for i in range(2)]
        sVB = [Sem(es, nc, f"sVB{i}") for i in range(2)]
        sCM = Sem(es, nc, "sCM")
        sU = Sem(es, nc, "sU")
        sKS = [Sem(es, nc, f"sKS{i}") for i in range(2)]
        sVS = [Sem(es, nc, f"sVS{i}") for i in range(2)]
        sUP = [Sem(es, nc, f"sUP{i}") for i in range(2)]
        sSP = Sem(es, nc, "sSP")
        sTS = Sem(es, nc, "sTS")
        sKN = Sem(es, nc, "sKN")
        sW = [Sem(es, nc, f"sW{i}") for i in range(4)]
        sWV = [Sem(es, nc, f"sWV{i}") for i in range(2)]

        blk = es.enter_context(nc.Block())

        def TEN(f): blk.tensor(f)
        def VEC(f): blk.vector(f)
        def SCA(f): blk.scalar(f)
        def SYN(f): blk.sync(f)
        def GPS(f): blk.gpsimd(f)

        def marker(deps):
            vals = [(d.h, d.n) for d in deps]

            def _(sy):
                for h, v in vals:
                    sy.wait_ge(h, v)
                s_mk.inc(sy.dma_start(out=dummy_sb[:], in_=dummy_d), 16)
            SYN(_)
            return s_mk.n

        def barrier(allow_safe=True):
            tv, vv, sv = s_t.n, s_v.n, s_s.n

            def _(tn):
                tn.wait_ge(s_v.h, vv)
                tn.wait_ge(s_s.h, sv)
            TEN(_)

            def _(v):
                v.wait_ge(s_t.h, tv)
                v.wait_ge(s_s.h, sv)
            VEC(_)

            def _(sc):
                sc.wait_ge(s_t.h, tv)
                sc.wait_ge(s_v.h, vv)
            SCA(_)
            if allow_safe and tern_pos[0] < len(tern_steps) \
                    and tern_steps[tern_pos[0]][0] == "safe":
                pump(4, safe=True)

        def _(sy):
            sP.inc(sy.dma_start(out=gsc[:], in_=gs_col), 16)
            sP.inc(sy.dma_start(out=usc[:], in_=us_col), 16)
            sP.inc(sy.dma_start(out=dsc[:], in_=ds_col), 16)
            sP.inc(sy.dma_start(out=gam[:], in_=gam_col), 16)
            sP.inc(sy.dma_start(out=qkvb[:], in_=qkvb_col), 16)
            sP.inc(sy.dma_start(out=obc[:], in_=ob_col), 16)
            sP.inc(sy.dma_start(out=ident[:], in_=ident_in), 16)
        SYN(_)
        setup_sy = sP.n

        def _(g):
            sP2.inc(g.dma_start(out=bvr[:], in_=bv_row), 16)
        GPS(_)
        setup_gp = sP2.n

        def _(v):
            s_v.inc(v.memset(kvst[:], 0.0))
        VEC(_)
        zv = s_v.n

        def _(sy):
            sy.wait_ge(s_v.h, zv)
            for rr in range(16):
                sP.inc(sy.dma_start(out=cck_i[D + rr:D + rr + 1, :],
                                    in_=kvst[0:1, :]), 16)
        SYN(_)

        def _(v):
            v.wait_ge(sP.h, setup_sy)
            v.memset(onesrow_f[:], 1.0)
            v.memset(onesrow_bf[:], 1.0)
            v.memset(m128row[:], -128.0)
            v.memset(onescol_f[:], 1.0)
            v.memset(sccol[:], SCALE)
            s_v.inc(v.memset(onescol_bf[:], 1.0))
        VEC(_)

        tern_steps = []
        tern_abs_cnt = {}
        tern_absred = {}
        tern_p2_cnt = {}
        tern_st_cnt = {}
        tern_marker = {}
        tern_state = {}

        def build_tern():
            if n_layers < 2:
                return
            TC2 = TCH // 2
            specs = [(0, w1g, tg_d), (1, w1u, tu_d), (2, w1d, td_d)]
            for wi, wsrc, wdst in specs:
                Mo, K = wsrc.shape[0], wsrc.shape[2]
                ncols = K // TC2
                nch = Mo * ncols
                NEL = float(Mo * 128 * K)

                def mean_chunk(ci, wi=wi, wsrc=wsrc, ncols=ncols, nch=nch):
                    mo, co = divmod(ci, ncols)
                    hf = ci % 2
                    my = [0]

                    def _(sy):
                        if ci >= 2:
                            sy.wait_ge(s_v.h, tern_absred[(wi, ci - 2)])
                        elif wi >= 1:
                            sy.wait_ge(s_v.h, tern_p2_cnt[(wi - 1, 256 - 1)])
                        my[0] = sT.inc(sy.dma_start(
                            out=terni[:, hf * TC2:(hf + 1) * TC2],
                            in_=wsrc[mo, :, co * TC2:(co + 1) * TC2]), 16)
                    SYN(_)

                    def _(v):
                        v.wait_ge(sT.h, my[0])
                        tern_absred[(wi, ci)] = s_v.inc(v.reduce_sum(
                            out=acc[:, ci:ci + 1],
                            in_=terni[:, hf * TC2:(hf + 1) * TC2],
                            axis=AX.X, apply_absolute_value=True))
                    VEC(_)
                for ci in range(nch):
                    tern_steps.append(("any", lambda ci=ci, f=mean_chunk: f(ci)))

                def finish(wi=wi, nch=nch, NEL=NEL):
                    need_v = tern_absred[(wi, nch - 1)]
                    vpre, tpre = s_v.n, s_t.n

                    def _(v):
                        v.wait_ge(s_v.h, need_v)
                        v.wait_ge(s_t.h, tpre)
                        c = s_v.inc(v.reduce_sum(out=mcols[:, 6:7], in_=acc[:, 0:nch],
                                                 axis=AX.X))
                        v.wait_ge(s_v.h, c)
                        s_v.inc(v.tensor_copy(out=mcols[:, 7:8], in_=mcols[:, 6:7]))
                    VEC(_)
                    v1 = s_v.n

                    def _(tn):
                        tn.wait_ge(s_v.h, v1)
                        s_t.inc(tn.matmul(pr0[0:1, 0:1], onescol_f[:], mcols[:, 7:8],
                                          start=True, stop=True))
                    TEN(_)
                    t1 = s_t.n

                    def _(v):
                        v.wait_ge(s_t.h, t1)
                        v.tensor_scalar(out=wmean[0:1, wi + 1:wi + 2],
                                        in0=pr0[0:1, 0:1],
                                        scalar1=1.0 / (NEL * 127.0),
                                        scalar2=1e-5 / 127.0,
                                        op0=ALU.mult, op1=ALU.max)
                        c = s_v.inc(v.tensor_scalar(out=sc1[0:1, 5:6],
                                                    in0=pr0[0:1, 0:1],
                                                    scalar1=1.0 / NEL, scalar2=1e-5,
                                                    op0=ALU.mult, op1=ALU.max))
                        v.wait_ge(s_v.h, c)
                        s_v.inc(v.reciprocal(out=wrecip[wi][:], in_=sc1[0:1, 5:6]))
                    VEC(_)
                    v2 = s_v.n

                    def _(tn):
                        tn.wait_ge(s_v.h, v2)
                        s_t.inc(tn.matmul(pw[5][:, 0:1], onesrow_f[:], wrecip[wi][:],
                                          start=True, stop=True))
                    TEN(_)
                    t2 = s_t.n

                    def _(v):
                        v.wait_ge(s_t.h, t2)
                        s_v.inc(v.tensor_copy(out=swsp[wi][:], in_=pw[5][:, 0:1]))
                    VEC(_)
                tern_steps.append(("safe", finish))

                def tern_chunk(ci, wi=wi, wsrc=wsrc, wdst=wdst, ncols=ncols, nch=nch):
                    mo, co = divmod(ci, ncols)
                    hf = ci % 2
                    sl = slice(hf * TC2, (hf + 1) * TC2)
                    my = [0]

                    def _(sy):
                        if ci >= 2:
                            sy.wait_ge(s_v.h, tern_p2_cnt[(wi, ci - 2)])
                        else:
                            sy.wait_ge(s_v.h, tern_absred[(wi, nch - 1)])
                        my[0] = sT.inc(sy.dma_start(
                            out=terni[:, sl],
                            in_=wsrc[mo, :, co * TC2:(co + 1) * TC2]), 16)
                    SYN(_)

                    def _(v):
                        v.wait_ge(sT.h, my[0])
                        if ci >= 2:
                            v.wait_ge(sTS.h, tern_st_cnt[(wi, ci - 2)])
                        v.tensor_scalar(out=ternm[:, sl],
                                        in0=terni[:, sl],
                                        scalar1=swsp[wi][:], scalar2=None, op0=ALU.mult)
                        tern_p2_cnt[(wi, ci)] = s_v.inc(v.tensor_scalar(
                            out=terno[:, sl], in0=ternm[:, sl],
                            scalar1=-1.4375, scalar2=1.4375, op0=ALU.max, op1=ALU.min))
                    VEC(_)
                    v1 = s_v.n

                    def _(sy):
                        sy.wait_ge(s_v.h, v1)
                        tern_st_cnt[(wi, ci)] = sTS.inc(sy.dma_start(
                            out=wdst[mo, :, co * TC2:(co + 1) * TC2],
                            in_=terno[:, sl]), 16)
                    SYN(_)
                for ci in range(nch):
                    tern_steps.append(("any", lambda ci=ci, f=tern_chunk: f(ci)))

                def endmark(wi=wi):
                    tern_marker[wi] = marker([sTS])
                tern_steps.append(("any", endmark))

        build_tern()
        tern_pos = [0]

        def pump(n, safe=False):
            k = 0
            while k < n and tern_pos[0] < len(tern_steps):
                tag, fn = tern_steps[tern_pos[0]]
                if tag == "safe" and not safe:
                    break
                fn()
                tern_pos[0] += 1
                k += 1

        def pump_flush():
            pump(len(tern_steps), safe=True)

        spill_idx = {}
        spill_mk = {}

        def spill(row_ap, key, v_cnt):
            idx = spill_idx.setdefault(key, len(spill_idx))
            assert idx < 64

            def _(sy):
                sy.wait_ge(s_v.h, v_cnt)
                sSP.inc(sy.dma_start(out=rowspill[idx:idx + 1, :], in_=row_ap), 16)
            SYN(_)
            spill_mk[key] = None

        def unspill(row_ap, key):
            idx = spill_idx[key]
            pre = sSP.n
            res = [0]

            def _(sy):
                sy.wait_ge(sSP.h, pre)
                res[0] = sU.inc(sy.dma_start(out=row_ap, in_=rowspill[idx:idx + 1, :]), 16)
            SYN(_)
            return res[0]

        war_free = [0, 0, 0, 0]
        warslot = [0]
        kst_gate = [0, 0]
        vst_gate = [0, 0]
        wv_free = [0, 0]
        qh_free = [0, 0]
        exp_free = [0, 0]
        pv_free = [0, 0, 0]
        mask_free = [0, 0]
        kv_free = [0, 0]
        upd_free = [0, 0]
        layer_e_marker = [0, 0]

        def load_war(src_ap):
            slot = warslot[0]
            warslot[0] = (slot + 1) % 4
            free = war_free[slot]
            cnt = [0]

            def _(g):
                if free:
                    g.wait_ge(s_t.h, free)
                cnt[0] = sW[slot].inc(g.dma_start(out=war[:, slot * D:(slot + 1) * D],
                                                  in_=src_ap), 16)
            GPS(_)
            return slot, cnt[0]

        def wsl(slot, dm):
            return war[:, slot * D + dm * 128: slot * D + (dm + 1) * 128]

        xsrcs = [x0, xres]
        xdsts = [xres, out]
        if n_layers == 1 or l0_out:
            xdsts[0] = out

        def emit_layer(l):
            L1 = (l == 1)
            xsrc, xdst = xsrcs[l], xdsts[l]
            wqk = w1qk if L1 else w0qk
            wvp = w1v if L1 else w0v
            wop = w1o if L1 else w0o
            wg_t = tg_d if L1 else w0g
            wu_t = tu_d if L1 else w0u
            wd_t = td_d if L1 else w0d

            if L1:
                pump_flush()
                mask_free[0] = max(mask_free[0], s_v.n)
                mask_free[1] = max(mask_free[1], s_v.n)
                barrier()
                xmark = layer_e_marker[0]

                def _(sy):
                    sy.wait_ge(s_mk.h, xmark)
                SYN(_)

            def _(v):
                s_v.inc(v.memset(knm[0:1, 0:32], 0.0))
            VEC(_)

            for t in range(NT):
                barrier()
                for f in range(DC):
                    slot = f % 2
                    my = [0]
                    tpre = s_t.n

                    def _(sy, f=f, slot=slot, my=my, tpre=tpre):
                        sy.wait_ge(s_t.h, tpre)
                        my[0] = sX[slot].inc(sy.dma_start(
                            out=xck[:, slot * T:(slot + 1) * T],
                            in_=xsrc[f * 128:(f + 1) * 128, t * T:(t + 1) * T]), 16)
                    SYN(_)

                    def _(sc, slot=slot, my=my):
                        sc.wait_ge(sX[slot].h, my[0])
                        s_s.inc(sc.activation(out=ftmp[:, slot * T:(slot + 1) * T],
                                              in_=xck[:, slot * T:(slot + 1) * T],
                                              func=AF.Square))
                    SCA(_)
                    mys = s_s.n

                    def _(tn, f=f, slot=slot, my=my, mys=mys):
                        tn.wait_ge(sX[slot].h, my[0])
                        tn.matmul(pr0[:], onescol_f[:], xck[:, slot * T:(slot + 1) * T],
                                  start=(f == 0), stop=(f == DC - 1),
                                  skip_group_check=True)
                        tn.wait_ge(s_s.h, mys)
                        s_t.inc(tn.matmul(pr1[:], onescol_f[:],
                                          ftmp[:, slot * T:(slot + 1) * T],
                                          start=(f == 0), stop=(f == DC - 1),
                                          skip_group_check=True))
                    TEN(_)
                stats_t = s_t.n

                def _(v, stats_t=stats_t):
                    v.wait_ge(s_t.h, stats_t)
                    v.tensor_scalar(out=rows[0][:], in0=pr0[:], scalar1=1.0 / D,
                                    scalar2=None, op0=ALU.mult)
                    v.tensor_mul(out=rows[2][:], in0=rows[0][:], in1=rows[0][:])
                    s_v.inc(v.scalar_tensor_tensor(out=rows[2][:], in0=pr1[:],
                                                   scalar=1.0 / D, in1=rows[2][:],
                                                   op0=ALU.mult, op1=ALU.subtract))
                VEC(_)
                v1 = s_v.n

                def _(v):
                    s_v.inc(v.tensor_scalar(out=rows[3][:], in0=rows[2][:],
                                            scalar1=1e-5, scalar2=None, op0=ALU.add))
                VEC(_)
                v1b = s_v.n

                def _(sc, v1b=v1b):
                    sc.wait_ge(s_v.h, v1b)
                    s_s.inc(sc.activation(out=rows[2][:], in_=rows[3][:], func=AF.Sqrt))
                SCA(_)
                s1 = s_s.n

                def _(v, s1=s1):
                    v.wait_ge(s_s.h, s1)
                    s_v.inc(v.reciprocal(out=rows[1][:], in_=rows[2][:]))
                VEC(_)
                vr = s_v.n

                def _(tn, vr=vr):
                    tn.wait_ge(s_v.h, vr)
                    tn.matmul(pw[4][:], onesrow_f[:], rows[0][:], start=True, stop=True)
                    s_t.inc(tn.matmul(pw[5][:], onesrow_f[:], rows[1][:],
                                      start=True, stop=True))
                TEN(_)
                t1 = s_t.n

                def _(v, t1=t1):
                    v.wait_ge(s_t.h, t1)
                    v.tensor_copy(out=bct[:, 0:T], in_=pw[4][:])
                    s_v.inc(v.tensor_copy(out=bct[:, T:2 * T], in_=pw[5][:]))
                VEC(_)
                bc_ready = s_v.n
                spill(rows[0][:], ("m", l, t), bc_ready)

                if L1:
                    def _(v, bc_ready=bc_ready):
                        v.wait_ge(s_v.h, bc_ready)
                        v.memset(mrun[:, 0:T], -1e30)
                        s_v.inc(v.memset(mrun[:, T:2 * T], 1e30))
                    VEC(_)
                for f in range(DC):
                    slot = f % 2
                    my = [0]
                    vpre = s_v.n

                    def _(sy, f=f, slot=slot, my=my, vpre=vpre):
                        sy.wait_ge(s_v.h, vpre)
                        my[0] = sX[slot].inc(sy.dma_start(
                            out=xck[:, slot * T:(slot + 1) * T],
                            in_=xsrc[f * 128:(f + 1) * 128, t * T:(t + 1) * T]), 16)
                    SYN(_)

                    def _(v, f=f, slot=slot, my=my, t=t, bc_ready=bc_ready):
                        v.wait_ge(sX[slot].h, my[0])
                        v.wait_ge(s_v.h, bc_ready)
                        v.tensor_sub(out=ftmp[:, slot * T:(slot + 1) * T],
                                     in0=xck[:, slot * T:(slot + 1) * T],
                                     in1=bct[:, 0:T])
                        ins = v.tensor_mul(out=xn[t][:, f * T:(f + 1) * T],
                                           in0=ftmp[:, slot * T:(slot + 1) * T],
                                           in1=bct[:, T:2 * T])
                        if L1:
                            v.tensor_tensor(out=mrun[:, 0:T],
                                            in0=ftmp[:, slot * T:(slot + 1) * T],
                                            in1=mrun[:, 0:T], op=ALU.max)
                            ins = v.tensor_tensor(out=mrun[:, T:2 * T],
                                                  in0=ftmp[:, slot * T:(slot + 1) * T],
                                                  in1=mrun[:, T:2 * T], op=ALU.min)
                        s_v.inc(ins)
                    VEC(_)
                xn_ready = s_v.n
                if stage == "LN" + str(t):
                    return

                if L1:
                    def _(v, xn_ready=xn_ready):
                        v.wait_ge(s_v.h, xn_ready)
                        v.tensor_scalar(out=mrun[:, T:2 * T], in0=mrun[:, T:2 * T],
                                        scalar1=-1.0, scalar2=None, op0=ALU.mult)
                        s_v.inc(v.tensor_tensor(out=mrun[:, 0:T], in0=mrun[:, 0:T],
                                                in1=mrun[:, T:2 * T], op=ALU.max))
                    VEC(_)
                    mv = s_v.n
                    for tb in range(4):
                        def _(tn, tb=tb, mv=mv):
                            tn.wait_ge(s_v.h, mv)
                            s_t.inc(tn.transpose(pw[tb][:, 0:128],
                                                 mrun[:, tb * 128:(tb + 1) * 128],
                                                 ident[:]))
                        TEN(_)
                    t2 = s_t.n
                    mcbase = mcol_ctr[0]
                    mcol_ctr[0] += 4

                    def _(v, t2=t2, mcbase=mcbase):
                        v.wait_ge(s_t.h, t2)
                        for tb in range(4):
                            ins = v.reduce_max(out=mcols[:, mcbase + tb:mcbase + tb + 1],
                                               in_=pw[tb][:, 0:128], axis=AX.X)
                        s_v.inc(ins)
                    VEC(_)
                    v2 = s_v.n
                    for tb in range(4):
                        def _(tn, tb=tb, v2=v2, mcbase=mcbase):
                            tn.wait_ge(s_v.h, v2)
                            s_t.inc(tn.matmul(pr0[0:1, tb * 128:(tb + 1) * 128],
                                              mcols[:, mcbase + tb:mcbase + tb + 1],
                                              ident[:], start=True, stop=True,
                                              skip_group_check=True))
                        TEN(_)
                    t3 = s_t.n

                    def _(v, t3=t3):
                        v.wait_ge(s_t.h, t3)
                        v.tensor_copy(out=rows[3][:], in_=pr0[:])
                        v.tensor_mul(out=rows[4][:], in0=rows[3][:], in1=rows[1][:])
                        v.tensor_scalar(out=rows[4][:], in0=rows[4][:], scalar1=1e-5,
                                        scalar2=None, op0=ALU.max)
                        v.reciprocal(out=rows[5][:], in_=rows[4][:])
                        s_v.inc(v.scalar_tensor_tensor(out=rows[5][:], in0=rows[5][:],
                                                       scalar=127.0, in1=rows[1][:],
                                                       op0=ALU.mult, op1=ALU.mult))
                    VEC(_)
                    v3 = s_v.n
                    spill(rows[4][:], ("mx", l, t), v3)
                    spill(rows[5][:], ("qscT", l, t), v3)
                else:
                    for f in range(DC):
                        def _(tn, f=f, t=t, xn_ready=xn_ready):
                            tn.wait_ge(s_v.h, xn_ready)
                            s_t.inc(tn.matmul(pr1[:], onescol_bf[:],
                                              xn[t][:, f * T:(f + 1) * T],
                                              start=(f == 0), stop=(f == DC - 1),
                                              skip_group_check=True))
                        TEN(_)
                    t2 = s_t.n

                    def _(v, t2=t2, t_=t):
                        v.wait_ge(s_t.h, t2)
                        s_v.inc(v.tensor_copy(out=srow_bf[t_][:], in_=pr1[:]))
                    VEC(_)

                for hmo in range(16):
                    slot, gcnt = load_war(wqk[16 + hmo])
                    vpre = s_v.n

                    def _(tn, slot=slot, gcnt=gcnt, t=t, xn_ready=xn_ready, vpre=vpre):
                        tn.wait_ge(sW[slot].h, gcnt)
                        tn.wait_ge(s_v.h, vpre)
                        for dm in range(DC):
                            ins = tn.matmul(pw[0][:], wsl(slot, dm),
                                            xn[t][:, dm * T:(dm + 1) * T],
                                            start=(dm == 0), stop=(dm == DC - 1),
                                            skip_group_check=True)
                        s_t.inc(ins)
                    TEN(_)
                    war_free[slot] = s_t.n
                    t1 = s_t.n
                    ks = hmo % 2

                    def _(v, hmo=hmo, t1=t1, ks=ks):
                        v.wait_ge(s_t.h, t1)
                        v.wait_ge(sKS[ks].h, kst_gate[ks])
                        v.wait_ge(sVS[ks].h, vst_gate[ks])
                        b = qkvb[:, 17 + hmo:18 + hmo] if L1 else 0.0
                        v.tensor_scalar(out=kvst[:, ks * T:(ks + 1) * T],
                                        in0=pw[0][:], scalar1=b, scalar2=None,
                                        op0=ALU.add)
                        s_v.inc(v.tensor_mul(out=bfs[:, 0:T],
                                             in0=kvst[:, ks * T:(ks + 1) * T],
                                             in1=kvst[:, ks * T:(ks + 1) * T]))
                    VEC(_)
                    s2 = s_v.n
                    v2 = s_v.n

                    def _(tn, v2=v2):
                        tn.wait_ge(s_v.h, v2)
                        s_t.inc(tn.matmul(pr1[:], onescol_bf[:], bfs[:, 0:T],
                                          start=True, stop=True, skip_group_check=True))
                    TEN(_)
                    t3 = s_t.n

                    def _(v, hmo=hmo, t3=t3):
                        v.wait_ge(s_t.h, t3)
                        c = s_v.inc(v.reduce_max(out=sc1[0:1, 4:5], in_=pr1[:],
                                                 axis=AX.X))
                        v.wait_ge(s_v.h, c)
                        s_v.inc(v.tensor_tensor(out=knm[0:1, hmo:hmo + 1],
                                                in0=knm[0:1, hmo:hmo + 1],
                                                in1=sc1[0:1, 4:5], op=ALU.max))
                    VEC(_)

                    def _(sy, hmo=hmo, s2=s2, ks=ks, t=t):
                        sy.wait_ge(s_v.h, s2)
                        kst_gate[ks] = sKS[ks].inc(sy.dma_start(
                            out=cck_i[hmo * 128:(hmo + 1) * 128, t * T:(t + 1) * T],
                            in_=kvst[:, ks * T:(ks + 1) * T]), 16)
                    SYN(_)
                    pump(6)
                if stage == "K" + str(t):
                    return

                prev_vevac = [s_v.n]
                for cg in range(4):
                    for dm in range(DC):
                        slot = dm % 2
                        my = [0]
                        free = wv_free[slot]

                        def _(g, dm=dm, cg=cg, slot=slot, my=my, free=free):
                            if free:
                                g.wait_ge(s_t.h, free)
                            my[0] = sWV[slot].inc(g.dma_start(
                                out=wvb[:, slot * T:(slot + 1) * T],
                                in_=wvp[dm * 128:(dm + 1) * 128, cg * T:(cg + 1) * T]), 16)
                        GPS(_)

                        def _(tn, dm=dm, slot=slot, my=my, t=t, xn_ready=xn_ready,
                              pe=prev_vevac[0]):
                            tn.wait_ge(sWV[slot].h, my[0])
                            if dm == 0:
                                tn.wait_ge(s_v.h, max(xn_ready, pe))
                            last = (dm == DC - 1) and (not L1)
                            for tb in range(4):
                                ins = tn.matmul(
                                    pw[tb][:],
                                    xn[t][:, dm * T + tb * 128: dm * T + (tb + 1) * 128],
                                    wvb[:, slot * T:(slot + 1) * T],
                                    start=(dm == 0), stop=last, skip_group_check=True)
                            s_t.inc(ins)
                        TEN(_)
                        wv_free[slot] = s_t.n
                    if L1:
                        def _(tn, cg=cg):
                            tn.wait_ge(sP2.h, setup_gp)
                            for tb in range(4):
                                ins = tn.matmul(pw[tb][:], onesrow_bf[:],
                                                bvr[0:1, cg * T:(cg + 1) * T],
                                                start=False, stop=True,
                                                skip_group_check=True)
                            s_t.inc(ins)
                        TEN(_)
                    t4 = s_t.n
                    for tb in range(4):
                        vs = tb % 2
                        if "x" in stage:
                            continue

                        def _(v, tb=tb, t4=t4, vs=vs):
                            v.wait_ge(s_t.h, t4)
                            v.wait_ge(sVS[vs].h, vst_gate[vs])
                            v.wait_ge(sKS[vs].h, kst_gate[vs])
                            s_v.inc(v.tensor_copy(out=kvst[:, vs * T:(vs + 1) * T],
                                                  in_=pw[tb][:]))
                        VEC(_)
                        v4 = s_v.n
                        if "y" in stage:
                            continue

                        def _(sy, tb=tb, cg=cg, v4=v4, vs=vs, t=t):
                            sy.wait_ge(s_v.h, v4)
                            vst_gate[vs] = sVS[vs].inc(sy.dma_start(
                                out=ccv_i[t * T + tb * 128: t * T + (tb + 1) * 128,
                                          cg * T:(cg + 1) * T],
                                in_=kvst[:, vs * T:(vs + 1) * T]), 16)
                        SYN(_)
                    prev_vevac[0] = s_v.n
                    pump(8)
                if stage.startswith("V") and stage.endswith(str(t)):
                    return

            vpre = s_v.n

            def _(g, vpre=vpre):
                g.wait_ge(s_v.h, vpre)
                sKN.inc(g.dma_start(out=cck_i[D:D + 1, 0:32], in_=knm[0:1, 0:32]), 16)
            GPS(_)
            ccmark = marker([sKS[0], sKS[1], sVS[0], sVS[1], sKN, sP])

            def _(g, ccmark=ccmark, l=l):
                g.wait_ge(s_mk.h, ccmark)
                if l == 1:
                    for sdep in (sKB[0], sKB[1], sVB[0], sVB[1]):
                        g.wait_ge(sdep.h, sdep.n)
                s_cc.inc(g.collective_compute("AllGather", ALU.bypass, ins=[cck_i],
                                              outs=[cck_o], replica_groups=RG), 1)
                s_cc.inc(g.collective_compute("AllGather", ALU.bypass, ins=[ccv_i],
                                              outs=[ccv_o], replica_groups=RG), 1)
                g.wait_ge(s_cc.h, s_cc.n)
            GPS(_)
            cc_done = s_cc.n
            if stage == "A":
                return

            for t in range(NT):
                barrier()
                if L1:
                    u1 = unspill(rows[0][:], ("m", l, t))
                    u2 = unspill(rows[5][:], ("qscT", l, t))
                    u3 = unspill(rows[4][:], ("mx", l, t))
                    vpre = s_v.n

                    def _(tn, u1=u1, u2=u2, u3=u3, vpre=vpre):
                        tn.wait_ge(sU.h, max(u1, u2, u3))
                        tn.wait_ge(s_v.h, vpre)
                        tn.matmul(pw[4][:], onesrow_f[:], rows[0][:], start=True,
                                  stop=True)
                        s_t.inc(tn.matmul(pw[5][:], onesrow_f[:], rows[5][:],
                                          start=True, stop=True))
                    TEN(_)
                    t1 = s_t.n

                    def _(v, t1=t1):
                        v.wait_ge(s_t.h, t1)
                        v.tensor_copy(out=bct[:, 0:T], in_=pw[4][:])
                        s_v.inc(v.tensor_copy(out=bct[:, 2 * T:3 * T], in_=pw[5][:]))
                    VEC(_)
                    bcq = s_v.n
                    for f in range(DC):
                        slot = f % 2
                        my = [0]
                        vpre = s_v.n

                        def _(sy, f=f, slot=slot, my=my, vpre=vpre):
                            sy.wait_ge(s_v.h, vpre)
                            my[0] = sX[slot].inc(sy.dma_start(
                                out=xck[:, slot * T:(slot + 1) * T],
                                in_=xsrc[f * 128:(f + 1) * 128, t * T:(t + 1) * T]), 16)
                        SYN(_)

                        def _(v, f=f, slot=slot, my=my, bcq=bcq):
                            v.wait_ge(sX[slot].h, my[0])
                            v.wait_ge(s_v.h, bcq)
                            v.tensor_sub(out=ftmp[:, slot * T:(slot + 1) * T],
                                         in0=xck[:, slot * T:(slot + 1) * T],
                                         in1=bct[:, 0:T])
                            v.tensor_mul(out=qi8[:, slot * T:(slot + 1) * T],
                                         in0=ftmp[:, slot * T:(slot + 1) * T],
                                         in1=bct[:, 2 * T:3 * T])
                            s_v.inc(v.tensor_copy(out=qg[:, f * T:(f + 1) * T],
                                                  in_=qi8[:, slot * T:(slot + 1) * T]))
                        VEC(_)
                    rhs = qg
                    def _(v):
                        v.tensor_scalar(out=rows[0][:], in0=rows[4][:],
                                        scalar1=wmean[0:1, 1:2], scalar2=None,
                                        op0=ALU.mult)
                        s_v.inc(v.tensor_scalar(out=rows[2][:], in0=rows[4][:],
                                                scalar1=wmean[0:1, 2:3], scalar2=None,
                                                op0=ALU.mult))
                    VEC(_)
                    v2 = s_v.n

                    def _(tn, v2=v2):
                        tn.wait_ge(s_v.h, v2)
                        tn.matmul(pw[4][:], onesrow_f[:], rows[0][:], start=True,
                                  stop=True)
                        s_t.inc(tn.matmul(pw[5][:], onesrow_f[:], rows[2][:],
                                          start=True, stop=True))
                    TEN(_)
                    t2 = s_t.n

                    def _(v, t2=t2):
                        v.wait_ge(s_t.h, t2)
                        v.tensor_copy(out=bct[:, 0:T], in_=pw[4][:])
                        v.tensor_copy(out=bct[:, T:2 * T], in_=pw[5][:])
                        v.memset(mrun[:, 0:T], -1e30)
                        s_v.inc(v.memset(mrun[:, T:2 * T], 1e30))
                    VEC(_)
                    cgu = s_v.n

                    def _(g):
                        g.wait_ge(s_mk.h, max(tern_marker.get(0, 0),
                                              tern_marker.get(1, 0)))
                    GPS(_)
                else:
                    rhs = xn[t]
                    cgu = s_v.n

                gu_free = [0, 0, 0, 0]
                for sl in range(4):
                    gu_free[sl] = war_free[sl]
                for ic in range(IC):
                    gsl = ic % 2
                    usl = 2 + (ic % 2)
                    mg = [0]
                    mu = [0]

                    def _(g, ic=ic, gsl=gsl, usl=usl, mg=mg, mu=mu,
                          f1=gu_free[gsl], f2=gu_free[usl]):
                        if f1:
                            g.wait_ge(s_t.h, f1)
                        mg[0] = sW[gsl].inc(g.dma_start(out=war[:, gsl * D:(gsl + 1) * D],
                                                        in_=wg_t[ic]), 16)
                        if f2:
                            g.wait_ge(s_t.h, f2)
                        mu[0] = sW[usl].inc(g.dma_start(out=war[:, usl * D:(usl + 1) * D],
                                                        in_=wu_t[ic]), 16)
                    GPS(_)

                    pg = ic % 2
                    pu = ic % 2
                    gb = pw[0] if pg == 0 else pw[2]
                    ub = pw[1] if pu == 0 else pw[3]

                    def _(tn, ic=ic, gsl=gsl, usl=usl, mg=mg, mu=mu, gb=gb, ub=ub,
                          cgu=cgu, t=t, ef=evac_free):
                        tn.wait_ge(sW[gsl].h, mg[0])
                        if ic == 0:
                            tn.wait_ge(s_v.h, cgu)
                        if ic >= 2:
                            tn.wait_ge(s_v.h, ef[ic % 2])
                            tn.wait_ge(s_s.h, ef_s[ic % 2])
                        for dm in range(DC):
                            ins = tn.matmul(gb[:], wsl(gsl, dm), rhs[:, dm * T:(dm + 1) * T],
                                            start=(dm == 0),
                                            stop=(L1 and dm == DC - 1),
                                            skip_group_check=True)
                        if not L1:
                            ins = tn.matmul(gb[:], m128row[:], srow_bf[t][:],
                                            start=False, stop=True, skip_group_check=True)
                        s_t.inc(ins)
                        gu_free[gsl] = s_t.n
                        tn.wait_ge(sW[usl].h, mu[0])
                        for dm in range(DC):
                            ins = tn.matmul(ub[:], wsl(usl, dm), rhs[:, dm * T:(dm + 1) * T],
                                            start=(dm == 0),
                                            stop=(L1 and dm == DC - 1),
                                            skip_group_check=True)
                        if not L1:
                            ins = tn.matmul(ub[:], m128row[:], srow_bf[t][:],
                                            start=False, stop=True, skip_group_check=True)
                        s_t.inc(ins)
                        gu_free[usl] = s_t.n
                    TEN(_)
                    t1 = s_t.n

                    if not L1:
                        def _(sc, ic=ic, t1=t1, gb=gb):
                            sc.wait_ge(s_t.h, t1)
                            s_s.inc(sc.activation(out=bfs[:, 0:T], in_=gb[:],
                                                  func=AF.Silu, bias=0.0,
                                                  scale=gsc[:, ic + 1:ic + 2]))
                        SCA(_)
                        ef_s[ic % 2] = s_s.n
                        s2 = s_s.n

                        def _(v, ic=ic, t1=t1, s2=s2, ub=ub):
                            v.wait_ge(s_t.h, t1)
                            v.tensor_scalar(out=bfs[:, T:2 * T], in0=ub[:],
                                            scalar1=usc[:, ic + 1:ic + 2], scalar2=None,
                                            op0=ALU.mult)
                            v.wait_ge(s_s.h, s2)
                            s_v.inc(v.tensor_mul(out=har[:, ic * T:(ic + 1) * T],
                                                 in0=bfs[:, 0:T], in1=bfs[:, T:2 * T]))
                        VEC(_)
                        evac_free[ic % 2] = s_v.n
                        v2 = s_v.n

                        def _(tn, ic=ic, v2=v2):
                            tn.wait_ge(s_v.h, v2)
                            s_t.inc(tn.matmul(pr1[:], onescol_bf[:],
                                              har[:, ic * T:(ic + 1) * T],
                                              start=(ic == 0), stop=(ic == IC - 1),
                                              skip_group_check=True))
                        TEN(_)
                    else:
                        def _(v, ic=ic, t1=t1, gb=gb):
                            v.wait_ge(s_t.h, t1)
                            s_v.inc(v.tensor_mul(out=ftmp[:, 0:T], in0=gb[:],
                                                 in1=bct[:, 0:T]))
                        VEC(_)
                        v1 = s_v.n

                        def _(sc, v1=v1):
                            sc.wait_ge(s_v.h, v1)
                            s_s.inc(sc.activation(out=bfs[:, 0:T], in_=ftmp[:, 0:T],
                                                  func=AF.Silu))
                        SCA(_)
                        ef_s[ic % 2] = s_s.n
                        s2 = s_s.n

                        def _(v, ic=ic, s2=s2, ub=ub):
                            v.tensor_mul(out=ftmp[:, T:2 * T], in0=ub[:],
                                         in1=bct[:, T:2 * T])
                            v.wait_ge(s_s.h, s2)
                            v.tensor_mul(out=har[:, ic * T:(ic + 1) * T],
                                         in0=ftmp[:, T:2 * T], in1=bfs[:, 0:T])
                            v.tensor_mul(out=bfs[:, 2 * T:3 * T],
                                         in0=har[:, ic * T:(ic + 1) * T],
                                         in1=har[:, ic * T:(ic + 1) * T])
                            v.scalar_tensor_tensor(out=ftmp[:, 0:T],
                                                   in0=har[:, ic * T:(ic + 1) * T],
                                                   scalar=gam[:, ic + 1:ic + 2],
                                                   in1=mrun[:, 0:T],
                                                   op0=ALU.mult, op1=ALU.max)
                            v.tensor_copy(out=mrun[:, 0:T], in_=ftmp[:, 0:T])
                            v.scalar_tensor_tensor(out=ftmp[:, 0:T],
                                                   in0=har[:, ic * T:(ic + 1) * T],
                                                   scalar=gam[:, ic + 1:ic + 2],
                                                   in1=mrun[:, T:2 * T],
                                                   op0=ALU.mult, op1=ALU.min)
                            s_v.inc(v.tensor_copy(out=mrun[:, T:2 * T], in_=ftmp[:, 0:T]))
                        VEC(_)
                        evac_free[ic % 2] = s_v.n
                        v2 = s_v.n

                        def _(tn, ic=ic, v2=v2):
                            tn.wait_ge(s_v.h, v2)
                            s_t.inc(tn.matmul(pr1[:], onescol_bf[:], bfs[:, 2 * T:3 * T],
                                              start=(ic == 0), stop=(ic == IC - 1),
                                              skip_group_check=True))
                        TEN(_)
                    pump(2)
                hs_t = s_t.n
                hs_v = s_v.n

                if not L1:
                    def _(v, hs_t=hs_t, t_=t):
                        v.wait_ge(s_t.h, hs_t)
                        s_v.inc(v.tensor_copy(out=srow_bf[t_][:], in_=pr1[:]))
                    VEC(_)
                else:
                    def _(v, hs_t=hs_t):
                        v.wait_ge(s_t.h, hs_t)
                        s_v.inc(v.tensor_scalar(out=rows[3][:], in0=pr1[:],
                                                scalar1=1.0 / II, scalar2=1e-6,
                                                op0=ALU.mult, op1=ALU.add))
                    VEC(_)
                    rss_v = s_v.n

                    def _(sc, rss_v=rss_v):
                        sc.wait_ge(s_v.h, rss_v)
                        s_s.inc(sc.activation(out=rows[6][:], in_=rows[3][:],
                                              func=AF.Sqrt))
                    SCA(_)
                    rstd_ss = s_s.n

                    def _(v, rstd_ss=rstd_ss):
                        v.wait_ge(s_s.h, rstd_ss)
                        s_v.inc(v.reciprocal(out=rows[2][:], in_=rows[6][:]))
                    VEC(_)
                    rstd_s = s_s.n

                    def _(v, hs_v=hs_v):
                        v.wait_ge(s_v.h, hs_v)
                        v.tensor_scalar(out=mrun[:, T:2 * T], in0=mrun[:, T:2 * T],
                                        scalar1=-1.0, scalar2=None, op0=ALU.mult)
                        s_v.inc(v.tensor_tensor(out=mrun[:, 0:T], in0=mrun[:, 0:T],
                                                in1=mrun[:, T:2 * T], op=ALU.max))
                    VEC(_)
                    mv = s_v.n
                    for tb in range(4):
                        def _(tn, tb=tb, mv=mv):
                            tn.wait_ge(s_v.h, mv)
                            s_t.inc(tn.transpose(pw[tb][:, 0:128],
                                                 mrun[:, tb * 128:(tb + 1) * 128],
                                                 ident[:]))
                        TEN(_)
                    t2 = s_t.n
                    mcbase = mcol_ctr[0]
                    mcol_ctr[0] += 4

                    def _(v, t2=t2, mcbase=mcbase):
                        v.wait_ge(s_t.h, t2)
                        for tb in range(4):
                            ins = v.reduce_max(out=mcols[:, mcbase + tb:mcbase + tb + 1],
                                               in_=pw[tb][:, 0:128], axis=AX.X)
                        s_v.inc(ins)
                    VEC(_)
                    v2 = s_v.n
                    for tb in range(4):
                        def _(tn, tb=tb, v2=v2, mcbase=mcbase):
                            tn.wait_ge(s_v.h, v2)
                            s_t.inc(tn.matmul(pr0[0:1, tb * 128:(tb + 1) * 128],
                                              mcols[:, mcbase + tb:mcbase + tb + 1],
                                              ident[:], start=True, stop=True,
                                              skip_group_check=True))
                        TEN(_)
                    t3 = s_t.n

                    def _(v, t3=t3, rstd_s=rstd_s):
                        v.wait_ge(s_t.h, t3)
                        v.wait_ge(s_s.h, rstd_s)
                        v.tensor_copy(out=rows[3][:], in_=pr0[:])
                        v.tensor_mul(out=rows[4][:], in0=rows[3][:], in1=rows[2][:])
                        v.tensor_scalar(out=rows[4][:], in0=rows[4][:], scalar1=1e-5,
                                        scalar2=None, op0=ALU.max)
                        v.reciprocal(out=rows[5][:], in_=rows[4][:])
                        v.scalar_tensor_tensor(out=rows[5][:], in0=rows[5][:],
                                               scalar=127.0, in1=rows[2][:],
                                               op0=ALU.mult, op1=ALU.mult)
                        s_v.inc(v.tensor_scalar(out=rows[6][:], in0=rows[4][:],
                                                scalar1=wmean[0:1, 3:4], scalar2=None,
                                                op0=ALU.mult))
                    VEC(_)
                    v3 = s_v.n

                    def _(tn, v3=v3):
                        tn.wait_ge(s_v.h, v3)
                        tn.matmul(pw[4][:], onesrow_f[:], rows[5][:], start=True,
                                  stop=True)
                        s_t.inc(tn.matmul(pw[5][:], onesrow_f[:], rows[6][:],
                                          start=True, stop=True))
                    TEN(_)
                    t4 = s_t.n

                    def _(v, t4=t4):
                        v.wait_ge(s_t.h, t4)
                        v.tensor_copy(out=bct[:, 2 * T:3 * T], in_=pw[4][:])
                        s_v.inc(v.tensor_copy(out=bct[:, 3 * T:4 * T], in_=pw[5][:]))
                    VEC(_)
                    qsch_v = s_v.n

                    for ic2 in range(IC):
                        slot = ic2 % 2

                        def _(v, ic2=ic2, slot=slot, qsch_v=qsch_v):
                            if ic2 == 0:
                                v.wait_ge(s_v.h, qsch_v)
                            v.scalar_tensor_tensor(out=qi8[:, slot * T:(slot + 1) * T],
                                                   in0=har[:, ic2 * T:(ic2 + 1) * T],
                                                   scalar=gam[:, ic2 + 1:ic2 + 2],
                                                   in1=bct[:, 2 * T:3 * T],
                                                   op0=ALU.mult, op1=ALU.mult)
                            s_v.inc(v.tensor_copy(out=har[:, ic2 * T:(ic2 + 1) * T],
                                                  in_=qi8[:, slot * T:(slot + 1) * T]))
                        VEC(_)
                    def _(g):
                        g.wait_ge(s_mk.h, tern_marker.get(2, 0))
                    GPS(_)
                qdone_v = s_v.n

                dn_free = list(gu_free)
                for hf in range(DC):
                    pd = pw[4] if hf % 2 == 0 else pw[5]
                    for sub in range(4):
                        my = [0]
                        free = dn_free[sub]

                        def _(g, hf=hf, sub=sub, my=my, free=free):
                            if free:
                                g.wait_ge(s_t.h, free)
                            my[0] = sW[sub].inc(g.dma_start(
                                out=war[:, sub * D:(sub + 1) * D],
                                in_=wd_t[hf, :, sub * D:(sub + 1) * D]), 16)
                        GPS(_)

                        def _(tn, hf=hf, sub=sub, my=my, pd=pd, qdone_v=qdone_v, t=t,
                              dse=down_evac):
                            tn.wait_ge(sW[sub].h, my[0])
                            if sub == 0:
                                tn.wait_ge(s_v.h, qdone_v)
                                if hf >= 2 and dse[hf % 2] is not None:
                                    tn.wait_ge(dse[hf % 2][0].h, dse[hf % 2][1])
                                    tn.wait_ge(s_s.h, dse_s[hf % 2])
                            for k in range(DC):
                                ic2 = sub * DC + k
                                ins = tn.matmul(pd[:], wsl(sub, k),
                                                har[:, ic2 * T:(ic2 + 1) * T],
                                                start=(ic2 == 0),
                                                stop=(L1 and ic2 == IC - 1),
                                                skip_group_check=True)
                            if sub == 3 and not L1:
                                ins = tn.matmul(pd[:], m128row[:], srow_bf[t][:],
                                                start=False, stop=True,
                                                skip_group_check=True)
                            s_t.inc(ins)
                            dn_free[sub] = s_t.n
                        TEN(_)
                    t1 = s_t.n

                    if not L1:
                        def _(sc, hf=hf, t1=t1, pd=pd):
                            sc.wait_ge(s_t.h, t1)
                            s_s.inc(sc.activation(out=qg[:, hf * T:(hf + 1) * T],
                                                  in_=pd[:], func=AF.Silu, bias=0.0,
                                                  scale=dsc[:, hf + 1:hf + 2]))
                        SCA(_)
                        down_evac[hf % 2] = (s_s, s_s.n)
                        dse_s[hf % 2] = s_s.n
                    else:
                        def _(v, hf=hf, t1=t1, pd=pd):
                            v.wait_ge(s_t.h, t1)
                            s_v.inc(v.tensor_mul(out=ftmp[:, 0:T], in0=pd[:],
                                                 in1=bct[:, 3 * T:4 * T]))
                        VEC(_)
                        down_evac[hf % 2] = (s_v, s_v.n)
                        v1 = s_v.n

                        def _(sc, hf=hf, v1=v1):
                            sc.wait_ge(s_v.h, v1)
                            s_s.inc(sc.activation(out=qg[:, hf * T:(hf + 1) * T],
                                                  in_=ftmp[:, 0:T], func=AF.Silu))
                        SCA(_)
                        dse_s[hf % 2] = s_s.n
                    pump(4)
                for sl in range(4):
                    war_free[sl] = dn_free[sl]
                g_done = s_s.n
                if stage == "C":
                    return

                barrier()
                my = [0]

                def _(sy, cc_done=cc_done, my=my):
                    sy.wait_ge(s_cc.h, cc_done)
                    sCM.inc(sy.dma_start(out=ccm_sb[0:1, 0:16],
                                         in_=cck_o[D:D + 1, 0:16]), 16)
                    my[0] = sCM.inc(sy.dma_start(out=ccm_sb[0:1, 16:32],
                                                 in_=cck_o[2 * D + 16:2 * D + 17, 0:16]), 16)
                SYN(_)
                sy1 = sCM.n

                def _(v, sy1=sy1):
                    v.wait_ge(sCM.h, sy1)
                    s_v.inc(v.tensor_tensor(out=knm[0:1, 16:32], in0=ccm_sb[0:1, 0:16],
                                            in1=ccm_sb[0:1, 16:32], op=ALU.max))
                VEC(_)
                knall_v = s_v.n

                for h in range(NH):
                    vpre, spre = s_v.n, s_s.n

                    def _(tn, vpre=vpre, spre=spre):
                        tn.wait_ge(s_v.h, vpre)
                        tn.wait_ge(s_s.h, spre)
                    TEN(_)

                    slot, gcnt = load_war(wqk[h])
                    qs = h % 2

                    def _(tn, slot=slot, gcnt=gcnt, t=t):
                        tn.wait_ge(sW[slot].h, gcnt)
                        for dm in range(DC):
                            ins = tn.matmul(pw[3][:], wsl(slot, dm),
                                            xn[t][:, dm * T:(dm + 1) * T],
                                            start=(dm == 0), stop=(dm == DC - 1),
                                            skip_group_check=True)
                        s_t.inc(ins)
                    TEN(_)
                    war_free[slot] = s_t.n
                    t1 = s_t.n

                    def _(v, h=h, t1=t1, qs=qs, qf=qh_free[qs]):
                        v.wait_ge(s_t.h, max(t1, qf))
                        b = qkvb[:, h + 1:h + 2] if L1 else 0.0
                        v.tensor_scalar(out=qhb[:, qs * T:(qs + 1) * T],
                                        in0=pw[3][:], scalar1=b, scalar2=None,
                                        op0=ALU.add)
                        s_v.inc(v.tensor_mul(out=bfs[:, 0:T],
                                             in0=qhb[:, qs * T:(qs + 1) * T],
                                             in1=qhb[:, qs * T:(qs + 1) * T]))
                    VEC(_)
                    v1 = s_v.n

                    def _(tn, v1=v1):
                        tn.wait_ge(s_v.h, v1)
                        s_t.inc(tn.matmul(pr0[:], onescol_bf[:], bfs[:, 0:T],
                                          start=True, stop=True, skip_group_check=True))
                    TEN(_)
                    t2 = s_t.n

                    def _(v, h=h, t2=t2, knall_v=knall_v):
                        v.wait_ge(s_t.h, t2)
                        v.wait_ge(s_v.h, knall_v)
                        c = s_v.inc(v.reduce_max(out=sc1[0:1, 1:2], in_=pr0[:],
                                                 axis=AX.X))
                        v.wait_ge(s_v.h, c)
                        s_v.inc(v.tensor_tensor(out=sc1[0:1, 2:3], in0=sc1[0:1, 1:2],
                                                in1=knm[0:1, 16 + h:17 + h],
                                                op=ALU.mult))
                    VEC(_)
                    v2 = s_v.n

                    def _(sc, v2=v2):
                        sc.wait_ge(s_v.h, v2)
                        s_s.inc(sc.activation(out=sc1[0:1, 3:4], in_=sc1[0:1, 2:3],
                                              func=AF.Sqrt))
                    SCA(_)
                    s2 = s_s.n

                    def _(v, s2=s2):
                        v.wait_ge(s_s.h, s2)
                        s_v.inc(v.tensor_scalar(out=sc1[0:1, 4:5], in0=sc1[0:1, 3:4],
                                                scalar1=-SCALE, scalar2=None,
                                                op0=ALU.mult))
                    VEC(_)
                    v3 = s_v.n

                    def _(tn, v3=v3):
                        tn.wait_ge(s_v.h, v3)
                        s_t.inc(tn.matmul(pw[5][:, 0:1], onesrow_f[:], sc1[0:1, 4:5],
                                          start=True, stop=True))
                    TEN(_)
                    t3 = s_t.n

                    def _(v, t3=t3):
                        v.wait_ge(s_t.h, t3)
                        s_v.inc(v.tensor_copy(out=biascol[:], in_=pw[5][:, 0:1]))
                    VEC(_)
                    bias_v = s_v.n

                    first = [True]
                    for r in range(2):
                        myk = [0]
                        myv_ = [0]

                        def _(sy, h=h, r=r, myk=myk, myv_=myv_, cc_done=cc_done,
                              kf=kv_free[r]):
                            sy.wait_ge(s_cc.h, cc_done)
                            if kf:
                                sy.wait_ge(s_t.h, kf)
                            myk[0] = sKB[r].inc(sy.dma_start(
                                out=kbuf[:, r * TPC:(r + 1) * TPC],
                                in_=cck_o[r * (D + 16) + h * 128:
                                          r * (D + 16) + (h + 1) * 128, :]), 16)
                            inap = ccv_o[r * TPC:(r + 1) * TPC,
                                         h * 128:(h + 1) * 128].rearrange(
                                "(b p) d -> p b d", p=128)
                            outap = vbuf[:, r * TPC:(r + 1) * TPC].rearrange(
                                "p (b d) -> p b d", d=128)
                            myv_[0] = sVB[r].inc(sy.dma_start(out=outap, in_=inap), 16)
                        SYN(_)

                        for kb in range(8):
                            gkb = r * 8 + kb
                            psl = gkb % 2
                            pslot = gkb % 3
                            sb = pw[0] if psl == 0 else pw[1]
                            mym = [0]
                            if L1:
                                def _(sy, t=t, gkb=gkb, mym=mym, mf=mask_free[gkb % 2]):
                                    if mf:
                                        sy.wait_ge(s_v.h, mf)
                                    mym[0] = sM[gkb % 2].inc(sy.dma_start(
                                        out=mbuf[:, (gkb % 2) * T:(gkb % 2 + 1) * T],
                                        in_=mask_d[t * 16 + gkb]), 16)
                                SYN(_)

                            def _(tn, kb=kb, r=r, sb=sb, myk=myk, qs=qs,
                                  ef=exp_free[psl]):
                                tn.wait_ge(sKB[r].h, myk[0])
                                if ef:
                                    tn.wait_ge(s_s.h, ef)
                                s_t.inc(tn.matmul(
                                    sb[:],
                                    kbuf[:, r * TPC + kb * 128: r * TPC + (kb + 1) * 128],
                                    qhb[:, qs * T:(qs + 1) * T],
                                    start=True, stop=True, skip_group_check=True))
                            TEN(_)
                            t4 = s_t.n

                            def _(sc, sb=sb, psl=psl, pslot=pslot, t4=t4,
                                  bias_v=bias_v, pf=pv_free[pslot]):
                                sc.wait_ge(s_t.h, t4)
                                sc.wait_ge(s_v.h, bias_v)
                                if pf:
                                    sc.wait_ge(s_t.h, pf)
                                ins = sc.activation(out=pbuf[:, pslot * T:(pslot + 1) * T],
                                                    in_=sb[:], func=AF.Exp,
                                                    bias=biascol[:], scale=sccol[:])
                                exp_free[psl] = s_s.inc(ins)
                            SCA(_)
                            s3 = s_s.n

                            if L1:
                                def _(v, pslot=pslot, gkb=gkb, s3=s3, mym=mym):
                                    v.wait_ge(s_s.h, s3)
                                    v.wait_ge(sM[gkb % 2].h, mym[0])
                                    ins = v.tensor_tensor(
                                        out=pbuf[:, pslot * T:(pslot + 1) * T],
                                        in0=pbuf[:, pslot * T:(pslot + 1) * T],
                                        in1=mbuf[:, (gkb % 2) * T:(gkb % 2 + 1) * T],
                                        op=ALU.mult)
                                    mask_free[gkb % 2] = s_v.inc(ins)
                                VEC(_)
                                pvw = ("v", s_v.n)
                            else:
                                pvw = ("s", s3)

                            def _(tn, kb=kb, r=r, pslot=pslot, myv_=myv_, pvw=pvw,
                                  first=first, gkb=gkb):
                                if pvw[0] == "v":
                                    tn.wait_ge(s_v.h, pvw[1])
                                else:
                                    tn.wait_ge(s_s.h, pvw[1])
                                tn.wait_ge(sVB[r].h, myv_[0])
                                tn.matmul(pr1[:], onescol_bf[:],
                                          pbuf[:, pslot * T:(pslot + 1) * T],
                                          start=first[0], stop=(gkb == 15),
                                          skip_group_check=True)
                                ins = tn.matmul(
                                    pw[4][:],
                                    vbuf[:, r * TPC + kb * 128: r * TPC + (kb + 1) * 128],
                                    pbuf[:, pslot * T:(pslot + 1) * T],
                                    start=first[0], stop=(gkb == 15),
                                    skip_group_check=True)
                                pv_free[pslot] = s_t.inc(ins)
                                first[0] = False
                            TEN(_)
                        kv_free[r] = s_t.n
                    qh_free[qs] = s_t.n
                    pvdone = s_t.n

                    def _(v, pvdone=pvdone):
                        v.wait_ge(s_t.h, pvdone)
                        s_v.inc(v.reciprocal(out=rows[2][:], in_=pr1[:]))
                    VEC(_)
                    v4 = s_v.n

                    def _(tn, v4=v4):
                        tn.wait_ge(s_v.h, v4)
                        s_t.inc(tn.matmul(pw[5][:], onesrow_f[:], rows[2][:],
                                          start=True, stop=True))
                    TEN(_)
                    t5 = s_t.n

                    def _(v, h=h, t5=t5):
                        v.wait_ge(s_t.h, t5)
                        v.tensor_copy(out=bct[:, 3 * T:4 * T], in_=pw[5][:])
                        s_v.inc(v.tensor_mul(out=har[:, (16 + h) * T:(17 + h) * T],
                                             in0=pw[4][:], in1=bct[:, 3 * T:4 * T]))
                    VEC(_)
                    pump(2)
                ao_done = s_v.n

                for mo in range(DC):
                    slot, gcnt = load_war(wop[mo])

                    def _(tn, slot=slot, gcnt=gcnt, ao_done=ao_done, mo=mo,
                          spre=s_v.n):
                        tn.wait_ge(sW[slot].h, gcnt)
                        if mo == 0:
                            tn.wait_ge(s_v.h, ao_done)
                        else:
                            tn.wait_ge(s_v.h, spre)
                        for dm in range(DC):
                            ins = tn.matmul(pw[3][:], wsl(slot, dm),
                                            har[:, (16 + dm) * T:(17 + dm) * T],
                                            start=(dm == 0), stop=(dm == DC - 1),
                                            skip_group_check=True)
                        s_t.inc(ins)
                    TEN(_)
                    war_free[slot] = s_t.n
                    t1 = s_t.n

                    def _(v, mo=mo, t1=t1):
                        v.wait_ge(s_t.h, t1)
                        b = obc[:, mo + 1:mo + 2] if L1 else 0.0
                        s_v.inc(v.tensor_scalar(out=har[:, (32 + mo) * T:(33 + mo) * T],
                                                in0=pw[3][:], scalar1=b, scalar2=None,
                                                op0=ALU.add))
                    VEC(_)
                    pump(2)
                s_done = s_v.n
                if stage == "D":
                    return

                for f in range(DC):
                    slot = f % 2
                    my = [0]

                    def _(sy, f=f, slot=slot, my=my, uf=upd_free[slot]):
                        if uf:
                            sy.wait_ge(sUP[slot].h, uf)
                        my[0] = sX[slot].inc(sy.dma_start(
                            out=xck[:, slot * T:(slot + 1) * T],
                            in_=xsrc[f * 128:(f + 1) * 128, t * T:(t + 1) * T]), 16)
                    SYN(_)

                    def _(v, f=f, slot=slot, my=my, s_done=s_done, g_done=g_done):
                        v.wait_ge(s_s.h, g_done)
                        v.wait_ge(s_v.h, s_done)
                        v.wait_ge(sX[slot].h, my[0])
                        if stage == "dumps" and l == n_layers - 1:
                            s_v.inc(v.tensor_copy(out=xck[:, slot * T:(slot + 1) * T],
                                                  in_=har[:, (32 + f) * T:(33 + f) * T]))
                        elif stage == "dumpg" and l == n_layers - 1:
                            s_v.inc(v.tensor_copy(out=xck[:, slot * T:(slot + 1) * T],
                                                  in_=qg[:, f * T:(f + 1) * T]))
                        else:
                            v.tensor_mul(out=ftmp[:, slot * T:(slot + 1) * T],
                                         in0=har[:, (32 + f) * T:(33 + f) * T],
                                         in1=qg[:, f * T:(f + 1) * T])
                            s_v.inc(v.tensor_add(out=xck[:, slot * T:(slot + 1) * T],
                                                 in0=xck[:, slot * T:(slot + 1) * T],
                                                 in1=ftmp[:, slot * T:(slot + 1) * T]))
                    VEC(_)
                    v1 = s_v.n

                    def _(sy, f=f, slot=slot, v1=v1, t=t):
                        sy.wait_ge(s_v.h, v1)
                        upd_free[slot] = sUP[slot].inc(sy.dma_start(
                            out=xdst[f * 128:(f + 1) * 128, t * T:(t + 1) * T],
                            in_=xck[:, slot * T:(slot + 1) * T]), 16)
                    SYN(_)
                    pump(10)
            layer_e_marker[l] = marker([sUP[0], sUP[1]])

        mcol_ctr = [8]
        evac_free = [0, 0]
        ef_s = [0, 0]
        down_evac = [None, None]
        dse_s = [0, 0]

        for l in range(n_layers):
            emit_layer(l)
        pump_flush()

        if stage == "dumpt":
            pre = (s_mk.n, sUP[0].n, sUP[1].n)

            def _(g, pre=pre):
                g.wait_ge(s_mk.h, pre[0])
                g.wait_ge(sUP[0].h, pre[1])
                g.wait_ge(sUP[1].h, pre[2])
                for mo in range(16):
                    sKN.inc(g.dma_start(out=out[mo * 128:(mo + 1) * 128, :],
                                        in_=tg_d[mo, :, 0:TPC]), 16)
                for i in range(3):
                    sKN.inc(g.dma_start(out=out[0:128, 1020 + i:1021 + i],
                                        in_=swsp[i][:],
                                        allow_slow_non_contiguous=True), 16)
                sKN.inc(g.dma_start(out=out[128:129, 1008:1016],
                                    in_=wmean[0:1, :]), 16)
                sKN.inc(g.dma_start(out=out[129:130, 1008:1016],
                                    in_=sc1[0:1, :]), 16)
                sKN.inc(g.dma_start(out=out[130:258, 1016:1017],
                                    in_=acc[:, 0:1],
                                    allow_slow_non_contiguous=True), 16)
            GPS(_)
            marker([sKN])

        marker([sKS[0], sKS[1], sVS[0], sVS[1], sUP[0], sUP[1], sSP, sTS, sKN])
        mk_fin = s_mk.n

        def _(sy):
            sy.wait_ge(s_mk.h, mk_fin)
        SYN(_)

    return nc, sbuf_left


def _tile_lhsT(w):
    K, M = w.shape
    Ko, Mo = K // 128, M // 128
    return np.ascontiguousarray(
        w.reshape(Ko, 128, Mo, 128).transpose(2, 1, 0, 3).reshape(Mo, 128, K))


def _u8(wi32):
    return np.ascontiguousarray(wi32.view(np.uint8)[..., 0::4])


def _padcol(a):
    return np.ascontiguousarray(
        np.concatenate([np.zeros((a.shape[0], 1), a.dtype), a], axis=1))


def make_inputs(kw):
    f32 = np.float32
    x = np.asarray(kw["L0_attn_x"], f32)
    w0qkv = np.asarray(kw["L0_attn_w_qkv"], f32)
    w0o = np.asarray(kw["L0_attn_w_out"], f32)
    g0 = _u8(np.ascontiguousarray(np.asarray(kw["L0_ffn_gate_w"], np.int32)))
    u0 = _u8(np.ascontiguousarray(np.asarray(kw["L0_ffn_up_w"], np.int32)))
    d0 = _u8(np.ascontiguousarray(np.asarray(kw["L0_ffn_down_w"], np.int32)))
    gs = np.asarray(kw["L0_ffn_gate_scale"], f32)
    us = np.asarray(kw["L0_ffn_up_scale"], f32)
    ds = np.asarray(kw["L0_ffn_down_scale"], f32)
    w1qkv = np.asarray(kw["L1_attn_attn_qkvw"], f32)
    qkvb = np.asarray(kw["L1_attn_attn_qkvb"], f32)
    w1o = np.asarray(kw["L1_attn_attn_ow"], f32)
    ob = np.asarray(kw["L1_attn_attn_ob"], f32)
    wg1 = np.asarray(kw["L1_ffn_w_gate"], f32)
    wu1 = np.asarray(kw["L1_ffn_w_up"], f32)
    wd1 = np.asarray(kw["L1_ffn_w_down"], f32)
    gam = np.asarray(kw["L1_ffn_ln_weight"], f32)

    shared = {
        "w0qk": _tile_lhsT(w0qkv[:, 0:4096]),
        "w0v": np.ascontiguousarray(w0qkv[:, 4096:6144]),
        "w0o": _tile_lhsT(w0o),
        "w0g": _tile_lhsT(np.ascontiguousarray(g0.T)),
        "w0u": _tile_lhsT(np.ascontiguousarray(u0.T)),
        "w0d": _tile_lhsT(np.ascontiguousarray(d0.T)),
        "gs_col": _padcol(np.ascontiguousarray(gs.reshape(64, 128).T)),
        "us_col": _padcol(np.ascontiguousarray(us.reshape(64, 128).T)),
        "ds_col": _padcol(np.ascontiguousarray(ds.reshape(16, 128).T)),
        "w1qk": _tile_lhsT(w1qkv[:, 0:4096]),
        "w1v": np.ascontiguousarray(w1qkv[:, 4096:6144]),
        "w1o": _tile_lhsT(w1o),
        "w1g": _tile_lhsT(np.ascontiguousarray(wg1.T)),
        "w1u": _tile_lhsT(np.ascontiguousarray(wu1.T)),
        "w1d": _tile_lhsT(np.ascontiguousarray(wd1.T)),
        "qkvb_col": _padcol(np.ascontiguousarray(qkvb[0:4096].reshape(32, 128).T)),
        "bv_row": np.ascontiguousarray(qkvb[4096:6144].reshape(1, 2048)),
        "ob_col": _padcol(np.ascontiguousarray(ob.reshape(16, 128).T)),
        "gam_col": _padcol(np.ascontiguousarray(gam.reshape(64, 128).T)),
        "ident_in": np.eye(128, dtype=f32),
        "dummy_d": np.zeros((1, 64), f32),
    }
    in_maps = []
    for c in range(8):
        b, half = c // 2, c % 2
        qbase = half * TPC
        m = dict(shared)
        m["x0"] = np.ascontiguousarray(x[b, qbase:qbase + TPC, :].T)
        msk = np.zeros((32, 128, T), dtype=ml_dtypes.bfloat16)
        for t in range(NT):
            for gkb in range(16):
                kpos = gkb * 128 + np.arange(128)[:, None]
                qpos = qbase + t * T + np.arange(T)[None, :]
                msk[t * 16 + gkb] = (kpos <= qpos).astype(ml_dtypes.bfloat16)
        m["mask_d"] = msk
        in_maps.append(m)
    return in_maps


_CACHE = {}


def kernel(**kw):
    if "nc" not in _CACHE:
        _CACHE["nc"] = build()[0]
    nc = _CACHE["nc"]
    in_maps = make_inputs(kw)
    res = run_bass_kernel_spmd(nc, in_maps, core_ids=list(range(8)))
    outp = np.zeros((B, S, D), np.float32)
    for c in range(8):
        b, half = c // 2, c % 2
        outp[b, half * TPC:(half + 1) * TPC, :] = res.results[c]["out"].T
    return outp


# version 04
# speedup vs baseline: 1.0535x; 1.0406x over previous
import numpy as np
import ml_dtypes
from contextlib import ExitStack

import concourse.bass as bass
import concourse.mybir as mybir
from concourse.bass_utils import run_bass_kernel_spmd

F32 = mybir.dt.float32
BF16 = mybir.dt.bfloat16
U8 = mybir.dt.uint8
I8 = mybir.dt.int8
AF = mybir.ActivationFunctionType
ALU = mybir.AluOpType
AX = mybir.AxisListType

B, S, D = 4, 2048, 2048
II = 8192
NH = 16
TPC = 1024
T = 512
NT = TPC // T
DC = D // 128
IC = II // 128
SCALE = 1.0 / float(np.sqrt(128.0))
RG = [[0, 1], [2, 3], [4, 5], [6, 7]]
TCH = 1024


class Sem:
    def __init__(self, es, nc, name):
        self.h = es.enter_context(nc.semaphore(name))
        self.n = 0

    def inc(self, ins, k=1):
        ins.then_inc(self.h, k)
        self.n += k
        return self.n


def build(n_layers=2, l0_out=False, stage="full"):
    nc = bass.Bass("TRN2", target_bir_lowering=False, debug=False,
                   detect_race_conditions=False)

    def din(name, shape, dt):
        return nc.dram_tensor(name, list(shape), dt, kind="ExternalInput").ap()

    def dint(name, shape, dt):
        return nc.dram_tensor(name, list(shape), dt, kind="Internal").ap()

    x0 = din("x0", [D, TPC], F32)
    w0qk = din("w0qk", [32, 128, D], F32)
    w0v = din("w0v", [D, D], F32)
    w0o = din("w0o", [16, 128, D], F32)
    w0g = din("w0g", [64, 128, D], U8)
    w0u = din("w0u", [64, 128, D], U8)
    w0d = din("w0d", [16, 128, II], U8)
    gs_col = din("gs_col", [128, 65], F32)
    us_col = din("us_col", [128, 65], F32)
    ds_col = din("ds_col", [128, 17], F32)
    w1qk = din("w1qk", [32, 128, D], F32)
    w1v = din("w1v", [D, D], F32)
    w1o = din("w1o", [16, 128, D], F32)
    w1g = din("w1g", [64, 128, D], F32)
    w1u = din("w1u", [64, 128, D], F32)
    w1d = din("w1d", [16, 128, II], F32)
    qkvb_col = din("qkvb_col", [128, 33], F32)
    bv_row = din("bv_row", [1, D], F32)
    ob_col = din("ob_col", [128, 17], F32)
    gam_col = din("gam_col", [128, 65], F32)
    mask_d = din("mask_d", [32, 128, T], BF16)
    ident_in = din("ident_in", [128, 128], F32)

    out = nc.dram_tensor("out", [D, TPC], F32, kind="ExternalOutput").ap()
    xres = dint("xres", [D, TPC], F32)
    cck_i = dint("cck_i", [D + 16, TPC], BF16)
    ccv_i = dint("ccv_i", [TPC, D], BF16)
    cck_o = dint("cck_o", [2 * (D + 16), TPC], BF16)
    ccv_o = dint("ccv_o", [2 * TPC, D], BF16)
    tg_d = dint("tg_d", [64, 128, D], I8)
    tu_d = dint("tu_d", [64, 128, D], I8)
    td_d = dint("td_d", [16, 128, II], I8)
    rowspill = dint("rowspill", [64, T], F32)
    dummy_d = dint("dummy_d", [1, 64], F32)

    es = ExitStack()
    with es:
        def sbt(name, shape, dt):
            return es.enter_context(nc.sbuf_tensor(name, list(shape), dt))

        xn = [sbt(f"xn{t}", [128, DC * T], BF16) for t in range(NT)]
        qg = sbt("qg", [128, DC * T], BF16)
        har = sbt("har", [128, IC * T], BF16)
        war = sbt("war", [128, 4 * D], BF16)
        wvb = sbt("wvb", [128, 2 * T], BF16)
        kbuf = sbt("kbuf", [128, 2 * TPC], BF16)
        vbuf = sbt("vbuf", [128, 2 * TPC], BF16)
        pbuf = sbt("pbuf", [128, 3 * T], BF16)
        qhb = sbt("qhb", [128, 2 * T], BF16)
        bct = sbt("bct", [128, 4 * T], F32)
        xck = sbt("xck", [128, 2 * T], F32)
        ftmp = sbt("ftmp", [128, 2 * T], F32)
        mrun = sbt("mrun", [128, 2 * T], F32)
        mcols = sbt("mcols", [128, 48], F32)
        bfs = sbt("bfs", [128, 3 * T], BF16)
        kvst = sbt("kvst", [128, 2 * T], BF16)
        terni = sbt("terni", [128, TCH], F32)
        ternm = sbt("ternm", [128, TCH], F32)
        terno = sbt("terno", [128, TCH], I8)
        acc = sbt("acc", [128, 256], F32)
        qi8 = sbt("qi8", [128, 2 * T], I8)
        mbuf = sbt("mbuf", [128, 2 * T], BF16)
        ident = sbt("ident", [128, 128], F32)
        gsc = sbt("gsc", [128, 65], F32)
        usc = sbt("usc", [128, 65], F32)
        dsc = sbt("dsc", [128, 17], F32)
        gam = sbt("gam", [128, 65], F32)
        qkvb = sbt("qkvb", [128, 33], F32)
        obc = sbt("obc", [128, 17], F32)
        bvr = sbt("bvr", [1, D], BF16)
        onesrow_f = sbt("onesrow_f", [1, 128], F32)
        onesrow_bf = sbt("onesrow_bf", [1, 128], BF16)
        m128row = sbt("m128row", [1, 128], BF16)
        onescol_f = sbt("onescol_f", [128, 1], F32)
        onescol_bf = sbt("onescol_bf", [128, 1], BF16)
        biascol = sbt("biascol", [128, 1], F32)
        sccol = sbt("sccol", [128, 1], F32)
        swsp = [sbt(f"swsp{i}", [128, 1], F32) for i in range(3)]
        rows = [sbt(f"row{i}", [1, T], F32) for i in range(7)]
        srow_bf = [sbt(f"rbf{i}", [1, T], BF16) for i in range(2)]
        ccm_sb = sbt("ccm_sb", [1, 64], BF16)
        knm = sbt("knm", [1, 32], F32)
        sc1 = sbt("sc1", [1, 8], F32)
        wmean = sbt("wmean", [1, 8], F32)
        wrecip = [sbt(f"wrecip{i}", [1, 1], F32) for i in range(3)]
        dummy_sb = sbt("dummy_sb", [1, 64], F32)

        sbuf_left = nc.sbuf_bytes_remaining

        def psm(name, shape):
            return es.enter_context(nc.psum_tensor(name, list(shape), F32))

        pw = [psm(f"pw{i}", [128, T]) for i in range(6)]
        pr0 = psm("pr0", [1, T])
        pr1 = psm("pr1", [1, T])

        s_cc = Sem(es, nc, "s_cc")
        s_t = Sem(es, nc, "s_t")
        s_v = Sem(es, nc, "s_v")
        s_s = Sem(es, nc, "s_s")
        s_mk = Sem(es, nc, "s_mk")
        sP = Sem(es, nc, "sP")
        sP2 = Sem(es, nc, "sP2")
        sX = [Sem(es, nc, f"sX{i}") for i in range(2)]
        sT = Sem(es, nc, "sT")
        sM = [Sem(es, nc, f"sM{i}") for i in range(2)]
        sKB = [Sem(es, nc, f"sKB{i}") for i in range(2)]
        sVB = [Sem(es, nc, f"sVB{i}") for i in range(2)]
        sCM = Sem(es, nc, "sCM")
        sU = Sem(es, nc, "sU")
        sKS = [Sem(es, nc, f"sKS{i}") for i in range(2)]
        sVS = [Sem(es, nc, f"sVS{i}") for i in range(2)]
        sUP = [Sem(es, nc, f"sUP{i}") for i in range(2)]
        sSP = Sem(es, nc, "sSP")
        sTS = Sem(es, nc, "sTS")
        sKN = Sem(es, nc, "sKN")
        sW = [Sem(es, nc, f"sW{i}") for i in range(4)]
        sWV = [Sem(es, nc, f"sWV{i}") for i in range(2)]

        blk = es.enter_context(nc.Block())

        def TEN(f): blk.tensor(f)
        def VEC(f): blk.vector(f)
        def SCA(f): blk.scalar(f)
        def SYN(f): blk.sync(f)
        def GPS(f): blk.gpsimd(f)

        def marker(deps):
            vals = [(d.h, d.n) for d in deps]

            def _(sy):
                for h, v in vals:
                    sy.wait_ge(h, v)
                s_mk.inc(sy.dma_start(out=dummy_sb[:], in_=dummy_d), 16)
            SYN(_)
            return s_mk.n

        def barrier(allow_safe=True):
            tv, vv, sv = s_t.n, s_v.n, s_s.n

            def _(tn):
                tn.wait_ge(s_v.h, vv)
                tn.wait_ge(s_s.h, sv)
            TEN(_)

            def _(v):
                v.wait_ge(s_t.h, tv)
                v.wait_ge(s_s.h, sv)
            VEC(_)

            def _(sc):
                sc.wait_ge(s_t.h, tv)
                sc.wait_ge(s_v.h, vv)
            SCA(_)
            if allow_safe and tern_pos[0] < len(tern_steps) \
                    and tern_steps[tern_pos[0]][0] == "safe":
                pump(4, safe=True)

        def _(sy):
            sP.inc(sy.dma_start(out=gsc[:], in_=gs_col), 16)
            sP.inc(sy.dma_start(out=usc[:], in_=us_col), 16)
            sP.inc(sy.dma_start(out=dsc[:], in_=ds_col), 16)
            sP.inc(sy.dma_start(out=gam[:], in_=gam_col), 16)
            sP.inc(sy.dma_start(out=qkvb[:], in_=qkvb_col), 16)
            sP.inc(sy.dma_start(out=obc[:], in_=ob_col), 16)
            sP.inc(sy.dma_start(out=ident[:], in_=ident_in), 16)
        SYN(_)
        setup_sy = sP.n

        def _(g):
            sP2.inc(g.dma_start(out=bvr[:], in_=bv_row), 16)
        GPS(_)
        setup_gp = sP2.n

        def _(v):
            s_v.inc(v.memset(kvst[:], 0.0))
        VEC(_)
        zv = s_v.n

        def _(sy):
            sy.wait_ge(s_v.h, zv)
            for rr in range(16):
                sP.inc(sy.dma_start(out=cck_i[D + rr:D + rr + 1, :],
                                    in_=kvst[0:1, :]), 16)
        SYN(_)

        def _(v):
            v.wait_ge(sP.h, setup_sy)
            v.memset(onesrow_f[:], 1.0)
            v.memset(onesrow_bf[:], 1.0)
            v.memset(m128row[:], -128.0)
            v.memset(onescol_f[:], 1.0)
            v.memset(sccol[:], SCALE)
            s_v.inc(v.memset(onescol_bf[:], 1.0))
        VEC(_)

        tern_steps = []
        tern_abs_cnt = {}
        tern_absred = {}
        tern_p2_cnt = {}
        tern_st_cnt = {}
        tern_marker = {}
        tern_state = {}

        def build_tern():
            if n_layers < 2:
                return
            TC2 = TCH // 2
            specs = [(0, w1g, tg_d), (1, w1u, tu_d), (2, w1d, td_d)]
            for wi, wsrc, wdst in specs:
                Mo, K = wsrc.shape[0], wsrc.shape[2]
                ncols = K // TC2
                nch = Mo * ncols
                NEL = float(Mo * 128 * K)

                def mean_chunk(ci, wi=wi, wsrc=wsrc, ncols=ncols, nch=nch):
                    mo, co = divmod(ci, ncols)
                    bq = ci % 4
                    buf = terni if bq < 2 else ternm
                    hf = bq % 2
                    bsl = buf[:, hf * TC2:(hf + 1) * TC2]
                    my = [0]

                    def _(sy):
                        if ci >= 4:
                            sy.wait_ge(s_v.h, tern_absred[(wi, ci - 4)])
                        elif wi >= 1:
                            sy.wait_ge(s_v.h, tern_p2_cnt[(wi - 1, 256 - 1)])
                        my[0] = sT.inc(sy.dma_start(
                            out=bsl,
                            in_=wsrc[mo, :, co * TC2:(co + 1) * TC2]), 16)
                    SYN(_)

                    def _(v):
                        v.wait_ge(sT.h, my[0])
                        tern_absred[(wi, ci)] = s_v.inc(v.reduce_sum(
                            out=acc[:, ci:ci + 1], in_=bsl,
                            axis=AX.X, apply_absolute_value=True))
                    VEC(_)
                for ci in range(nch):
                    tern_steps.append(("any", lambda ci=ci, f=mean_chunk: f(ci)))

                def finish(wi=wi, nch=nch, NEL=NEL):
                    need_v = tern_absred[(wi, nch - 1)]
                    vpre, tpre = s_v.n, s_t.n

                    def _(v):
                        v.wait_ge(s_v.h, need_v)
                        v.wait_ge(s_t.h, tpre)
                        c = s_v.inc(v.reduce_sum(out=mcols[:, 6:7], in_=acc[:, 0:nch],
                                                 axis=AX.X))
                        v.wait_ge(s_v.h, c)
                        s_v.inc(v.tensor_copy(out=mcols[:, 7:8], in_=mcols[:, 6:7]))
                    VEC(_)
                    v1 = s_v.n

                    def _(tn):
                        tn.wait_ge(s_v.h, v1)
                        s_t.inc(tn.matmul(pr0[0:1, 0:1], onescol_f[:], mcols[:, 7:8],
                                          start=True, stop=True))
                    TEN(_)
                    t1 = s_t.n

                    def _(v):
                        v.wait_ge(s_t.h, t1)
                        v.tensor_scalar(out=wmean[0:1, wi + 1:wi + 2],
                                        in0=pr0[0:1, 0:1],
                                        scalar1=1.0 / (NEL * 127.0),
                                        scalar2=1e-5 / 127.0,
                                        op0=ALU.mult, op1=ALU.max)
                        c = s_v.inc(v.tensor_scalar(out=sc1[0:1, 5:6],
                                                    in0=pr0[0:1, 0:1],
                                                    scalar1=1.0 / NEL, scalar2=1e-5,
                                                    op0=ALU.mult, op1=ALU.max))
                        v.wait_ge(s_v.h, c)
                        s_v.inc(v.reciprocal(out=wrecip[wi][:], in_=sc1[0:1, 5:6]))
                    VEC(_)
                    v2 = s_v.n

                    def _(tn):
                        tn.wait_ge(s_v.h, v2)
                        s_t.inc(tn.matmul(pw[5][:, 0:1], onesrow_f[:], wrecip[wi][:],
                                          start=True, stop=True))
                    TEN(_)
                    t2 = s_t.n

                    def _(v):
                        v.wait_ge(s_t.h, t2)
                        s_v.inc(v.tensor_copy(out=swsp[wi][:], in_=pw[5][:, 0:1]))
                    VEC(_)
                tern_steps.append(("safe", finish))

                def tern_chunk(ci, wi=wi, wsrc=wsrc, wdst=wdst, ncols=ncols, nch=nch):
                    mo, co = divmod(ci, ncols)
                    hf = ci % 2
                    sl = slice(hf * TC2, (hf + 1) * TC2)
                    my = [0]

                    def _(sy):
                        if ci >= 2:
                            sy.wait_ge(s_v.h, tern_p2_cnt[(wi, ci - 2)])
                        else:
                            sy.wait_ge(s_v.h, tern_absred[(wi, nch - 1)])
                        my[0] = sT.inc(sy.dma_start(
                            out=terni[:, sl],
                            in_=wsrc[mo, :, co * TC2:(co + 1) * TC2]), 16)
                    SYN(_)

                    def _(v):
                        v.wait_ge(sT.h, my[0])
                        if ci >= 2:
                            v.wait_ge(sTS.h, tern_st_cnt[(wi, ci - 2)])
                        v.tensor_scalar(out=ternm[:, sl],
                                        in0=terni[:, sl],
                                        scalar1=swsp[wi][:], scalar2=None, op0=ALU.mult)
                        tern_p2_cnt[(wi, ci)] = s_v.inc(v.tensor_scalar(
                            out=terno[:, sl], in0=ternm[:, sl],
                            scalar1=-1.4375, scalar2=1.4375, op0=ALU.max, op1=ALU.min))
                    VEC(_)
                    v1 = s_v.n

                    def _(sy):
                        sy.wait_ge(s_v.h, v1)
                        tern_st_cnt[(wi, ci)] = sTS.inc(sy.dma_start(
                            out=wdst[mo, :, co * TC2:(co + 1) * TC2],
                            in_=terno[:, sl]), 16)
                    SYN(_)
                for ci in range(nch):
                    tern_steps.append(("any", lambda ci=ci, f=tern_chunk: f(ci)))

                def endmark(wi=wi):
                    tern_marker[wi] = marker([sTS])
                tern_steps.append(("any", endmark))

        build_tern()
        tern_pos = [0]

        def pump(n, safe=False):
            k = 0
            while k < n and tern_pos[0] < len(tern_steps):
                tag, fn = tern_steps[tern_pos[0]]
                if tag == "safe" and not safe:
                    break
                fn()
                tern_pos[0] += 1
                k += 1

        def pump_flush():
            pump(len(tern_steps), safe=True)

        spill_idx = {}
        spill_mk = {}

        def spill(row_ap, key, v_cnt):
            idx = spill_idx.setdefault(key, len(spill_idx))
            assert idx < 64

            def _(sy):
                sy.wait_ge(s_v.h, v_cnt)
                sSP.inc(sy.dma_start(out=rowspill[idx:idx + 1, :], in_=row_ap), 16)
            SYN(_)
            spill_mk[key] = None

        def unspill(row_ap, key):
            idx = spill_idx[key]
            pre = sSP.n
            res = [0]

            def _(sy):
                sy.wait_ge(sSP.h, pre)
                res[0] = sU.inc(sy.dma_start(out=row_ap, in_=rowspill[idx:idx + 1, :]), 16)
            SYN(_)
            return res[0]

        war_free = [0, 0, 0, 0]
        warslot = [0]
        kst_gate = [0, 0]
        vst_gate = [0, 0]
        wv_free = [0, 0]
        qh_free = [0, 0]
        exp_free = [0, 0]
        pv_free = [0, 0, 0]
        mask_free = [0, 0]
        kv_free = [0, 0]
        upd_free = [0, 0]
        layer_e_marker = [0, 0]

        def load_war(src_ap):
            slot = warslot[0]
            warslot[0] = (slot + 1) % 4
            free = war_free[slot]
            cnt = [0]

            def _(g):
                if free:
                    g.wait_ge(s_t.h, free)
                cnt[0] = sW[slot].inc(g.dma_start(out=war[:, slot * D:(slot + 1) * D],
                                                  in_=src_ap), 16)
            GPS(_)
            return slot, cnt[0]

        def wsl(slot, dm):
            return war[:, slot * D + dm * 128: slot * D + (dm + 1) * 128]

        xsrcs = [x0, xres]
        xdsts = [xres, out]
        if n_layers == 1 or l0_out:
            xdsts[0] = out

        def emit_layer(l):
            L1 = (l == 1)
            xsrc, xdst = xsrcs[l], xdsts[l]
            wqk = w1qk if L1 else w0qk
            wvp = w1v if L1 else w0v
            wop = w1o if L1 else w0o
            wg_t = tg_d if L1 else w0g
            wu_t = tu_d if L1 else w0u
            wd_t = td_d if L1 else w0d

            if L1:
                barrier()
                xmark = layer_e_marker[0]

                def _(sy):
                    sy.wait_ge(s_mk.h, xmark)
                SYN(_)

            def _(v):
                s_v.inc(v.memset(knm[0:1, 0:32], 0.0))
            VEC(_)

            for t in range(NT):
                barrier()
                for f in range(DC):
                    slot = f % 2
                    my = [0]
                    tpre = s_t.n

                    def _(sy, f=f, slot=slot, my=my, tpre=tpre):
                        sy.wait_ge(s_t.h, tpre)
                        my[0] = sX[slot].inc(sy.dma_start(
                            out=xck[:, slot * T:(slot + 1) * T],
                            in_=xsrc[f * 128:(f + 1) * 128, t * T:(t + 1) * T]), 16)
                    SYN(_)

                    def _(sc, slot=slot, my=my):
                        sc.wait_ge(sX[slot].h, my[0])
                        s_s.inc(sc.activation(out=ftmp[:, slot * T:(slot + 1) * T],
                                              in_=xck[:, slot * T:(slot + 1) * T],
                                              func=AF.Square))
                    SCA(_)
                    mys = s_s.n

                    def _(tn, f=f, slot=slot, my=my, mys=mys):
                        tn.wait_ge(sX[slot].h, my[0])
                        tn.matmul(pr0[:], onescol_f[:], xck[:, slot * T:(slot + 1) * T],
                                  start=(f == 0), stop=(f == DC - 1),
                                  skip_group_check=True)
                        tn.wait_ge(s_s.h, mys)
                        s_t.inc(tn.matmul(pr1[:], onescol_f[:],
                                          ftmp[:, slot * T:(slot + 1) * T],
                                          start=(f == 0), stop=(f == DC - 1),
                                          skip_group_check=True))
                    TEN(_)
                stats_t = s_t.n

                def _(v, stats_t=stats_t):
                    v.wait_ge(s_t.h, stats_t)
                    v.tensor_scalar(out=rows[0][:], in0=pr0[:], scalar1=1.0 / D,
                                    scalar2=None, op0=ALU.mult)
                    v.tensor_mul(out=rows[2][:], in0=rows[0][:], in1=rows[0][:])
                    s_v.inc(v.scalar_tensor_tensor(out=rows[2][:], in0=pr1[:],
                                                   scalar=1.0 / D, in1=rows[2][:],
                                                   op0=ALU.mult, op1=ALU.subtract))
                VEC(_)
                v1 = s_v.n

                def _(v):
                    s_v.inc(v.tensor_scalar(out=rows[3][:], in0=rows[2][:],
                                            scalar1=1e-5, scalar2=None, op0=ALU.add))
                VEC(_)
                v1b = s_v.n

                def _(sc, v1b=v1b):
                    sc.wait_ge(s_v.h, v1b)
                    s_s.inc(sc.activation(out=rows[2][:], in_=rows[3][:], func=AF.Sqrt))
                SCA(_)
                s1 = s_s.n

                def _(v, s1=s1):
                    v.wait_ge(s_s.h, s1)
                    s_v.inc(v.reciprocal(out=rows[1][:], in_=rows[2][:]))
                VEC(_)
                vr = s_v.n

                def _(tn, vr=vr):
                    tn.wait_ge(s_v.h, vr)
                    tn.matmul(pw[4][:], onesrow_f[:], rows[0][:], start=True, stop=True)
                    s_t.inc(tn.matmul(pw[5][:], onesrow_f[:], rows[1][:],
                                      start=True, stop=True))
                TEN(_)
                t1 = s_t.n

                def _(v, t1=t1):
                    v.wait_ge(s_t.h, t1)
                    v.tensor_copy(out=bct[:, 0:T], in_=pw[4][:])
                    s_v.inc(v.tensor_copy(out=bct[:, T:2 * T], in_=pw[5][:]))
                VEC(_)
                bc_ready = s_v.n
                spill(rows[0][:], ("m", l, t), bc_ready)

                if L1:
                    def _(v, bc_ready=bc_ready):
                        v.wait_ge(s_v.h, bc_ready)
                        v.memset(mrun[:, 0:T], -1e30)
                        s_v.inc(v.memset(mrun[:, T:2 * T], 1e30))
                    VEC(_)
                for f in range(DC):
                    slot = f % 2
                    my = [0]
                    vpre = s_v.n

                    def _(sy, f=f, slot=slot, my=my, vpre=vpre):
                        sy.wait_ge(s_v.h, vpre)
                        my[0] = sX[slot].inc(sy.dma_start(
                            out=xck[:, slot * T:(slot + 1) * T],
                            in_=xsrc[f * 128:(f + 1) * 128, t * T:(t + 1) * T]), 16)
                    SYN(_)

                    def _(v, f=f, slot=slot, my=my, t=t, bc_ready=bc_ready):
                        v.wait_ge(sX[slot].h, my[0])
                        v.wait_ge(s_v.h, bc_ready)
                        v.tensor_sub(out=ftmp[:, slot * T:(slot + 1) * T],
                                     in0=xck[:, slot * T:(slot + 1) * T],
                                     in1=bct[:, 0:T])
                        ins = v.tensor_mul(out=xn[t][:, f * T:(f + 1) * T],
                                           in0=ftmp[:, slot * T:(slot + 1) * T],
                                           in1=bct[:, T:2 * T])
                        if L1:
                            v.tensor_tensor(out=mrun[:, 0:T],
                                            in0=ftmp[:, slot * T:(slot + 1) * T],
                                            in1=mrun[:, 0:T], op=ALU.max)
                            ins = v.tensor_tensor(out=mrun[:, T:2 * T],
                                                  in0=ftmp[:, slot * T:(slot + 1) * T],
                                                  in1=mrun[:, T:2 * T], op=ALU.min)
                        s_v.inc(ins)
                    VEC(_)
                xn_ready = s_v.n
                if stage == "LN" + str(t):
                    return

                if L1:
                    def _(v, xn_ready=xn_ready):
                        v.wait_ge(s_v.h, xn_ready)
                        v.tensor_scalar(out=mrun[:, T:2 * T], in0=mrun[:, T:2 * T],
                                        scalar1=-1.0, scalar2=None, op0=ALU.mult)
                        s_v.inc(v.tensor_tensor(out=mrun[:, 0:T], in0=mrun[:, 0:T],
                                                in1=mrun[:, T:2 * T], op=ALU.max))
                    VEC(_)
                    mv = s_v.n
                    for tb in range(4):
                        def _(tn, tb=tb, mv=mv):
                            tn.wait_ge(s_v.h, mv)
                            s_t.inc(tn.transpose(pw[tb][:, 0:128],
                                                 mrun[:, tb * 128:(tb + 1) * 128],
                                                 ident[:]))
                        TEN(_)
                    t2 = s_t.n
                    mcbase = mcol_ctr[0]
                    mcol_ctr[0] += 4

                    def _(v, t2=t2, mcbase=mcbase):
                        v.wait_ge(s_t.h, t2)
                        for tb in range(4):
                            ins = v.reduce_max(out=mcols[:, mcbase + tb:mcbase + tb + 1],
                                               in_=pw[tb][:, 0:128], axis=AX.X)
                        s_v.inc(ins)
                    VEC(_)
                    v2 = s_v.n
                    for tb in range(4):
                        def _(tn, tb=tb, v2=v2, mcbase=mcbase):
                            tn.wait_ge(s_v.h, v2)
                            s_t.inc(tn.matmul(pr0[0:1, tb * 128:(tb + 1) * 128],
                                              mcols[:, mcbase + tb:mcbase + tb + 1],
                                              ident[:], start=True, stop=True,
                                              skip_group_check=True))
                        TEN(_)
                    t3 = s_t.n

                    def _(v, t3=t3):
                        v.wait_ge(s_t.h, t3)
                        v.tensor_copy(out=rows[3][:], in_=pr0[:])
                        v.tensor_mul(out=rows[4][:], in0=rows[3][:], in1=rows[1][:])
                        v.tensor_scalar(out=rows[4][:], in0=rows[4][:], scalar1=1e-5,
                                        scalar2=None, op0=ALU.max)
                        v.reciprocal(out=rows[5][:], in_=rows[4][:])
                        s_v.inc(v.scalar_tensor_tensor(out=rows[5][:], in0=rows[5][:],
                                                       scalar=127.0, in1=rows[1][:],
                                                       op0=ALU.mult, op1=ALU.mult))
                    VEC(_)
                    v3 = s_v.n
                    spill(rows[4][:], ("mx", l, t), v3)
                    spill(rows[5][:], ("qscT", l, t), v3)
                else:
                    for f in range(DC):
                        def _(tn, f=f, t=t, xn_ready=xn_ready):
                            tn.wait_ge(s_v.h, xn_ready)
                            s_t.inc(tn.matmul(pr1[:], onescol_bf[:],
                                              xn[t][:, f * T:(f + 1) * T],
                                              start=(f == 0), stop=(f == DC - 1),
                                              skip_group_check=True))
                        TEN(_)
                    t2 = s_t.n

                    def _(v, t2=t2, t_=t):
                        v.wait_ge(s_t.h, t2)
                        s_v.inc(v.tensor_copy(out=srow_bf[t_][:], in_=pr1[:]))
                    VEC(_)

                for hmo in range(16):
                    slot, gcnt = load_war(wqk[16 + hmo])
                    vpre = s_v.n

                    def _(tn, slot=slot, gcnt=gcnt, t=t, xn_ready=xn_ready, vpre=vpre):
                        tn.wait_ge(sW[slot].h, gcnt)
                        tn.wait_ge(s_v.h, vpre)
                        for dm in range(DC):
                            ins = tn.matmul(pw[0][:], wsl(slot, dm),
                                            xn[t][:, dm * T:(dm + 1) * T],
                                            start=(dm == 0), stop=(dm == DC - 1),
                                            skip_group_check=True)
                        s_t.inc(ins)
                    TEN(_)
                    war_free[slot] = s_t.n
                    t1 = s_t.n
                    ks = hmo % 2

                    def _(v, hmo=hmo, t1=t1, ks=ks):
                        v.wait_ge(s_t.h, t1)
                        v.wait_ge(sKS[ks].h, kst_gate[ks])
                        v.wait_ge(sVS[ks].h, vst_gate[ks])
                        b = qkvb[:, 17 + hmo:18 + hmo] if L1 else 0.0
                        v.tensor_scalar(out=kvst[:, ks * T:(ks + 1) * T],
                                        in0=pw[0][:], scalar1=b, scalar2=None,
                                        op0=ALU.add)
                        s_v.inc(v.tensor_mul(out=bfs[:, 0:T],
                                             in0=kvst[:, ks * T:(ks + 1) * T],
                                             in1=kvst[:, ks * T:(ks + 1) * T]))
                    VEC(_)
                    s2 = s_v.n
                    v2 = s_v.n

                    def _(tn, v2=v2):
                        tn.wait_ge(s_v.h, v2)
                        s_t.inc(tn.matmul(pr1[:], onescol_bf[:], bfs[:, 0:T],
                                          start=True, stop=True, skip_group_check=True))
                    TEN(_)
                    t3 = s_t.n

                    def _(v, hmo=hmo, t3=t3):
                        v.wait_ge(s_t.h, t3)
                        c = s_v.inc(v.reduce_max(out=sc1[0:1, 4:5], in_=pr1[:],
                                                 axis=AX.X))
                        v.wait_ge(s_v.h, c)
                        s_v.inc(v.tensor_tensor(out=knm[0:1, hmo:hmo + 1],
                                                in0=knm[0:1, hmo:hmo + 1],
                                                in1=sc1[0:1, 4:5], op=ALU.max))
                    VEC(_)

                    def _(sy, hmo=hmo, s2=s2, ks=ks, t=t):
                        sy.wait_ge(s_v.h, s2)
                        kst_gate[ks] = sKS[ks].inc(sy.dma_start(
                            out=cck_i[hmo * 128:(hmo + 1) * 128, t * T:(t + 1) * T],
                            in_=kvst[:, ks * T:(ks + 1) * T]), 16)
                    SYN(_)
                    pump(6)
                if stage == "K" + str(t):
                    return

                prev_vevac = [s_v.n]
                for cg in range(4):
                    for dm in range(DC):
                        slot = dm % 2
                        my = [0]
                        free = wv_free[slot]

                        def _(g, dm=dm, cg=cg, slot=slot, my=my, free=free):
                            if free:
                                g.wait_ge(s_t.h, free)
                            my[0] = sWV[slot].inc(g.dma_start(
                                out=wvb[:, slot * T:(slot + 1) * T],
                                in_=wvp[dm * 128:(dm + 1) * 128, cg * T:(cg + 1) * T]), 16)
                        GPS(_)

                        def _(tn, dm=dm, slot=slot, my=my, t=t, xn_ready=xn_ready,
                              pe=prev_vevac[0]):
                            tn.wait_ge(sWV[slot].h, my[0])
                            if dm == 0:
                                tn.wait_ge(s_v.h, max(xn_ready, pe))
                            last = (dm == DC - 1) and (not L1)
                            for tb in range(4):
                                ins = tn.matmul(
                                    pw[tb][:],
                                    xn[t][:, dm * T + tb * 128: dm * T + (tb + 1) * 128],
                                    wvb[:, slot * T:(slot + 1) * T],
                                    start=(dm == 0), stop=last, skip_group_check=True)
                            s_t.inc(ins)
                        TEN(_)
                        wv_free[slot] = s_t.n
                    if L1:
                        def _(tn, cg=cg):
                            tn.wait_ge(sP2.h, setup_gp)
                            for tb in range(4):
                                ins = tn.matmul(pw[tb][:], onesrow_bf[:],
                                                bvr[0:1, cg * T:(cg + 1) * T],
                                                start=False, stop=True,
                                                skip_group_check=True)
                            s_t.inc(ins)
                        TEN(_)
                    t4 = s_t.n
                    for tb in range(4):
                        vs = tb % 2
                        if "x" in stage:
                            continue

                        def _(v, tb=tb, t4=t4, vs=vs):
                            v.wait_ge(s_t.h, t4)
                            v.wait_ge(sVS[vs].h, vst_gate[vs])
                            v.wait_ge(sKS[vs].h, kst_gate[vs])
                            s_v.inc(v.tensor_copy(out=kvst[:, vs * T:(vs + 1) * T],
                                                  in_=pw[tb][:]))
                        VEC(_)
                        v4 = s_v.n
                        if "y" in stage:
                            continue

                        def _(sy, tb=tb, cg=cg, v4=v4, vs=vs, t=t):
                            sy.wait_ge(s_v.h, v4)
                            vst_gate[vs] = sVS[vs].inc(sy.dma_start(
                                out=ccv_i[t * T + tb * 128: t * T + (tb + 1) * 128,
                                          cg * T:(cg + 1) * T],
                                in_=kvst[:, vs * T:(vs + 1) * T]), 16)
                        SYN(_)
                    prev_vevac[0] = s_v.n
                    pump(8)
                if stage.startswith("V") and stage.endswith(str(t)):
                    return

            vpre = s_v.n

            def _(g, vpre=vpre):
                g.wait_ge(s_v.h, vpre)
                sKN.inc(g.dma_start(out=cck_i[D:D + 1, 0:32], in_=knm[0:1, 0:32]), 16)
            GPS(_)
            ccmark = marker([sKS[0], sKS[1], sVS[0], sVS[1], sKN, sP])

            def _(g, ccmark=ccmark, l=l):
                g.wait_ge(s_mk.h, ccmark)
                if l == 1:
                    for sdep in (sKB[0], sKB[1], sVB[0], sVB[1]):
                        g.wait_ge(sdep.h, sdep.n)
                s_cc.inc(g.collective_compute("AllGather", ALU.bypass, ins=[cck_i],
                                              outs=[cck_o], replica_groups=RG), 1)
                s_cc.inc(g.collective_compute("AllGather", ALU.bypass, ins=[ccv_i],
                                              outs=[ccv_o], replica_groups=RG), 1)
                g.wait_ge(s_cc.h, s_cc.n)
            GPS(_)
            cc_done = s_cc.n
            if stage == "A":
                return

            for t in range(NT):
                barrier()
                if L1:
                    u1 = unspill(rows[0][:], ("m", l, t))
                    u2 = unspill(rows[5][:], ("qscT", l, t))
                    u3 = unspill(rows[4][:], ("mx", l, t))
                    vpre = s_v.n

                    def _(tn, u1=u1, u2=u2, u3=u3, vpre=vpre):
                        tn.wait_ge(sU.h, max(u1, u2, u3))
                        tn.wait_ge(s_v.h, vpre)
                        tn.matmul(pw[4][:], onesrow_f[:], rows[0][:], start=True,
                                  stop=True)
                        s_t.inc(tn.matmul(pw[5][:], onesrow_f[:], rows[5][:],
                                          start=True, stop=True))
                    TEN(_)
                    t1 = s_t.n

                    def _(v, t1=t1):
                        v.wait_ge(s_t.h, t1)
                        v.tensor_copy(out=bct[:, 0:T], in_=pw[4][:])
                        s_v.inc(v.tensor_copy(out=bct[:, 2 * T:3 * T], in_=pw[5][:]))
                    VEC(_)
                    bcq = s_v.n
                    for f in range(DC):
                        slot = f % 2
                        my = [0]
                        vpre = s_v.n

                        def _(sy, f=f, slot=slot, my=my, vpre=vpre):
                            sy.wait_ge(s_v.h, vpre)
                            my[0] = sX[slot].inc(sy.dma_start(
                                out=xck[:, slot * T:(slot + 1) * T],
                                in_=xsrc[f * 128:(f + 1) * 128, t * T:(t + 1) * T]), 16)
                        SYN(_)

                        def _(v, f=f, slot=slot, my=my, bcq=bcq):
                            v.wait_ge(sX[slot].h, my[0])
                            v.wait_ge(s_v.h, bcq)
                            v.tensor_sub(out=ftmp[:, slot * T:(slot + 1) * T],
                                         in0=xck[:, slot * T:(slot + 1) * T],
                                         in1=bct[:, 0:T])
                            v.tensor_mul(out=qi8[:, slot * T:(slot + 1) * T],
                                         in0=ftmp[:, slot * T:(slot + 1) * T],
                                         in1=bct[:, 2 * T:3 * T])
                            s_v.inc(v.tensor_copy(out=qg[:, f * T:(f + 1) * T],
                                                  in_=qi8[:, slot * T:(slot + 1) * T]))
                        VEC(_)
                    rhs = qg
                    def _(v):
                        v.tensor_scalar(out=rows[0][:], in0=rows[4][:],
                                        scalar1=wmean[0:1, 1:2], scalar2=None,
                                        op0=ALU.mult)
                        s_v.inc(v.tensor_scalar(out=rows[2][:], in0=rows[4][:],
                                                scalar1=wmean[0:1, 2:3], scalar2=None,
                                                op0=ALU.mult))
                    VEC(_)
                    v2 = s_v.n

                    def _(tn, v2=v2):
                        tn.wait_ge(s_v.h, v2)
                        tn.matmul(pw[4][:], onesrow_f[:], rows[0][:], start=True,
                                  stop=True)
                        s_t.inc(tn.matmul(pw[5][:], onesrow_f[:], rows[2][:],
                                          start=True, stop=True))
                    TEN(_)
                    t2 = s_t.n

                    def _(v, t2=t2):
                        v.wait_ge(s_t.h, t2)
                        v.tensor_copy(out=bct[:, 0:T], in_=pw[4][:])
                        v.tensor_copy(out=bct[:, T:2 * T], in_=pw[5][:])
                        v.memset(mrun[:, 0:T], -1e30)
                        s_v.inc(v.memset(mrun[:, T:2 * T], 1e30))
                    VEC(_)
                    cgu = s_v.n

                    pump_flush()

                    def _(g):
                        g.wait_ge(s_mk.h, max(tern_marker.get(0, 0),
                                              tern_marker.get(1, 0)))
                    GPS(_)
                else:
                    rhs = xn[t]
                    cgu = s_v.n

                gu_free = [0, 0, 0, 0]
                for sl in range(4):
                    gu_free[sl] = war_free[sl]
                for ic in range(IC):
                    gsl = ic % 2
                    usl = 2 + (ic % 2)
                    mg = [0]
                    mu = [0]

                    def _(g, ic=ic, gsl=gsl, usl=usl, mg=mg, mu=mu,
                          f1=gu_free[gsl], f2=gu_free[usl]):
                        if f1:
                            g.wait_ge(s_t.h, f1)
                        mg[0] = sW[gsl].inc(g.dma_start(out=war[:, gsl * D:(gsl + 1) * D],
                                                        in_=wg_t[ic]), 16)
                        if f2:
                            g.wait_ge(s_t.h, f2)
                        mu[0] = sW[usl].inc(g.dma_start(out=war[:, usl * D:(usl + 1) * D],
                                                        in_=wu_t[ic]), 16)
                    GPS(_)

                    pg = ic % 2
                    pu = ic % 2
                    gb = pw[0] if pg == 0 else pw[2]
                    ub = pw[1] if pu == 0 else pw[3]

                    def _(tn, ic=ic, gsl=gsl, usl=usl, mg=mg, mu=mu, gb=gb, ub=ub,
                          cgu=cgu, t=t, ef=evac_free):
                        tn.wait_ge(sW[gsl].h, mg[0])
                        if ic == 0:
                            tn.wait_ge(s_v.h, cgu)
                        if ic >= 2:
                            tn.wait_ge(s_v.h, ef[ic % 2])
                            tn.wait_ge(s_s.h, ef_s[ic % 2])
                        for dm in range(DC):
                            ins = tn.matmul(gb[:], wsl(gsl, dm), rhs[:, dm * T:(dm + 1) * T],
                                            start=(dm == 0),
                                            stop=(L1 and dm == DC - 1),
                                            skip_group_check=True)
                        if not L1:
                            ins = tn.matmul(gb[:], m128row[:], srow_bf[t][:],
                                            start=False, stop=True, skip_group_check=True)
                        s_t.inc(ins)
                        gu_free[gsl] = s_t.n
                        tn.wait_ge(sW[usl].h, mu[0])
                        for dm in range(DC):
                            ins = tn.matmul(ub[:], wsl(usl, dm), rhs[:, dm * T:(dm + 1) * T],
                                            start=(dm == 0),
                                            stop=(L1 and dm == DC - 1),
                                            skip_group_check=True)
                        if not L1:
                            ins = tn.matmul(ub[:], m128row[:], srow_bf[t][:],
                                            start=False, stop=True, skip_group_check=True)
                        s_t.inc(ins)
                        gu_free[usl] = s_t.n
                    TEN(_)
                    t1 = s_t.n

                    if not L1:
                        def _(sc, ic=ic, t1=t1, gb=gb):
                            sc.wait_ge(s_t.h, t1)
                            s_s.inc(sc.activation(out=bfs[:, 0:T], in_=gb[:],
                                                  func=AF.Silu, bias=0.0,
                                                  scale=gsc[:, ic + 1:ic + 2]))
                        SCA(_)
                        ef_s[ic % 2] = s_s.n
                        s2 = s_s.n

                        def _(v, ic=ic, t1=t1, s2=s2, ub=ub):
                            v.wait_ge(s_t.h, t1)
                            v.tensor_scalar(out=bfs[:, T:2 * T], in0=ub[:],
                                            scalar1=usc[:, ic + 1:ic + 2], scalar2=None,
                                            op0=ALU.mult)
                            v.wait_ge(s_s.h, s2)
                            s_v.inc(v.tensor_mul(out=har[:, ic * T:(ic + 1) * T],
                                                 in0=bfs[:, 0:T], in1=bfs[:, T:2 * T]))
                        VEC(_)
                        evac_free[ic % 2] = s_v.n
                        v2 = s_v.n

                        def _(tn, ic=ic, v2=v2):
                            tn.wait_ge(s_v.h, v2)
                            s_t.inc(tn.matmul(pr1[:], onescol_bf[:],
                                              har[:, ic * T:(ic + 1) * T],
                                              start=(ic == 0), stop=(ic == IC - 1),
                                              skip_group_check=True))
                        TEN(_)
                    else:
                        def _(v, ic=ic, t1=t1, gb=gb):
                            v.wait_ge(s_t.h, t1)
                            s_v.inc(v.tensor_mul(out=ftmp[:, 0:T], in0=gb[:],
                                                 in1=bct[:, 0:T]))
                        VEC(_)
                        v1 = s_v.n

                        def _(sc, v1=v1):
                            sc.wait_ge(s_v.h, v1)
                            s_s.inc(sc.activation(out=bfs[:, 0:T], in_=ftmp[:, 0:T],
                                                  func=AF.Silu))
                        SCA(_)
                        ef_s[ic % 2] = s_s.n
                        s2 = s_s.n

                        def _(v, ic=ic, s2=s2, ub=ub):
                            v.tensor_mul(out=ftmp[:, T:2 * T], in0=ub[:],
                                         in1=bct[:, T:2 * T])
                            v.wait_ge(s_s.h, s2)
                            v.tensor_mul(out=har[:, ic * T:(ic + 1) * T],
                                         in0=ftmp[:, T:2 * T], in1=bfs[:, 0:T])
                            v.tensor_mul(out=bfs[:, 2 * T:3 * T],
                                         in0=har[:, ic * T:(ic + 1) * T],
                                         in1=har[:, ic * T:(ic + 1) * T])
                            v.scalar_tensor_tensor(out=ftmp[:, 0:T],
                                                   in0=har[:, ic * T:(ic + 1) * T],
                                                   scalar=gam[:, ic + 1:ic + 2],
                                                   in1=mrun[:, 0:T],
                                                   op0=ALU.mult, op1=ALU.max)
                            v.tensor_copy(out=mrun[:, 0:T], in_=ftmp[:, 0:T])
                            v.scalar_tensor_tensor(out=ftmp[:, 0:T],
                                                   in0=har[:, ic * T:(ic + 1) * T],
                                                   scalar=gam[:, ic + 1:ic + 2],
                                                   in1=mrun[:, T:2 * T],
                                                   op0=ALU.mult, op1=ALU.min)
                            s_v.inc(v.tensor_copy(out=mrun[:, T:2 * T], in_=ftmp[:, 0:T]))
                        VEC(_)
                        evac_free[ic % 2] = s_v.n
                        v2 = s_v.n

                        def _(tn, ic=ic, v2=v2):
                            tn.wait_ge(s_v.h, v2)
                            s_t.inc(tn.matmul(pr1[:], onescol_bf[:], bfs[:, 2 * T:3 * T],
                                              start=(ic == 0), stop=(ic == IC - 1),
                                              skip_group_check=True))
                        TEN(_)
                    pump(2)
                hs_t = s_t.n
                hs_v = s_v.n

                if not L1:
                    def _(v, hs_t=hs_t, t_=t):
                        v.wait_ge(s_t.h, hs_t)
                        s_v.inc(v.tensor_copy(out=srow_bf[t_][:], in_=pr1[:]))
                    VEC(_)
                else:
                    def _(v, hs_t=hs_t):
                        v.wait_ge(s_t.h, hs_t)
                        s_v.inc(v.tensor_scalar(out=rows[3][:], in0=pr1[:],
                                                scalar1=1.0 / II, scalar2=1e-6,
                                                op0=ALU.mult, op1=ALU.add))
                    VEC(_)
                    rss_v = s_v.n

                    def _(sc, rss_v=rss_v):
                        sc.wait_ge(s_v.h, rss_v)
                        s_s.inc(sc.activation(out=rows[6][:], in_=rows[3][:],
                                              func=AF.Sqrt))
                    SCA(_)
                    rstd_ss = s_s.n

                    def _(v, rstd_ss=rstd_ss):
                        v.wait_ge(s_s.h, rstd_ss)
                        s_v.inc(v.reciprocal(out=rows[2][:], in_=rows[6][:]))
                    VEC(_)
                    rstd_s = s_s.n

                    def _(v, hs_v=hs_v):
                        v.wait_ge(s_v.h, hs_v)
                        v.tensor_scalar(out=mrun[:, T:2 * T], in0=mrun[:, T:2 * T],
                                        scalar1=-1.0, scalar2=None, op0=ALU.mult)
                        s_v.inc(v.tensor_tensor(out=mrun[:, 0:T], in0=mrun[:, 0:T],
                                                in1=mrun[:, T:2 * T], op=ALU.max))
                    VEC(_)
                    mv = s_v.n
                    for tb in range(4):
                        def _(tn, tb=tb, mv=mv):
                            tn.wait_ge(s_v.h, mv)
                            s_t.inc(tn.transpose(pw[tb][:, 0:128],
                                                 mrun[:, tb * 128:(tb + 1) * 128],
                                                 ident[:]))
                        TEN(_)
                    t2 = s_t.n
                    mcbase = mcol_ctr[0]
                    mcol_ctr[0] += 4

                    def _(v, t2=t2, mcbase=mcbase):
                        v.wait_ge(s_t.h, t2)
                        for tb in range(4):
                            ins = v.reduce_max(out=mcols[:, mcbase + tb:mcbase + tb + 1],
                                               in_=pw[tb][:, 0:128], axis=AX.X)
                        s_v.inc(ins)
                    VEC(_)
                    v2 = s_v.n
                    for tb in range(4):
                        def _(tn, tb=tb, v2=v2, mcbase=mcbase):
                            tn.wait_ge(s_v.h, v2)
                            s_t.inc(tn.matmul(pr0[0:1, tb * 128:(tb + 1) * 128],
                                              mcols[:, mcbase + tb:mcbase + tb + 1],
                                              ident[:], start=True, stop=True,
                                              skip_group_check=True))
                        TEN(_)
                    t3 = s_t.n

                    def _(v, t3=t3, rstd_s=rstd_s):
                        v.wait_ge(s_t.h, t3)
                        v.wait_ge(s_s.h, rstd_s)
                        v.tensor_copy(out=rows[3][:], in_=pr0[:])
                        v.tensor_mul(out=rows[4][:], in0=rows[3][:], in1=rows[2][:])
                        v.tensor_scalar(out=rows[4][:], in0=rows[4][:], scalar1=1e-5,
                                        scalar2=None, op0=ALU.max)
                        v.reciprocal(out=rows[5][:], in_=rows[4][:])
                        v.scalar_tensor_tensor(out=rows[5][:], in0=rows[5][:],
                                               scalar=127.0, in1=rows[2][:],
                                               op0=ALU.mult, op1=ALU.mult)
                        s_v.inc(v.tensor_scalar(out=rows[6][:], in0=rows[4][:],
                                                scalar1=wmean[0:1, 3:4], scalar2=None,
                                                op0=ALU.mult))
                    VEC(_)
                    v3 = s_v.n

                    def _(tn, v3=v3):
                        tn.wait_ge(s_v.h, v3)
                        tn.matmul(pw[4][:], onesrow_f[:], rows[5][:], start=True,
                                  stop=True)
                        s_t.inc(tn.matmul(pw[5][:], onesrow_f[:], rows[6][:],
                                          start=True, stop=True))
                    TEN(_)
                    t4 = s_t.n

                    def _(v, t4=t4):
                        v.wait_ge(s_t.h, t4)
                        v.tensor_copy(out=bct[:, 2 * T:3 * T], in_=pw[4][:])
                        s_v.inc(v.tensor_copy(out=bct[:, 3 * T:4 * T], in_=pw[5][:]))
                    VEC(_)
                    qsch_v = s_v.n

                    for ic2 in range(IC):
                        slot = ic2 % 2

                        def _(v, ic2=ic2, slot=slot, qsch_v=qsch_v):
                            if ic2 == 0:
                                v.wait_ge(s_v.h, qsch_v)
                            v.scalar_tensor_tensor(out=qi8[:, slot * T:(slot + 1) * T],
                                                   in0=har[:, ic2 * T:(ic2 + 1) * T],
                                                   scalar=gam[:, ic2 + 1:ic2 + 2],
                                                   in1=bct[:, 2 * T:3 * T],
                                                   op0=ALU.mult, op1=ALU.mult)
                            s_v.inc(v.tensor_copy(out=har[:, ic2 * T:(ic2 + 1) * T],
                                                  in_=qi8[:, slot * T:(slot + 1) * T]))
                        VEC(_)
                    def _(g):
                        g.wait_ge(s_mk.h, tern_marker.get(2, 0))
                    GPS(_)
                qdone_v = s_v.n

                dn_free = list(gu_free)
                for hf in range(DC):
                    pd = pw[4] if hf % 2 == 0 else pw[5]
                    for sub in range(4):
                        my = [0]
                        free = dn_free[sub]

                        def _(g, hf=hf, sub=sub, my=my, free=free):
                            if free:
                                g.wait_ge(s_t.h, free)
                            my[0] = sW[sub].inc(g.dma_start(
                                out=war[:, sub * D:(sub + 1) * D],
                                in_=wd_t[hf, :, sub * D:(sub + 1) * D]), 16)
                        GPS(_)

                        def _(tn, hf=hf, sub=sub, my=my, pd=pd, qdone_v=qdone_v, t=t,
                              dse=down_evac):
                            tn.wait_ge(sW[sub].h, my[0])
                            if sub == 0:
                                tn.wait_ge(s_v.h, qdone_v)
                                if hf >= 2 and dse[hf % 2] is not None:
                                    tn.wait_ge(dse[hf % 2][0].h, dse[hf % 2][1])
                                    tn.wait_ge(s_s.h, dse_s[hf % 2])
                            for k in range(DC):
                                ic2 = sub * DC + k
                                ins = tn.matmul(pd[:], wsl(sub, k),
                                                har[:, ic2 * T:(ic2 + 1) * T],
                                                start=(ic2 == 0),
                                                stop=(L1 and ic2 == IC - 1),
                                                skip_group_check=True)
                            if sub == 3 and not L1:
                                ins = tn.matmul(pd[:], m128row[:], srow_bf[t][:],
                                                start=False, stop=True,
                                                skip_group_check=True)
                            s_t.inc(ins)
                            dn_free[sub] = s_t.n
                        TEN(_)
                    t1 = s_t.n

                    if not L1:
                        def _(sc, hf=hf, t1=t1, pd=pd):
                            sc.wait_ge(s_t.h, t1)
                            s_s.inc(sc.activation(out=qg[:, hf * T:(hf + 1) * T],
                                                  in_=pd[:], func=AF.Silu, bias=0.0,
                                                  scale=dsc[:, hf + 1:hf + 2]))
                        SCA(_)
                        down_evac[hf % 2] = (s_s, s_s.n)
                        dse_s[hf % 2] = s_s.n
                    else:
                        def _(v, hf=hf, t1=t1, pd=pd):
                            v.wait_ge(s_t.h, t1)
                            s_v.inc(v.tensor_mul(out=ftmp[:, 0:T], in0=pd[:],
                                                 in1=bct[:, 3 * T:4 * T]))
                        VEC(_)
                        down_evac[hf % 2] = (s_v, s_v.n)
                        v1 = s_v.n

                        def _(sc, hf=hf, v1=v1):
                            sc.wait_ge(s_v.h, v1)
                            s_s.inc(sc.activation(out=qg[:, hf * T:(hf + 1) * T],
                                                  in_=ftmp[:, 0:T], func=AF.Silu))
                        SCA(_)
                        dse_s[hf % 2] = s_s.n
                    pump(4)
                for sl in range(4):
                    war_free[sl] = dn_free[sl]
                g_done = s_s.n
                if stage == "C":
                    return

                barrier()
                my = [0]

                def _(sy, cc_done=cc_done, my=my):
                    sy.wait_ge(s_cc.h, cc_done)
                    sCM.inc(sy.dma_start(out=ccm_sb[0:1, 0:16],
                                         in_=cck_o[D:D + 1, 0:16]), 16)
                    my[0] = sCM.inc(sy.dma_start(out=ccm_sb[0:1, 16:32],
                                                 in_=cck_o[2 * D + 16:2 * D + 17, 0:16]), 16)
                SYN(_)
                sy1 = sCM.n

                def _(v, sy1=sy1):
                    v.wait_ge(sCM.h, sy1)
                    s_v.inc(v.tensor_tensor(out=knm[0:1, 16:32], in0=ccm_sb[0:1, 0:16],
                                            in1=ccm_sb[0:1, 16:32], op=ALU.max))
                VEC(_)
                knall_v = s_v.n

                for h in range(NH):
                    vpre, spre = s_v.n, s_s.n

                    def _(tn, vpre=vpre, spre=spre):
                        tn.wait_ge(s_v.h, vpre)
                        tn.wait_ge(s_s.h, spre)
                    TEN(_)

                    slot, gcnt = load_war(wqk[h])
                    qs = h % 2

                    def _(tn, slot=slot, gcnt=gcnt, t=t):
                        tn.wait_ge(sW[slot].h, gcnt)
                        for dm in range(DC):
                            ins = tn.matmul(pw[3][:], wsl(slot, dm),
                                            xn[t][:, dm * T:(dm + 1) * T],
                                            start=(dm == 0), stop=(dm == DC - 1),
                                            skip_group_check=True)
                        s_t.inc(ins)
                    TEN(_)
                    war_free[slot] = s_t.n
                    t1 = s_t.n

                    def _(v, h=h, t1=t1, qs=qs, qf=qh_free[qs]):
                        v.wait_ge(s_t.h, max(t1, qf))
                        b = qkvb[:, h + 1:h + 2] if L1 else 0.0
                        v.tensor_scalar(out=qhb[:, qs * T:(qs + 1) * T],
                                        in0=pw[3][:], scalar1=b, scalar2=None,
                                        op0=ALU.add)
                        s_v.inc(v.tensor_mul(out=bfs[:, 0:T],
                                             in0=qhb[:, qs * T:(qs + 1) * T],
                                             in1=qhb[:, qs * T:(qs + 1) * T]))
                    VEC(_)
                    v1 = s_v.n

                    def _(tn, v1=v1):
                        tn.wait_ge(s_v.h, v1)
                        s_t.inc(tn.matmul(pr0[:], onescol_bf[:], bfs[:, 0:T],
                                          start=True, stop=True, skip_group_check=True))
                    TEN(_)
                    t2 = s_t.n

                    def _(v, h=h, t2=t2, knall_v=knall_v):
                        v.wait_ge(s_t.h, t2)
                        v.wait_ge(s_v.h, knall_v)
                        c = s_v.inc(v.reduce_max(out=sc1[0:1, 1:2], in_=pr0[:],
                                                 axis=AX.X))
                        v.wait_ge(s_v.h, c)
                        s_v.inc(v.tensor_tensor(out=sc1[0:1, 2:3], in0=sc1[0:1, 1:2],
                                                in1=knm[0:1, 16 + h:17 + h],
                                                op=ALU.mult))
                    VEC(_)
                    v2 = s_v.n

                    def _(sc, v2=v2):
                        sc.wait_ge(s_v.h, v2)
                        s_s.inc(sc.activation(out=sc1[0:1, 3:4], in_=sc1[0:1, 2:3],
                                              func=AF.Sqrt))
                    SCA(_)
                    s2 = s_s.n

                    def _(v, s2=s2):
                        v.wait_ge(s_s.h, s2)
                        s_v.inc(v.tensor_scalar(out=sc1[0:1, 4:5], in0=sc1[0:1, 3:4],
                                                scalar1=-SCALE, scalar2=None,
                                                op0=ALU.mult))
                    VEC(_)
                    v3 = s_v.n

                    def _(tn, v3=v3):
                        tn.wait_ge(s_v.h, v3)
                        s_t.inc(tn.matmul(pw[5][:, 0:1], onesrow_f[:], sc1[0:1, 4:5],
                                          start=True, stop=True))
                    TEN(_)
                    t3 = s_t.n

                    def _(v, t3=t3):
                        v.wait_ge(s_t.h, t3)
                        s_v.inc(v.tensor_copy(out=biascol[:], in_=pw[5][:, 0:1]))
                    VEC(_)
                    bias_v = s_v.n

                    first = [True]
                    for r in range(2):
                        myk = [0]
                        myv_ = [0]

                        def _(sy, h=h, r=r, myk=myk, myv_=myv_, cc_done=cc_done,
                              kf=kv_free[r]):
                            sy.wait_ge(s_cc.h, cc_done)
                            if kf:
                                sy.wait_ge(s_t.h, kf)
                            myk[0] = sKB[r].inc(sy.dma_start(
                                out=kbuf[:, r * TPC:(r + 1) * TPC],
                                in_=cck_o[r * (D + 16) + h * 128:
                                          r * (D + 16) + (h + 1) * 128, :]), 16)
                            inap = ccv_o[r * TPC:(r + 1) * TPC,
                                         h * 128:(h + 1) * 128].rearrange(
                                "(b p) d -> p b d", p=128)
                            outap = vbuf[:, r * TPC:(r + 1) * TPC].rearrange(
                                "p (b d) -> p b d", d=128)
                            myv_[0] = sVB[r].inc(sy.dma_start(out=outap, in_=inap), 16)
                        SYN(_)

                        for kb in range(8):
                            gkb = r * 8 + kb
                            psl = gkb % 2
                            pslot = gkb % 3
                            sb = pw[0] if psl == 0 else pw[1]
                            mym = [0]
                            if L1:
                                def _(sy, t=t, gkb=gkb, mym=mym, mf=mask_free[gkb % 2]):
                                    if mf:
                                        sy.wait_ge(s_v.h, mf)
                                    mym[0] = sM[gkb % 2].inc(sy.dma_start(
                                        out=mbuf[:, (gkb % 2) * T:(gkb % 2 + 1) * T],
                                        in_=mask_d[t * 16 + gkb]), 16)
                                SYN(_)

                            def _(tn, kb=kb, r=r, sb=sb, myk=myk, qs=qs,
                                  ef=exp_free[psl]):
                                tn.wait_ge(sKB[r].h, myk[0])
                                if ef:
                                    tn.wait_ge(s_s.h, ef)
                                s_t.inc(tn.matmul(
                                    sb[:],
                                    kbuf[:, r * TPC + kb * 128: r * TPC + (kb + 1) * 128],
                                    qhb[:, qs * T:(qs + 1) * T],
                                    start=True, stop=True, skip_group_check=True))
                            TEN(_)
                            t4 = s_t.n

                            def _(sc, sb=sb, psl=psl, pslot=pslot, t4=t4,
                                  bias_v=bias_v, pf=pv_free[pslot]):
                                sc.wait_ge(s_t.h, t4)
                                sc.wait_ge(s_v.h, bias_v)
                                if pf:
                                    sc.wait_ge(s_t.h, pf)
                                ins = sc.activation(out=pbuf[:, pslot * T:(pslot + 1) * T],
                                                    in_=sb[:], func=AF.Exp,
                                                    bias=biascol[:], scale=sccol[:])
                                exp_free[psl] = s_s.inc(ins)
                            SCA(_)
                            s3 = s_s.n

                            if L1:
                                def _(v, pslot=pslot, gkb=gkb, s3=s3, mym=mym):
                                    v.wait_ge(s_s.h, s3)
                                    v.wait_ge(sM[gkb % 2].h, mym[0])
                                    ins = v.tensor_tensor(
                                        out=pbuf[:, pslot * T:(pslot + 1) * T],
                                        in0=pbuf[:, pslot * T:(pslot + 1) * T],
                                        in1=mbuf[:, (gkb % 2) * T:(gkb % 2 + 1) * T],
                                        op=ALU.mult)
                                    mask_free[gkb % 2] = s_v.inc(ins)
                                VEC(_)
                                pvw = ("v", s_v.n)
                            else:
                                pvw = ("s", s3)

                            def _(tn, kb=kb, r=r, pslot=pslot, myv_=myv_, pvw=pvw,
                                  first=first, gkb=gkb):
                                if pvw[0] == "v":
                                    tn.wait_ge(s_v.h, pvw[1])
                                else:
                                    tn.wait_ge(s_s.h, pvw[1])
                                tn.wait_ge(sVB[r].h, myv_[0])
                                tn.matmul(pr1[:], onescol_bf[:],
                                          pbuf[:, pslot * T:(pslot + 1) * T],
                                          start=first[0], stop=(gkb == 15),
                                          skip_group_check=True)
                                ins = tn.matmul(
                                    pw[4][:],
                                    vbuf[:, r * TPC + kb * 128: r * TPC + (kb + 1) * 128],
                                    pbuf[:, pslot * T:(pslot + 1) * T],
                                    start=first[0], stop=(gkb == 15),
                                    skip_group_check=True)
                                pv_free[pslot] = s_t.inc(ins)
                                first[0] = False
                            TEN(_)
                        kv_free[r] = s_t.n
                    qh_free[qs] = s_t.n
                    pvdone = s_t.n

                    def _(v, pvdone=pvdone):
                        v.wait_ge(s_t.h, pvdone)
                        s_v.inc(v.reciprocal(out=rows[2][:], in_=pr1[:]))
                    VEC(_)
                    v4 = s_v.n

                    def _(tn, v4=v4):
                        tn.wait_ge(s_v.h, v4)
                        s_t.inc(tn.matmul(pw[5][:], onesrow_f[:], rows[2][:],
                                          start=True, stop=True))
                    TEN(_)
                    t5 = s_t.n

                    def _(v, h=h, t5=t5):
                        v.wait_ge(s_t.h, t5)
                        v.tensor_copy(out=bct[:, 3 * T:4 * T], in_=pw[5][:])
                        s_v.inc(v.tensor_mul(out=har[:, (16 + h) * T:(17 + h) * T],
                                             in0=pw[4][:], in1=bct[:, 3 * T:4 * T]))
                    VEC(_)
                    pump(2)
                ao_done = s_v.n

                for mo in range(DC):
                    slot, gcnt = load_war(wop[mo])

                    def _(tn, slot=slot, gcnt=gcnt, ao_done=ao_done, mo=mo,
                          spre=s_v.n):
                        tn.wait_ge(sW[slot].h, gcnt)
                        if mo == 0:
                            tn.wait_ge(s_v.h, ao_done)
                        else:
                            tn.wait_ge(s_v.h, spre)
                        for dm in range(DC):
                            ins = tn.matmul(pw[3][:], wsl(slot, dm),
                                            har[:, (16 + dm) * T:(17 + dm) * T],
                                            start=(dm == 0), stop=(dm == DC - 1),
                                            skip_group_check=True)
                        s_t.inc(ins)
                    TEN(_)
                    war_free[slot] = s_t.n
                    t1 = s_t.n

                    def _(v, mo=mo, t1=t1):
                        v.wait_ge(s_t.h, t1)
                        b = obc[:, mo + 1:mo + 2] if L1 else 0.0
                        s_v.inc(v.tensor_scalar(out=har[:, (32 + mo) * T:(33 + mo) * T],
                                                in0=pw[3][:], scalar1=b, scalar2=None,
                                                op0=ALU.add))
                    VEC(_)
                    pump(2)
                s_done = s_v.n
                if stage == "D":
                    return

                for f in range(DC):
                    slot = f % 2
                    my = [0]

                    def _(sy, f=f, slot=slot, my=my, uf=upd_free[slot]):
                        if uf:
                            sy.wait_ge(sUP[slot].h, uf)
                        my[0] = sX[slot].inc(sy.dma_start(
                            out=xck[:, slot * T:(slot + 1) * T],
                            in_=xsrc[f * 128:(f + 1) * 128, t * T:(t + 1) * T]), 16)
                    SYN(_)

                    def _(v, f=f, slot=slot, my=my, s_done=s_done, g_done=g_done):
                        v.wait_ge(s_s.h, g_done)
                        v.wait_ge(s_v.h, s_done)
                        v.wait_ge(sX[slot].h, my[0])
                        if stage == "dumps" and l == n_layers - 1:
                            s_v.inc(v.tensor_copy(out=xck[:, slot * T:(slot + 1) * T],
                                                  in_=har[:, (32 + f) * T:(33 + f) * T]))
                        elif stage == "dumpg" and l == n_layers - 1:
                            s_v.inc(v.tensor_copy(out=xck[:, slot * T:(slot + 1) * T],
                                                  in_=qg[:, f * T:(f + 1) * T]))
                        else:
                            v.tensor_mul(out=ftmp[:, slot * T:(slot + 1) * T],
                                         in0=har[:, (32 + f) * T:(33 + f) * T],
                                         in1=qg[:, f * T:(f + 1) * T])
                            s_v.inc(v.tensor_add(out=xck[:, slot * T:(slot + 1) * T],
                                                 in0=xck[:, slot * T:(slot + 1) * T],
                                                 in1=ftmp[:, slot * T:(slot + 1) * T]))
                    VEC(_)
                    v1 = s_v.n

                    def _(sy, f=f, slot=slot, v1=v1, t=t):
                        sy.wait_ge(s_v.h, v1)
                        upd_free[slot] = sUP[slot].inc(sy.dma_start(
                            out=xdst[f * 128:(f + 1) * 128, t * T:(t + 1) * T],
                            in_=xck[:, slot * T:(slot + 1) * T]), 16)
                    SYN(_)
                    pump(10)
            layer_e_marker[l] = marker([sUP[0], sUP[1]])

        mcol_ctr = [8]
        evac_free = [0, 0]
        ef_s = [0, 0]
        down_evac = [None, None]
        dse_s = [0, 0]

        for l in range(n_layers):
            emit_layer(l)
        pump_flush()

        if stage == "dumpt":
            pre = (s_mk.n, sUP[0].n, sUP[1].n)

            def _(g, pre=pre):
                g.wait_ge(s_mk.h, pre[0])
                g.wait_ge(sUP[0].h, pre[1])
                g.wait_ge(sUP[1].h, pre[2])
                for mo in range(16):
                    sKN.inc(g.dma_start(out=out[mo * 128:(mo + 1) * 128, :],
                                        in_=tg_d[mo, :, 0:TPC]), 16)
                for i in range(3):
                    sKN.inc(g.dma_start(out=out[0:128, 1020 + i:1021 + i],
                                        in_=swsp[i][:],
                                        allow_slow_non_contiguous=True), 16)
                sKN.inc(g.dma_start(out=out[128:129, 1008:1016],
                                    in_=wmean[0:1, :]), 16)
                sKN.inc(g.dma_start(out=out[129:130, 1008:1016],
                                    in_=sc1[0:1, :]), 16)
                sKN.inc(g.dma_start(out=out[130:258, 1016:1017],
                                    in_=acc[:, 0:1],
                                    allow_slow_non_contiguous=True), 16)
            GPS(_)
            marker([sKN])

        marker([sKS[0], sKS[1], sVS[0], sVS[1], sUP[0], sUP[1], sSP, sTS, sKN])
        mk_fin = s_mk.n

        def _(sy):
            sy.wait_ge(s_mk.h, mk_fin)
        SYN(_)

    return nc, sbuf_left


def _tile_lhsT(w):
    K, M = w.shape
    Ko, Mo = K // 128, M // 128
    return np.ascontiguousarray(
        w.reshape(Ko, 128, Mo, 128).transpose(2, 1, 0, 3).reshape(Mo, 128, K))


def _u8(wi32):
    return np.ascontiguousarray(wi32.view(np.uint8)[..., 0::4])


def _padcol(a):
    return np.ascontiguousarray(
        np.concatenate([np.zeros((a.shape[0], 1), a.dtype), a], axis=1))


def make_inputs(kw):
    f32 = np.float32
    x = np.asarray(kw["L0_attn_x"], f32)
    w0qkv = np.asarray(kw["L0_attn_w_qkv"], f32)
    w0o = np.asarray(kw["L0_attn_w_out"], f32)
    g0 = _u8(np.ascontiguousarray(np.asarray(kw["L0_ffn_gate_w"], np.int32)))
    u0 = _u8(np.ascontiguousarray(np.asarray(kw["L0_ffn_up_w"], np.int32)))
    d0 = _u8(np.ascontiguousarray(np.asarray(kw["L0_ffn_down_w"], np.int32)))
    gs = np.asarray(kw["L0_ffn_gate_scale"], f32)
    us = np.asarray(kw["L0_ffn_up_scale"], f32)
    ds = np.asarray(kw["L0_ffn_down_scale"], f32)
    w1qkv = np.asarray(kw["L1_attn_attn_qkvw"], f32)
    qkvb = np.asarray(kw["L1_attn_attn_qkvb"], f32)
    w1o = np.asarray(kw["L1_attn_attn_ow"], f32)
    ob = np.asarray(kw["L1_attn_attn_ob"], f32)
    wg1 = np.asarray(kw["L1_ffn_w_gate"], f32)
    wu1 = np.asarray(kw["L1_ffn_w_up"], f32)
    wd1 = np.asarray(kw["L1_ffn_w_down"], f32)
    gam = np.asarray(kw["L1_ffn_ln_weight"], f32)

    shared = {
        "w0qk": _tile_lhsT(w0qkv[:, 0:4096]),
        "w0v": np.ascontiguousarray(w0qkv[:, 4096:6144]),
        "w0o": _tile_lhsT(w0o),
        "w0g": _tile_lhsT(np.ascontiguousarray(g0.T)),
        "w0u": _tile_lhsT(np.ascontiguousarray(u0.T)),
        "w0d": _tile_lhsT(np.ascontiguousarray(d0.T)),
        "gs_col": _padcol(np.ascontiguousarray(gs.reshape(64, 128).T)),
        "us_col": _padcol(np.ascontiguousarray(us.reshape(64, 128).T)),
        "ds_col": _padcol(np.ascontiguousarray(ds.reshape(16, 128).T)),
        "w1qk": _tile_lhsT(w1qkv[:, 0:4096]),
        "w1v": np.ascontiguousarray(w1qkv[:, 4096:6144]),
        "w1o": _tile_lhsT(w1o),
        "w1g": _tile_lhsT(np.ascontiguousarray(wg1.T)),
        "w1u": _tile_lhsT(np.ascontiguousarray(wu1.T)),
        "w1d": _tile_lhsT(np.ascontiguousarray(wd1.T)),
        "qkvb_col": _padcol(np.ascontiguousarray(qkvb[0:4096].reshape(32, 128).T)),
        "bv_row": np.ascontiguousarray(qkvb[4096:6144].reshape(1, 2048)),
        "ob_col": _padcol(np.ascontiguousarray(ob.reshape(16, 128).T)),
        "gam_col": _padcol(np.ascontiguousarray(gam.reshape(64, 128).T)),
        "ident_in": np.eye(128, dtype=f32),
        "dummy_d": np.zeros((1, 64), f32),
    }
    in_maps = []
    for c in range(8):
        b, half = c // 2, c % 2
        qbase = half * TPC
        m = dict(shared)
        m["x0"] = np.ascontiguousarray(x[b, qbase:qbase + TPC, :].T)
        msk = np.zeros((32, 128, T), dtype=ml_dtypes.bfloat16)
        for t in range(NT):
            for gkb in range(16):
                kpos = gkb * 128 + np.arange(128)[:, None]
                qpos = qbase + t * T + np.arange(T)[None, :]
                msk[t * 16 + gkb] = (kpos <= qpos).astype(ml_dtypes.bfloat16)
        m["mask_d"] = msk
        in_maps.append(m)
    return in_maps


_CACHE = {}


def kernel(**kw):
    if "nc" not in _CACHE:
        _CACHE["nc"] = build()[0]
    nc = _CACHE["nc"]
    in_maps = make_inputs(kw)
    res = run_bass_kernel_spmd(nc, in_maps, core_ids=list(range(8)))
    outp = np.zeros((B, S, D), np.float32)
    for c in range(8):
        b, half = c // 2, c % 2
        outp[b, half * TPC:(half + 1) * TPC, :] = res.results[c]["out"].T
    return outp


# version 05
# speedup vs baseline: 1.1209x; 1.0640x over previous
import numpy as np
import ml_dtypes
from contextlib import ExitStack

import concourse.bass as bass
import concourse.mybir as mybir
from concourse.bass_utils import run_bass_kernel_spmd

F32 = mybir.dt.float32
BF16 = mybir.dt.bfloat16
U8 = mybir.dt.uint8
I8 = mybir.dt.int8
AF = mybir.ActivationFunctionType
ALU = mybir.AluOpType
AX = mybir.AxisListType

B, S, D = 4, 2048, 2048
II = 8192
NH = 16
TPC = 1024
T = 512
NT = TPC // T
DC = D // 128
IC = II // 128
SCALE = 1.0 / float(np.sqrt(128.0))
RG = [[0, 1], [2, 3], [4, 5], [6, 7]]
TCH = 1024


class Sem:
    def __init__(self, es, nc, name):
        self.h = es.enter_context(nc.semaphore(name))
        self.n = 0

    def inc(self, ins, k=1):
        ins.then_inc(self.h, k)
        self.n += k
        return self.n


def build(n_layers=2, l0_out=False, stage="full"):
    nc = bass.Bass("TRN2", target_bir_lowering=False, debug=False,
                   detect_race_conditions=False)

    def din(name, shape, dt):
        return nc.dram_tensor(name, list(shape), dt, kind="ExternalInput").ap()

    def dint(name, shape, dt):
        return nc.dram_tensor(name, list(shape), dt, kind="Internal").ap()

    x0 = din("x0", [D, TPC], F32)
    w0qk = din("w0qk", [32, 128, D], F32)
    w0v = din("w0v", [D, D], F32)
    w0o = din("w0o", [16, 128, D], F32)
    w0g = din("w0g", [64, 128, D], U8)
    w0u = din("w0u", [64, 128, D], U8)
    w0d = din("w0d", [16, 128, II], U8)
    gs_col = din("gs_col", [128, 65], F32)
    us_col = din("us_col", [128, 65], F32)
    ds_col = din("ds_col", [128, 17], F32)
    w1qk = din("w1qk", [32, 128, D], F32)
    w1v = din("w1v", [D, D], F32)
    w1o = din("w1o", [16, 128, D], F32)
    w1g = din("w1g", [64, 128, D], F32)
    w1u = din("w1u", [64, 128, D], F32)
    w1d = din("w1d", [16, 128, II], F32)
    qkvb_col = din("qkvb_col", [128, 33], F32)
    bv_row = din("bv_row", [1, D], F32)
    ob_col = din("ob_col", [128, 17], F32)
    gam_col = din("gam_col", [128, 65], F32)
    mask_d = din("mask_d", [32, 128, T], BF16)
    ident_in = din("ident_in", [128, 128], F32)

    out = nc.dram_tensor("out", [D, TPC], F32, kind="ExternalOutput").ap()
    xres = dint("xres", [D, TPC], F32)
    cck_i = dint("cck_i", [D + 16, TPC], BF16)
    ccv_i = dint("ccv_i", [TPC, D], BF16)
    cck_o = dint("cck_o", [2 * (D + 16), TPC], BF16)
    ccv_o = dint("ccv_o", [2 * TPC, D], BF16)
    tg_d = dint("tg_d", [64, 128, D], I8)
    tu_d = dint("tu_d", [64, 128, D], I8)
    td_d = dint("td_d", [16, 128, II], I8)
    rowspill = dint("rowspill", [64, T], F32)
    dummy_d = dint("dummy_d", [1, 64], F32)

    es = ExitStack()
    with es:
        def sbt(name, shape, dt):
            return es.enter_context(nc.sbuf_tensor(name, list(shape), dt))

        xn = [sbt(f"xn{t}", [128, DC * T], BF16) for t in range(NT)]
        qg = sbt("qg", [128, DC * T], BF16)
        har = sbt("har", [128, IC * T], BF16)
        war = sbt("war", [128, 4 * D], BF16)
        wvb = sbt("wvb", [128, 2 * T], BF16)
        kbuf = sbt("kbuf", [128, 2 * TPC], BF16)
        vbuf = sbt("vbuf", [128, 2 * TPC], BF16)
        pbuf = sbt("pbuf", [128, 4 * T], BF16)
        qhb = sbt("qhb", [128, 2 * T], BF16)
        bct = sbt("bct", [128, 4 * T], F32)
        xck = sbt("xck", [128, 2 * T], F32)
        ftmp = sbt("ftmp", [128, 2 * T], F32)
        mrun = sbt("mrun", [128, 2 * T], F32)
        mcols = sbt("mcols", [128, 48], F32)
        bfs = sbt("bfs", [128, 3 * T], BF16)
        kvst = sbt("kvst", [128, 2 * T], BF16)
        terni = sbt("terni", [128, TCH], F32)
        ternm = sbt("ternm", [128, TCH], F32)
        terno = sbt("terno", [128, TCH], I8)
        acc = sbt("acc", [128, 256], F32)
        qi8 = sbt("qi8", [128, 2 * T], I8)
        mbuf = sbt("mbuf", [128, 2 * T], BF16)
        ident = sbt("ident", [128, 128], F32)
        gsc = sbt("gsc", [128, 65], F32)
        usc = sbt("usc", [128, 65], F32)
        dsc = sbt("dsc", [128, 17], F32)
        gam = sbt("gam", [128, 65], F32)
        qkvb = sbt("qkvb", [128, 33], F32)
        obc = sbt("obc", [128, 17], F32)
        bvr = sbt("bvr", [1, D], BF16)
        onesrow_f = sbt("onesrow_f", [1, 128], F32)
        onesrow_bf = sbt("onesrow_bf", [1, 128], BF16)
        m128row = sbt("m128row", [1, 128], BF16)
        onescol_f = sbt("onescol_f", [128, 1], F32)
        onescol_bf = sbt("onescol_bf", [128, 1], BF16)
        biascol = sbt("biascol", [128, 1], F32)
        sccol = sbt("sccol", [128, 1], F32)
        swsp = [sbt(f"swsp{i}", [128, 1], F32) for i in range(3)]
        rows = [sbt(f"row{i}", [1, T], F32) for i in range(7)]
        srow_bf = [sbt(f"rbf{i}", [1, T], BF16) for i in range(2)]
        ccm_sb = sbt("ccm_sb", [1, 64], BF16)
        knm = sbt("knm", [1, 32], F32)
        sc1 = sbt("sc1", [1, 8], F32)
        wmean = sbt("wmean", [1, 8], F32)
        wrecip = [sbt(f"wrecip{i}", [1, 1], F32) for i in range(3)]
        dummy_sb = sbt("dummy_sb", [1, 64], F32)

        sbuf_left = nc.sbuf_bytes_remaining

        def psm(name, shape):
            return es.enter_context(nc.psum_tensor(name, list(shape), F32))

        class _PV:
            def __init__(self, t, c0):
                self.t, self.c0 = t, c0

            def __getitem__(self, ix):
                if isinstance(ix, tuple):
                    rs, cs = ix
                    a = cs.start or 0
                    b = T if cs.stop is None else cs.stop
                    return self.t[rs, self.c0 + a:self.c0 + b]
                return self.t[:, self.c0:self.c0 + T]

        pws = psm("pws", [128, 2 * T])
        pw = [_PV(pws, 0), _PV(pws, T)] + [psm(f"pw{i}", [128, T])
                                           for i in range(2, 6)]
        pr0 = psm("pr0", [1, T])
        pr1 = psm("pr1", [1, T])

        s_cc = Sem(es, nc, "s_cc")
        s_t = Sem(es, nc, "s_t")
        s_v = Sem(es, nc, "s_v")
        s_s = Sem(es, nc, "s_s")
        s_mk = Sem(es, nc, "s_mk")
        sP = Sem(es, nc, "sP")
        sP2 = Sem(es, nc, "sP2")
        sX = [Sem(es, nc, f"sX{i}") for i in range(2)]
        sT = Sem(es, nc, "sT")
        sM = [Sem(es, nc, f"sM{i}") for i in range(2)]
        sKB = [Sem(es, nc, f"sKB{i}") for i in range(2)]
        sVB = [Sem(es, nc, f"sVB{i}") for i in range(2)]
        sCM = Sem(es, nc, "sCM")
        sU = Sem(es, nc, "sU")
        sKS = [Sem(es, nc, f"sKS{i}") for i in range(2)]
        sVS = [Sem(es, nc, f"sVS{i}") for i in range(2)]
        sUP = [Sem(es, nc, f"sUP{i}") for i in range(2)]
        sSP = Sem(es, nc, "sSP")
        sTS = Sem(es, nc, "sTS")
        sKN = Sem(es, nc, "sKN")
        sW = [Sem(es, nc, f"sW{i}") for i in range(4)]
        sWV = [Sem(es, nc, f"sWV{i}") for i in range(2)]

        blk = es.enter_context(nc.Block())

        def TEN(f): blk.tensor(f)
        def VEC(f): blk.vector(f)
        def SCA(f): blk.scalar(f)
        def SYN(f): blk.sync(f)
        def GPS(f): blk.gpsimd(f)

        def marker(deps):
            vals = [(d.h, d.n) for d in deps]

            def _(sy):
                for h, v in vals:
                    sy.wait_ge(h, v)
                s_mk.inc(sy.dma_start(out=dummy_sb[:], in_=dummy_d), 16)
            SYN(_)
            return s_mk.n

        def barrier(allow_safe=True):
            tv, vv, sv = s_t.n, s_v.n, s_s.n

            def _(tn):
                tn.wait_ge(s_v.h, vv)
                tn.wait_ge(s_s.h, sv)
            TEN(_)

            def _(v):
                v.wait_ge(s_t.h, tv)
                v.wait_ge(s_s.h, sv)
            VEC(_)

            def _(sc):
                sc.wait_ge(s_t.h, tv)
                sc.wait_ge(s_v.h, vv)
            SCA(_)
            if allow_safe and tern_pos[0] < len(tern_steps) \
                    and tern_steps[tern_pos[0]][0] == "safe":
                pump(4, safe=True)

        def _(sy):
            sP.inc(sy.dma_start(out=gsc[:], in_=gs_col), 16)
            sP.inc(sy.dma_start(out=usc[:], in_=us_col), 16)
            sP.inc(sy.dma_start(out=dsc[:], in_=ds_col), 16)
            sP.inc(sy.dma_start(out=gam[:], in_=gam_col), 16)
            sP.inc(sy.dma_start(out=qkvb[:], in_=qkvb_col), 16)
            sP.inc(sy.dma_start(out=obc[:], in_=ob_col), 16)
            sP.inc(sy.dma_start(out=ident[:], in_=ident_in), 16)
        SYN(_)
        setup_sy = sP.n

        def _(g):
            sP2.inc(g.dma_start(out=bvr[:], in_=bv_row), 16)
        GPS(_)
        setup_gp = sP2.n

        def _(v):
            s_v.inc(v.memset(kvst[:], 0.0))
        VEC(_)
        zv = s_v.n

        def _(sy):
            sy.wait_ge(s_v.h, zv)
            for rr in range(16):
                sP.inc(sy.dma_start(out=cck_i[D + rr:D + rr + 1, :],
                                    in_=kvst[0:1, :]), 16)
        SYN(_)

        def _(v):
            v.wait_ge(sP.h, setup_sy)
            v.memset(onesrow_f[:], 1.0)
            v.memset(onesrow_bf[:], 1.0)
            v.memset(m128row[:], -128.0)
            v.memset(onescol_f[:], 1.0)
            v.memset(sccol[:], SCALE)
            s_v.inc(v.memset(onescol_bf[:], 1.0))
        VEC(_)

        tern_steps = []
        tern_abs_cnt = {}
        tern_absred = {}
        tern_p2_cnt = {}
        tern_st_cnt = {}
        tern_marker = {}
        tern_state = {}

        def build_tern():
            if n_layers < 2:
                return
            TC2 = TCH // 2
            specs = [(0, w1g, tg_d), (1, w1u, tu_d), (2, w1d, td_d)]
            for wi, wsrc, wdst in specs:
                Mo, K = wsrc.shape[0], wsrc.shape[2]
                ncols = K // TC2
                nch = Mo * ncols
                NEL = float(Mo * 128 * K)

                def mean_chunk(ci, wi=wi, wsrc=wsrc, ncols=ncols, nch=nch):
                    mo, co = divmod(ci, ncols)
                    bq = ci % 4
                    buf = terni if bq < 2 else ternm
                    hf = bq % 2
                    bsl = buf[:, hf * TC2:(hf + 1) * TC2]
                    my = [0]

                    def _(sy):
                        if ci >= 4:
                            sy.wait_ge(s_v.h, tern_absred[(wi, ci - 4)])
                        elif wi >= 1:
                            sy.wait_ge(s_v.h, tern_p2_cnt[(wi - 1, 256 - 1)])
                        my[0] = sT.inc(sy.dma_start(
                            out=bsl,
                            in_=wsrc[mo, :, co * TC2:(co + 1) * TC2]), 16)
                    SYN(_)

                    def _(v):
                        v.wait_ge(sT.h, my[0])
                        tern_absred[(wi, ci)] = s_v.inc(v.reduce_sum(
                            out=acc[:, ci:ci + 1], in_=bsl,
                            axis=AX.X, apply_absolute_value=True))
                    VEC(_)
                for ci in range(nch):
                    tern_steps.append(("any", lambda ci=ci, f=mean_chunk: f(ci)))

                def finish(wi=wi, nch=nch, NEL=NEL):
                    need_v = tern_absred[(wi, nch - 1)]
                    vpre, tpre = s_v.n, s_t.n

                    def _(v):
                        v.wait_ge(s_v.h, need_v)
                        v.wait_ge(s_t.h, tpre)
                        c = s_v.inc(v.reduce_sum(out=mcols[:, 6:7], in_=acc[:, 0:nch],
                                                 axis=AX.X))
                        v.wait_ge(s_v.h, c)
                        s_v.inc(v.tensor_copy(out=mcols[:, 7:8], in_=mcols[:, 6:7]))
                    VEC(_)
                    v1 = s_v.n

                    def _(tn):
                        tn.wait_ge(s_v.h, v1)
                        s_t.inc(tn.matmul(pr0[0:1, 0:1], onescol_f[:], mcols[:, 7:8],
                                          start=True, stop=True))
                    TEN(_)
                    t1 = s_t.n

                    def _(v):
                        v.wait_ge(s_t.h, t1)
                        v.tensor_scalar(out=wmean[0:1, wi + 1:wi + 2],
                                        in0=pr0[0:1, 0:1],
                                        scalar1=1.0 / (NEL * 127.0),
                                        scalar2=1e-5 / 127.0,
                                        op0=ALU.mult, op1=ALU.max)
                        c = s_v.inc(v.tensor_scalar(out=sc1[0:1, 5:6],
                                                    in0=pr0[0:1, 0:1],
                                                    scalar1=1.0 / NEL, scalar2=1e-5,
                                                    op0=ALU.mult, op1=ALU.max))
                        v.wait_ge(s_v.h, c)
                        s_v.inc(v.reciprocal(out=wrecip[wi][:], in_=sc1[0:1, 5:6]))
                    VEC(_)
                    v2 = s_v.n

                    def _(tn):
                        tn.wait_ge(s_v.h, v2)
                        s_t.inc(tn.matmul(pw[5][:, 0:1], onesrow_f[:], wrecip[wi][:],
                                          start=True, stop=True))
                    TEN(_)
                    t2 = s_t.n

                    def _(v):
                        v.wait_ge(s_t.h, t2)
                        s_v.inc(v.tensor_copy(out=swsp[wi][:], in_=pw[5][:, 0:1]))
                    VEC(_)
                tern_steps.append(("safe", finish))

                def tern_chunk(ci, wi=wi, wsrc=wsrc, wdst=wdst, ncols=ncols, nch=nch):
                    mo, co = divmod(ci, ncols)
                    hf = ci % 2
                    sl = slice(hf * TC2, (hf + 1) * TC2)
                    my = [0]

                    def _(sy):
                        if ci >= 2:
                            sy.wait_ge(s_v.h, tern_p2_cnt[(wi, ci - 2)])
                        else:
                            sy.wait_ge(s_v.h, tern_absred[(wi, nch - 1)])
                        my[0] = sT.inc(sy.dma_start(
                            out=terni[:, sl],
                            in_=wsrc[mo, :, co * TC2:(co + 1) * TC2]), 16)
                    SYN(_)

                    def _(v):
                        v.wait_ge(sT.h, my[0])
                        if ci >= 2:
                            v.wait_ge(sTS.h, tern_st_cnt[(wi, ci - 2)])
                        v.tensor_scalar(out=ternm[:, sl],
                                        in0=terni[:, sl],
                                        scalar1=swsp[wi][:], scalar2=None, op0=ALU.mult)
                        tern_p2_cnt[(wi, ci)] = s_v.inc(v.tensor_scalar(
                            out=terno[:, sl], in0=ternm[:, sl],
                            scalar1=-1.4375, scalar2=1.4375, op0=ALU.max, op1=ALU.min))
                    VEC(_)
                    v1 = s_v.n

                    def _(sy):
                        sy.wait_ge(s_v.h, v1)
                        tern_st_cnt[(wi, ci)] = sTS.inc(sy.dma_start(
                            out=wdst[mo, :, co * TC2:(co + 1) * TC2],
                            in_=terno[:, sl]), 16)
                    SYN(_)
                for ci in range(nch):
                    tern_steps.append(("any", lambda ci=ci, f=tern_chunk: f(ci)))

                def endmark(wi=wi):
                    tern_marker[wi] = marker([sTS])
                tern_steps.append(("any", endmark))

        build_tern()
        tern_pos = [0]

        def pump(n, safe=False):
            k = 0
            while k < n and tern_pos[0] < len(tern_steps):
                tag, fn = tern_steps[tern_pos[0]]
                if tag == "safe" and not safe:
                    break
                fn()
                tern_pos[0] += 1
                k += 1

        def pump_flush():
            pump(len(tern_steps), safe=True)

        spill_idx = {}
        spill_mk = {}

        def spill(row_ap, key, v_cnt):
            idx = spill_idx.setdefault(key, len(spill_idx))
            assert idx < 64

            def _(sy):
                sy.wait_ge(s_v.h, v_cnt)
                sSP.inc(sy.dma_start(out=rowspill[idx:idx + 1, :], in_=row_ap), 16)
            SYN(_)
            spill_mk[key] = None

        def unspill(row_ap, key):
            idx = spill_idx[key]
            pre = sSP.n
            res = [0]

            def _(sy):
                sy.wait_ge(sSP.h, pre)
                res[0] = sU.inc(sy.dma_start(out=row_ap, in_=rowspill[idx:idx + 1, :]), 16)
            SYN(_)
            return res[0]

        war_free = [0, 0, 0, 0]
        warslot = [0]
        kst_gate = [0, 0]
        vst_gate = [0, 0]
        wv_free = [0, 0]
        qh_free = [0, 0]
        exp_free = [0, 0]
        pv_free = [0, 0, 0]
        mask_free = [0, 0]
        kv_free = [0, 0]
        upd_free = [0, 0]
        layer_e_marker = [0, 0]

        def load_war(src_ap):
            slot = warslot[0]
            warslot[0] = (slot + 1) % 4
            free = war_free[slot]
            cnt = [0]

            def _(g):
                if free:
                    g.wait_ge(s_t.h, free)
                cnt[0] = sW[slot].inc(g.dma_start(out=war[:, slot * D:(slot + 1) * D],
                                                  in_=src_ap), 16)
            GPS(_)
            return slot, cnt[0]

        def wsl(slot, dm):
            return war[:, slot * D + dm * 128: slot * D + (dm + 1) * 128]

        xsrcs = [x0, xres]
        xdsts = [xres, out]
        if n_layers == 1 or l0_out:
            xdsts[0] = out

        def emit_layer(l):
            L1 = (l == 1)
            xsrc, xdst = xsrcs[l], xdsts[l]
            wqk = w1qk if L1 else w0qk
            wvp = w1v if L1 else w0v
            wop = w1o if L1 else w0o
            wg_t = tg_d if L1 else w0g
            wu_t = tu_d if L1 else w0u
            wd_t = td_d if L1 else w0d

            if L1:
                barrier()
                xmark = layer_e_marker[0]

                def _(sy):
                    sy.wait_ge(s_mk.h, xmark)
                SYN(_)

            def _(v):
                s_v.inc(v.memset(knm[0:1, 0:32], 0.0))
            VEC(_)

            for t in range(NT):
                barrier()
                for f in range(DC):
                    slot = f % 2
                    my = [0]
                    tpre = s_t.n

                    def _(sy, f=f, slot=slot, my=my, tpre=tpre):
                        sy.wait_ge(s_t.h, tpre)
                        my[0] = sX[slot].inc(sy.dma_start(
                            out=xck[:, slot * T:(slot + 1) * T],
                            in_=xsrc[f * 128:(f + 1) * 128, t * T:(t + 1) * T]), 16)
                    SYN(_)

                    def _(sc, slot=slot, my=my):
                        sc.wait_ge(sX[slot].h, my[0])
                        s_s.inc(sc.activation(out=ftmp[:, slot * T:(slot + 1) * T],
                                              in_=xck[:, slot * T:(slot + 1) * T],
                                              func=AF.Square))
                    SCA(_)
                    mys = s_s.n

                    def _(tn, f=f, slot=slot, my=my, mys=mys):
                        tn.wait_ge(sX[slot].h, my[0])
                        tn.matmul(pr0[:], onescol_f[:], xck[:, slot * T:(slot + 1) * T],
                                  start=(f == 0), stop=(f == DC - 1),
                                  skip_group_check=True)
                        tn.wait_ge(s_s.h, mys)
                        s_t.inc(tn.matmul(pr1[:], onescol_f[:],
                                          ftmp[:, slot * T:(slot + 1) * T],
                                          start=(f == 0), stop=(f == DC - 1),
                                          skip_group_check=True))
                    TEN(_)
                stats_t = s_t.n

                def _(v, stats_t=stats_t):
                    v.wait_ge(s_t.h, stats_t)
                    v.tensor_scalar(out=rows[0][:], in0=pr0[:], scalar1=1.0 / D,
                                    scalar2=None, op0=ALU.mult)
                    v.tensor_mul(out=rows[2][:], in0=rows[0][:], in1=rows[0][:])
                    s_v.inc(v.scalar_tensor_tensor(out=rows[2][:], in0=pr1[:],
                                                   scalar=1.0 / D, in1=rows[2][:],
                                                   op0=ALU.mult, op1=ALU.subtract))
                VEC(_)
                v1 = s_v.n

                def _(v):
                    s_v.inc(v.tensor_scalar(out=rows[3][:], in0=rows[2][:],
                                            scalar1=1e-5, scalar2=None, op0=ALU.add))
                VEC(_)
                v1b = s_v.n

                def _(sc, v1b=v1b):
                    sc.wait_ge(s_v.h, v1b)
                    s_s.inc(sc.activation(out=rows[2][:], in_=rows[3][:], func=AF.Sqrt))
                SCA(_)
                s1 = s_s.n

                def _(v, s1=s1):
                    v.wait_ge(s_s.h, s1)
                    s_v.inc(v.reciprocal(out=rows[1][:], in_=rows[2][:]))
                VEC(_)
                vr = s_v.n

                def _(tn, vr=vr):
                    tn.wait_ge(s_v.h, vr)
                    tn.matmul(pw[4][:], onesrow_f[:], rows[0][:], start=True, stop=True)
                    s_t.inc(tn.matmul(pw[5][:], onesrow_f[:], rows[1][:],
                                      start=True, stop=True))
                TEN(_)
                t1 = s_t.n

                def _(v, t1=t1):
                    v.wait_ge(s_t.h, t1)
                    v.tensor_copy(out=bct[:, 0:T], in_=pw[4][:])
                    s_v.inc(v.tensor_copy(out=bct[:, T:2 * T], in_=pw[5][:]))
                VEC(_)
                bc_ready = s_v.n
                spill(rows[0][:], ("m", l, t), bc_ready)

                if L1:
                    def _(v, bc_ready=bc_ready):
                        v.wait_ge(s_v.h, bc_ready)
                        v.memset(mrun[:, 0:T], -1e30)
                        s_v.inc(v.memset(mrun[:, T:2 * T], 1e30))
                    VEC(_)
                for f in range(DC):
                    slot = f % 2
                    my = [0]
                    vpre = s_v.n

                    def _(sy, f=f, slot=slot, my=my, vpre=vpre):
                        sy.wait_ge(s_v.h, vpre)
                        my[0] = sX[slot].inc(sy.dma_start(
                            out=xck[:, slot * T:(slot + 1) * T],
                            in_=xsrc[f * 128:(f + 1) * 128, t * T:(t + 1) * T]), 16)
                    SYN(_)

                    def _(v, f=f, slot=slot, my=my, t=t, bc_ready=bc_ready):
                        v.wait_ge(sX[slot].h, my[0])
                        v.wait_ge(s_v.h, bc_ready)
                        v.tensor_sub(out=ftmp[:, slot * T:(slot + 1) * T],
                                     in0=xck[:, slot * T:(slot + 1) * T],
                                     in1=bct[:, 0:T])
                        ins = v.tensor_mul(out=xn[t][:, f * T:(f + 1) * T],
                                           in0=ftmp[:, slot * T:(slot + 1) * T],
                                           in1=bct[:, T:2 * T])
                        if L1:
                            v.tensor_tensor(out=mrun[:, 0:T],
                                            in0=ftmp[:, slot * T:(slot + 1) * T],
                                            in1=mrun[:, 0:T], op=ALU.max)
                            ins = v.tensor_tensor(out=mrun[:, T:2 * T],
                                                  in0=ftmp[:, slot * T:(slot + 1) * T],
                                                  in1=mrun[:, T:2 * T], op=ALU.min)
                        s_v.inc(ins)
                    VEC(_)
                xn_ready = s_v.n
                if stage == "LN" + str(t):
                    return

                if L1:
                    def _(v, xn_ready=xn_ready):
                        v.wait_ge(s_v.h, xn_ready)
                        v.tensor_scalar(out=mrun[:, T:2 * T], in0=mrun[:, T:2 * T],
                                        scalar1=-1.0, scalar2=None, op0=ALU.mult)
                        s_v.inc(v.tensor_tensor(out=mrun[:, 0:T], in0=mrun[:, 0:T],
                                                in1=mrun[:, T:2 * T], op=ALU.max))
                    VEC(_)
                    mv = s_v.n
                    for tb in range(4):
                        def _(tn, tb=tb, mv=mv):
                            tn.wait_ge(s_v.h, mv)
                            s_t.inc(tn.transpose(pw[tb][:, 0:128],
                                                 mrun[:, tb * 128:(tb + 1) * 128],
                                                 ident[:]))
                        TEN(_)
                    t2 = s_t.n
                    mcbase = mcol_ctr[0]
                    mcol_ctr[0] += 4

                    def _(v, t2=t2, mcbase=mcbase):
                        v.wait_ge(s_t.h, t2)
                        for tb in range(4):
                            ins = v.reduce_max(out=mcols[:, mcbase + tb:mcbase + tb + 1],
                                               in_=pw[tb][:, 0:128], axis=AX.X)
                        s_v.inc(ins)
                    VEC(_)
                    v2 = s_v.n
                    for tb in range(4):
                        def _(tn, tb=tb, v2=v2, mcbase=mcbase):
                            tn.wait_ge(s_v.h, v2)
                            s_t.inc(tn.matmul(pr0[0:1, tb * 128:(tb + 1) * 128],
                                              mcols[:, mcbase + tb:mcbase + tb + 1],
                                              ident[:], start=True, stop=True,
                                              skip_group_check=True))
                        TEN(_)
                    t3 = s_t.n

                    def _(v, t3=t3):
                        v.wait_ge(s_t.h, t3)
                        v.tensor_copy(out=rows[3][:], in_=pr0[:])
                        v.tensor_mul(out=rows[4][:], in0=rows[3][:], in1=rows[1][:])
                        v.tensor_scalar(out=rows[4][:], in0=rows[4][:], scalar1=1e-5,
                                        scalar2=None, op0=ALU.max)
                        v.reciprocal(out=rows[5][:], in_=rows[4][:])
                        s_v.inc(v.scalar_tensor_tensor(out=rows[5][:], in0=rows[5][:],
                                                       scalar=127.0, in1=rows[1][:],
                                                       op0=ALU.mult, op1=ALU.mult))
                    VEC(_)
                    v3 = s_v.n
                    spill(rows[4][:], ("mx", l, t), v3)
                    spill(rows[5][:], ("qscT", l, t), v3)
                else:
                    for f in range(DC):
                        def _(tn, f=f, t=t, xn_ready=xn_ready):
                            tn.wait_ge(s_v.h, xn_ready)
                            s_t.inc(tn.matmul(pr1[:], onescol_bf[:],
                                              xn[t][:, f * T:(f + 1) * T],
                                              start=(f == 0), stop=(f == DC - 1),
                                              skip_group_check=True))
                        TEN(_)
                    t2 = s_t.n

                    def _(v, t2=t2, t_=t):
                        v.wait_ge(s_t.h, t2)
                        s_v.inc(v.tensor_copy(out=srow_bf[t_][:], in_=pr1[:]))
                    VEC(_)

                for hmo in range(16):
                    slot, gcnt = load_war(wqk[16 + hmo])
                    vpre = s_v.n

                    def _(tn, slot=slot, gcnt=gcnt, t=t, xn_ready=xn_ready, vpre=vpre):
                        tn.wait_ge(sW[slot].h, gcnt)
                        tn.wait_ge(s_v.h, vpre)
                        for dm in range(DC):
                            ins = tn.matmul(pw[0][:], wsl(slot, dm),
                                            xn[t][:, dm * T:(dm + 1) * T],
                                            start=(dm == 0), stop=(dm == DC - 1),
                                            skip_group_check=True)
                        s_t.inc(ins)
                    TEN(_)
                    war_free[slot] = s_t.n
                    t1 = s_t.n
                    ks = hmo % 2

                    def _(v, hmo=hmo, t1=t1, ks=ks):
                        v.wait_ge(s_t.h, t1)
                        v.wait_ge(sKS[ks].h, kst_gate[ks])
                        v.wait_ge(sVS[ks].h, vst_gate[ks])
                        b = qkvb[:, 17 + hmo:18 + hmo] if L1 else 0.0
                        v.tensor_scalar(out=kvst[:, ks * T:(ks + 1) * T],
                                        in0=pw[0][:], scalar1=b, scalar2=None,
                                        op0=ALU.add)
                        s_v.inc(v.tensor_mul(out=bfs[:, 0:T],
                                             in0=kvst[:, ks * T:(ks + 1) * T],
                                             in1=kvst[:, ks * T:(ks + 1) * T]))
                    VEC(_)
                    s2 = s_v.n
                    v2 = s_v.n

                    def _(tn, v2=v2):
                        tn.wait_ge(s_v.h, v2)
                        s_t.inc(tn.matmul(pr1[:], onescol_bf[:], bfs[:, 0:T],
                                          start=True, stop=True, skip_group_check=True))
                    TEN(_)
                    t3 = s_t.n

                    def _(v, hmo=hmo, t3=t3):
                        v.wait_ge(s_t.h, t3)
                        c = s_v.inc(v.reduce_max(out=sc1[0:1, 4:5], in_=pr1[:],
                                                 axis=AX.X))
                        v.wait_ge(s_v.h, c)
                        s_v.inc(v.tensor_tensor(out=knm[0:1, hmo:hmo + 1],
                                                in0=knm[0:1, hmo:hmo + 1],
                                                in1=sc1[0:1, 4:5], op=ALU.max))
                    VEC(_)

                    def _(sy, hmo=hmo, s2=s2, ks=ks, t=t):
                        sy.wait_ge(s_v.h, s2)
                        kst_gate[ks] = sKS[ks].inc(sy.dma_start(
                            out=cck_i[hmo * 128:(hmo + 1) * 128, t * T:(t + 1) * T],
                            in_=kvst[:, ks * T:(ks + 1) * T]), 16)
                    SYN(_)
                    pump(6)
                if stage == "K" + str(t):
                    return

                prev_vevac = [s_v.n]
                for cg in range(4):
                    for dm in range(DC):
                        slot = dm % 2
                        my = [0]
                        free = wv_free[slot]

                        def _(g, dm=dm, cg=cg, slot=slot, my=my, free=free):
                            if free:
                                g.wait_ge(s_t.h, free)
                            my[0] = sWV[slot].inc(g.dma_start(
                                out=wvb[:, slot * T:(slot + 1) * T],
                                in_=wvp[dm * 128:(dm + 1) * 128, cg * T:(cg + 1) * T]), 16)
                        GPS(_)

                        def _(tn, dm=dm, slot=slot, my=my, t=t, xn_ready=xn_ready,
                              pe=prev_vevac[0]):
                            tn.wait_ge(sWV[slot].h, my[0])
                            if dm == 0:
                                tn.wait_ge(s_v.h, max(xn_ready, pe))
                            last = (dm == DC - 1) and (not L1)
                            for tb in range(4):
                                ins = tn.matmul(
                                    pw[tb][:],
                                    xn[t][:, dm * T + tb * 128: dm * T + (tb + 1) * 128],
                                    wvb[:, slot * T:(slot + 1) * T],
                                    start=(dm == 0), stop=last, skip_group_check=True)
                            s_t.inc(ins)
                        TEN(_)
                        wv_free[slot] = s_t.n
                    if L1:
                        def _(tn, cg=cg):
                            tn.wait_ge(sP2.h, setup_gp)
                            for tb in range(4):
                                ins = tn.matmul(pw[tb][:], onesrow_bf[:],
                                                bvr[0:1, cg * T:(cg + 1) * T],
                                                start=False, stop=True,
                                                skip_group_check=True)
                            s_t.inc(ins)
                        TEN(_)
                    t4 = s_t.n
                    for tb in range(4):
                        vs = tb % 2
                        if "x" in stage:
                            continue

                        def _(v, tb=tb, t4=t4, vs=vs):
                            v.wait_ge(s_t.h, t4)
                            v.wait_ge(sVS[vs].h, vst_gate[vs])
                            v.wait_ge(sKS[vs].h, kst_gate[vs])
                            s_v.inc(v.tensor_copy(out=kvst[:, vs * T:(vs + 1) * T],
                                                  in_=pw[tb][:]))
                        VEC(_)
                        v4 = s_v.n
                        if "y" in stage:
                            continue

                        def _(sy, tb=tb, cg=cg, v4=v4, vs=vs, t=t):
                            sy.wait_ge(s_v.h, v4)
                            vst_gate[vs] = sVS[vs].inc(sy.dma_start(
                                out=ccv_i[t * T + tb * 128: t * T + (tb + 1) * 128,
                                          cg * T:(cg + 1) * T],
                                in_=kvst[:, vs * T:(vs + 1) * T]), 16)
                        SYN(_)
                    prev_vevac[0] = s_v.n
                    pump(8)
                if stage.startswith("V") and stage.endswith(str(t)):
                    return

            vpre = s_v.n

            def _(g, vpre=vpre):
                g.wait_ge(s_v.h, vpre)
                sKN.inc(g.dma_start(out=cck_i[D:D + 1, 0:32], in_=knm[0:1, 0:32]), 16)
            GPS(_)
            ccmark = marker([sKS[0], sKS[1], sVS[0], sVS[1], sKN, sP])

            def _(g, ccmark=ccmark, l=l):
                g.wait_ge(s_mk.h, ccmark)
                if l == 1:
                    for sdep in (sKB[0], sKB[1], sVB[0], sVB[1]):
                        g.wait_ge(sdep.h, sdep.n)
                s_cc.inc(g.collective_compute("AllGather", ALU.bypass, ins=[cck_i],
                                              outs=[cck_o], replica_groups=RG), 1)
                s_cc.inc(g.collective_compute("AllGather", ALU.bypass, ins=[ccv_i],
                                              outs=[ccv_o], replica_groups=RG), 1)
                g.wait_ge(s_cc.h, s_cc.n)
            GPS(_)
            cc_done = s_cc.n
            if stage == "A":
                return

            for t in range(NT):
                barrier()
                if L1:
                    u1 = unspill(rows[0][:], ("m", l, t))
                    u2 = unspill(rows[5][:], ("qscT", l, t))
                    u3 = unspill(rows[4][:], ("mx", l, t))
                    vpre = s_v.n

                    def _(tn, u1=u1, u2=u2, u3=u3, vpre=vpre):
                        tn.wait_ge(sU.h, max(u1, u2, u3))
                        tn.wait_ge(s_v.h, vpre)
                        tn.matmul(pw[4][:], onesrow_f[:], rows[0][:], start=True,
                                  stop=True)
                        s_t.inc(tn.matmul(pw[5][:], onesrow_f[:], rows[5][:],
                                          start=True, stop=True))
                    TEN(_)
                    t1 = s_t.n

                    def _(v, t1=t1):
                        v.wait_ge(s_t.h, t1)
                        v.tensor_copy(out=bct[:, 0:T], in_=pw[4][:])
                        s_v.inc(v.tensor_copy(out=bct[:, 2 * T:3 * T], in_=pw[5][:]))
                    VEC(_)
                    bcq = s_v.n
                    for f in range(DC):
                        slot = f % 2
                        my = [0]
                        vpre = s_v.n

                        def _(sy, f=f, slot=slot, my=my, vpre=vpre):
                            sy.wait_ge(s_v.h, vpre)
                            my[0] = sX[slot].inc(sy.dma_start(
                                out=xck[:, slot * T:(slot + 1) * T],
                                in_=xsrc[f * 128:(f + 1) * 128, t * T:(t + 1) * T]), 16)
                        SYN(_)

                        def _(v, f=f, slot=slot, my=my, bcq=bcq):
                            v.wait_ge(sX[slot].h, my[0])
                            v.wait_ge(s_v.h, bcq)
                            v.tensor_sub(out=ftmp[:, slot * T:(slot + 1) * T],
                                         in0=xck[:, slot * T:(slot + 1) * T],
                                         in1=bct[:, 0:T])
                            v.tensor_mul(out=qi8[:, slot * T:(slot + 1) * T],
                                         in0=ftmp[:, slot * T:(slot + 1) * T],
                                         in1=bct[:, 2 * T:3 * T])
                            s_v.inc(v.tensor_copy(out=qg[:, f * T:(f + 1) * T],
                                                  in_=qi8[:, slot * T:(slot + 1) * T]))
                        VEC(_)
                    rhs = qg
                    def _(v):
                        v.tensor_scalar(out=rows[0][:], in0=rows[4][:],
                                        scalar1=wmean[0:1, 1:2], scalar2=None,
                                        op0=ALU.mult)
                        s_v.inc(v.tensor_scalar(out=rows[2][:], in0=rows[4][:],
                                                scalar1=wmean[0:1, 2:3], scalar2=None,
                                                op0=ALU.mult))
                    VEC(_)
                    v2 = s_v.n

                    def _(tn, v2=v2):
                        tn.wait_ge(s_v.h, v2)
                        tn.matmul(pw[4][:], onesrow_f[:], rows[0][:], start=True,
                                  stop=True)
                        s_t.inc(tn.matmul(pw[5][:], onesrow_f[:], rows[2][:],
                                          start=True, stop=True))
                    TEN(_)
                    t2 = s_t.n

                    def _(v, t2=t2):
                        v.wait_ge(s_t.h, t2)
                        v.tensor_copy(out=bct[:, 0:T], in_=pw[4][:])
                        v.tensor_copy(out=bct[:, T:2 * T], in_=pw[5][:])
                        v.memset(mrun[:, 0:T], -1e30)
                        s_v.inc(v.memset(mrun[:, T:2 * T], 1e30))
                    VEC(_)
                    cgu = s_v.n

                    pump_flush()

                    def _(g):
                        g.wait_ge(s_mk.h, max(tern_marker.get(0, 0),
                                              tern_marker.get(1, 0)))
                    GPS(_)
                else:
                    rhs = xn[t]
                    cgu = s_v.n

                gu_free = [0, 0, 0, 0]
                for sl in range(4):
                    gu_free[sl] = war_free[sl]
                for ic in range(IC):
                    gsl = ic % 2
                    usl = 2 + (ic % 2)
                    mg = [0]
                    mu = [0]

                    def _(g, ic=ic, gsl=gsl, usl=usl, mg=mg, mu=mu,
                          f1=gu_free[gsl], f2=gu_free[usl]):
                        if f1:
                            g.wait_ge(s_t.h, f1)
                        mg[0] = sW[gsl].inc(g.dma_start(out=war[:, gsl * D:(gsl + 1) * D],
                                                        in_=wg_t[ic]), 16)
                        if f2:
                            g.wait_ge(s_t.h, f2)
                        mu[0] = sW[usl].inc(g.dma_start(out=war[:, usl * D:(usl + 1) * D],
                                                        in_=wu_t[ic]), 16)
                    GPS(_)

                    pg = ic % 2
                    pu = ic % 2
                    gb = pw[0] if pg == 0 else pw[2]
                    ub = pw[1] if pu == 0 else pw[3]

                    def _(tn, ic=ic, gsl=gsl, usl=usl, mg=mg, mu=mu, gb=gb, ub=ub,
                          cgu=cgu, t=t, ef=evac_free):
                        tn.wait_ge(sW[gsl].h, mg[0])
                        if ic == 0:
                            tn.wait_ge(s_v.h, cgu)
                        if ic >= 2:
                            tn.wait_ge(s_v.h, ef[ic % 2])
                            tn.wait_ge(s_s.h, ef_s[ic % 2])
                        for dm in range(DC):
                            ins = tn.matmul(gb[:], wsl(gsl, dm), rhs[:, dm * T:(dm + 1) * T],
                                            start=(dm == 0),
                                            stop=(L1 and dm == DC - 1),
                                            skip_group_check=True)
                        if not L1:
                            ins = tn.matmul(gb[:], m128row[:], srow_bf[t][:],
                                            start=False, stop=True, skip_group_check=True)
                        s_t.inc(ins)
                        gu_free[gsl] = s_t.n
                        tn.wait_ge(sW[usl].h, mu[0])
                        for dm in range(DC):
                            ins = tn.matmul(ub[:], wsl(usl, dm), rhs[:, dm * T:(dm + 1) * T],
                                            start=(dm == 0),
                                            stop=(L1 and dm == DC - 1),
                                            skip_group_check=True)
                        if not L1:
                            ins = tn.matmul(ub[:], m128row[:], srow_bf[t][:],
                                            start=False, stop=True, skip_group_check=True)
                        s_t.inc(ins)
                        gu_free[usl] = s_t.n
                    TEN(_)
                    t1 = s_t.n

                    if not L1:
                        def _(sc, ic=ic, t1=t1, gb=gb):
                            sc.wait_ge(s_t.h, t1)
                            s_s.inc(sc.activation(out=bfs[:, 0:T], in_=gb[:],
                                                  func=AF.Silu, bias=0.0,
                                                  scale=gsc[:, ic + 1:ic + 2]))
                        SCA(_)
                        ef_s[ic % 2] = s_s.n
                        s2 = s_s.n

                        def _(v, ic=ic, t1=t1, s2=s2, ub=ub):
                            v.wait_ge(s_t.h, t1)
                            v.tensor_scalar(out=bfs[:, T:2 * T], in0=ub[:],
                                            scalar1=usc[:, ic + 1:ic + 2], scalar2=None,
                                            op0=ALU.mult)
                            v.wait_ge(s_s.h, s2)
                            s_v.inc(v.tensor_mul(out=har[:, ic * T:(ic + 1) * T],
                                                 in0=bfs[:, 0:T], in1=bfs[:, T:2 * T]))
                        VEC(_)
                        evac_free[ic % 2] = s_v.n
                        v2 = s_v.n

                        def _(tn, ic=ic, v2=v2):
                            tn.wait_ge(s_v.h, v2)
                            s_t.inc(tn.matmul(pr1[:], onescol_bf[:],
                                              har[:, ic * T:(ic + 1) * T],
                                              start=(ic == 0), stop=(ic == IC - 1),
                                              skip_group_check=True))
                        TEN(_)
                    else:
                        def _(v, ic=ic, t1=t1, gb=gb):
                            v.wait_ge(s_t.h, t1)
                            s_v.inc(v.tensor_mul(out=ftmp[:, 0:T], in0=gb[:],
                                                 in1=bct[:, 0:T]))
                        VEC(_)
                        v1 = s_v.n

                        def _(sc, v1=v1):
                            sc.wait_ge(s_v.h, v1)
                            s_s.inc(sc.activation(out=bfs[:, 0:T], in_=ftmp[:, 0:T],
                                                  func=AF.Silu))
                        SCA(_)
                        ef_s[ic % 2] = s_s.n
                        s2 = s_s.n

                        def _(v, ic=ic, s2=s2, ub=ub):
                            v.tensor_mul(out=ftmp[:, T:2 * T], in0=ub[:],
                                         in1=bct[:, T:2 * T])
                            v.wait_ge(s_s.h, s2)
                            v.tensor_mul(out=har[:, ic * T:(ic + 1) * T],
                                         in0=ftmp[:, T:2 * T], in1=bfs[:, 0:T])
                            v.tensor_mul(out=bfs[:, 2 * T:3 * T],
                                         in0=har[:, ic * T:(ic + 1) * T],
                                         in1=har[:, ic * T:(ic + 1) * T])
                            v.scalar_tensor_tensor(out=ftmp[:, 0:T],
                                                   in0=har[:, ic * T:(ic + 1) * T],
                                                   scalar=gam[:, ic + 1:ic + 2],
                                                   in1=mrun[:, 0:T],
                                                   op0=ALU.mult, op1=ALU.max)
                            v.tensor_copy(out=mrun[:, 0:T], in_=ftmp[:, 0:T])
                            v.scalar_tensor_tensor(out=ftmp[:, 0:T],
                                                   in0=har[:, ic * T:(ic + 1) * T],
                                                   scalar=gam[:, ic + 1:ic + 2],
                                                   in1=mrun[:, T:2 * T],
                                                   op0=ALU.mult, op1=ALU.min)
                            s_v.inc(v.tensor_copy(out=mrun[:, T:2 * T], in_=ftmp[:, 0:T]))
                        VEC(_)
                        evac_free[ic % 2] = s_v.n
                        v2 = s_v.n

                        def _(tn, ic=ic, v2=v2):
                            tn.wait_ge(s_v.h, v2)
                            s_t.inc(tn.matmul(pr1[:], onescol_bf[:], bfs[:, 2 * T:3 * T],
                                              start=(ic == 0), stop=(ic == IC - 1),
                                              skip_group_check=True))
                        TEN(_)
                    pump(2)
                hs_t = s_t.n
                hs_v = s_v.n

                if not L1:
                    def _(v, hs_t=hs_t, t_=t):
                        v.wait_ge(s_t.h, hs_t)
                        s_v.inc(v.tensor_copy(out=srow_bf[t_][:], in_=pr1[:]))
                    VEC(_)
                else:
                    def _(v, hs_t=hs_t):
                        v.wait_ge(s_t.h, hs_t)
                        s_v.inc(v.tensor_scalar(out=rows[3][:], in0=pr1[:],
                                                scalar1=1.0 / II, scalar2=1e-6,
                                                op0=ALU.mult, op1=ALU.add))
                    VEC(_)
                    rss_v = s_v.n

                    def _(sc, rss_v=rss_v):
                        sc.wait_ge(s_v.h, rss_v)
                        s_s.inc(sc.activation(out=rows[6][:], in_=rows[3][:],
                                              func=AF.Sqrt))
                    SCA(_)
                    rstd_ss = s_s.n

                    def _(v, rstd_ss=rstd_ss):
                        v.wait_ge(s_s.h, rstd_ss)
                        s_v.inc(v.reciprocal(out=rows[2][:], in_=rows[6][:]))
                    VEC(_)
                    rstd_s = s_s.n

                    def _(v, hs_v=hs_v):
                        v.wait_ge(s_v.h, hs_v)
                        v.tensor_scalar(out=mrun[:, T:2 * T], in0=mrun[:, T:2 * T],
                                        scalar1=-1.0, scalar2=None, op0=ALU.mult)
                        s_v.inc(v.tensor_tensor(out=mrun[:, 0:T], in0=mrun[:, 0:T],
                                                in1=mrun[:, T:2 * T], op=ALU.max))
                    VEC(_)
                    mv = s_v.n
                    for tb in range(4):
                        def _(tn, tb=tb, mv=mv):
                            tn.wait_ge(s_v.h, mv)
                            s_t.inc(tn.transpose(pw[tb][:, 0:128],
                                                 mrun[:, tb * 128:(tb + 1) * 128],
                                                 ident[:]))
                        TEN(_)
                    t2 = s_t.n
                    mcbase = mcol_ctr[0]
                    mcol_ctr[0] += 4

                    def _(v, t2=t2, mcbase=mcbase):
                        v.wait_ge(s_t.h, t2)
                        for tb in range(4):
                            ins = v.reduce_max(out=mcols[:, mcbase + tb:mcbase + tb + 1],
                                               in_=pw[tb][:, 0:128], axis=AX.X)
                        s_v.inc(ins)
                    VEC(_)
                    v2 = s_v.n
                    for tb in range(4):
                        def _(tn, tb=tb, v2=v2, mcbase=mcbase):
                            tn.wait_ge(s_v.h, v2)
                            s_t.inc(tn.matmul(pr0[0:1, tb * 128:(tb + 1) * 128],
                                              mcols[:, mcbase + tb:mcbase + tb + 1],
                                              ident[:], start=True, stop=True,
                                              skip_group_check=True))
                        TEN(_)
                    t3 = s_t.n

                    def _(v, t3=t3, rstd_s=rstd_s):
                        v.wait_ge(s_t.h, t3)
                        v.wait_ge(s_s.h, rstd_s)
                        v.tensor_copy(out=rows[3][:], in_=pr0[:])
                        v.tensor_mul(out=rows[4][:], in0=rows[3][:], in1=rows[2][:])
                        v.tensor_scalar(out=rows[4][:], in0=rows[4][:], scalar1=1e-5,
                                        scalar2=None, op0=ALU.max)
                        v.reciprocal(out=rows[5][:], in_=rows[4][:])
                        v.scalar_tensor_tensor(out=rows[5][:], in0=rows[5][:],
                                               scalar=127.0, in1=rows[2][:],
                                               op0=ALU.mult, op1=ALU.mult)
                        s_v.inc(v.tensor_scalar(out=rows[6][:], in0=rows[4][:],
                                                scalar1=wmean[0:1, 3:4], scalar2=None,
                                                op0=ALU.mult))
                    VEC(_)
                    v3 = s_v.n

                    def _(tn, v3=v3):
                        tn.wait_ge(s_v.h, v3)
                        tn.matmul(pw[4][:], onesrow_f[:], rows[5][:], start=True,
                                  stop=True)
                        s_t.inc(tn.matmul(pw[5][:], onesrow_f[:], rows[6][:],
                                          start=True, stop=True))
                    TEN(_)
                    t4 = s_t.n

                    def _(v, t4=t4):
                        v.wait_ge(s_t.h, t4)
                        v.tensor_copy(out=bct[:, 2 * T:3 * T], in_=pw[4][:])
                        s_v.inc(v.tensor_copy(out=bct[:, 3 * T:4 * T], in_=pw[5][:]))
                    VEC(_)
                    qsch_v = s_v.n

                    for ic2 in range(IC):
                        slot = ic2 % 2

                        def _(v, ic2=ic2, slot=slot, qsch_v=qsch_v):
                            if ic2 == 0:
                                v.wait_ge(s_v.h, qsch_v)
                            v.scalar_tensor_tensor(out=qi8[:, slot * T:(slot + 1) * T],
                                                   in0=har[:, ic2 * T:(ic2 + 1) * T],
                                                   scalar=gam[:, ic2 + 1:ic2 + 2],
                                                   in1=bct[:, 2 * T:3 * T],
                                                   op0=ALU.mult, op1=ALU.mult)
                            s_v.inc(v.tensor_copy(out=har[:, ic2 * T:(ic2 + 1) * T],
                                                  in_=qi8[:, slot * T:(slot + 1) * T]))
                        VEC(_)
                    def _(g):
                        g.wait_ge(s_mk.h, tern_marker.get(2, 0))
                    GPS(_)
                qdone_v = s_v.n

                dn_free = list(gu_free)
                for hf in range(DC):
                    pd = pw[4] if hf % 2 == 0 else pw[5]
                    for sub in range(4):
                        my = [0]
                        free = dn_free[sub]

                        def _(g, hf=hf, sub=sub, my=my, free=free):
                            if free:
                                g.wait_ge(s_t.h, free)
                            my[0] = sW[sub].inc(g.dma_start(
                                out=war[:, sub * D:(sub + 1) * D],
                                in_=wd_t[hf, :, sub * D:(sub + 1) * D]), 16)
                        GPS(_)

                        def _(tn, hf=hf, sub=sub, my=my, pd=pd, qdone_v=qdone_v, t=t,
                              dse=down_evac):
                            tn.wait_ge(sW[sub].h, my[0])
                            if sub == 0:
                                tn.wait_ge(s_v.h, qdone_v)
                                if hf >= 2 and dse[hf % 2] is not None:
                                    tn.wait_ge(dse[hf % 2][0].h, dse[hf % 2][1])
                                    tn.wait_ge(s_s.h, dse_s[hf % 2])
                            for k in range(DC):
                                ic2 = sub * DC + k
                                ins = tn.matmul(pd[:], wsl(sub, k),
                                                har[:, ic2 * T:(ic2 + 1) * T],
                                                start=(ic2 == 0),
                                                stop=(L1 and ic2 == IC - 1),
                                                skip_group_check=True)
                            if sub == 3 and not L1:
                                ins = tn.matmul(pd[:], m128row[:], srow_bf[t][:],
                                                start=False, stop=True,
                                                skip_group_check=True)
                            s_t.inc(ins)
                            dn_free[sub] = s_t.n
                        TEN(_)
                    t1 = s_t.n

                    if not L1:
                        def _(sc, hf=hf, t1=t1, pd=pd):
                            sc.wait_ge(s_t.h, t1)
                            s_s.inc(sc.activation(out=qg[:, hf * T:(hf + 1) * T],
                                                  in_=pd[:], func=AF.Silu, bias=0.0,
                                                  scale=dsc[:, hf + 1:hf + 2]))
                        SCA(_)
                        down_evac[hf % 2] = (s_s, s_s.n)
                        dse_s[hf % 2] = s_s.n
                    else:
                        def _(v, hf=hf, t1=t1, pd=pd):
                            v.wait_ge(s_t.h, t1)
                            s_v.inc(v.tensor_mul(out=ftmp[:, 0:T], in0=pd[:],
                                                 in1=bct[:, 3 * T:4 * T]))
                        VEC(_)
                        down_evac[hf % 2] = (s_v, s_v.n)
                        v1 = s_v.n

                        def _(sc, hf=hf, v1=v1):
                            sc.wait_ge(s_v.h, v1)
                            s_s.inc(sc.activation(out=qg[:, hf * T:(hf + 1) * T],
                                                  in_=ftmp[:, 0:T], func=AF.Silu))
                        SCA(_)
                        dse_s[hf % 2] = s_s.n
                    pump(4)
                for sl in range(4):
                    war_free[sl] = dn_free[sl]
                g_done = s_s.n
                if stage == "C":
                    return

                barrier()
                my = [0]

                def _(sy, cc_done=cc_done, my=my):
                    sy.wait_ge(s_cc.h, cc_done)
                    sCM.inc(sy.dma_start(out=ccm_sb[0:1, 0:16],
                                         in_=cck_o[D:D + 1, 0:16]), 16)
                    my[0] = sCM.inc(sy.dma_start(out=ccm_sb[0:1, 16:32],
                                                 in_=cck_o[2 * D + 16:2 * D + 17, 0:16]), 16)
                SYN(_)
                sy1 = sCM.n

                def _(v, sy1=sy1):
                    v.wait_ge(sCM.h, sy1)
                    s_v.inc(v.tensor_tensor(out=knm[0:1, 16:32], in0=ccm_sb[0:1, 0:16],
                                            in1=ccm_sb[0:1, 16:32], op=ALU.max))
                VEC(_)
                knall_v = s_v.n

                for h in range(NH):
                    vpre, spre = s_v.n, s_s.n

                    def _(tn, vpre=vpre, spre=spre):
                        tn.wait_ge(s_v.h, vpre)
                        tn.wait_ge(s_s.h, spre)
                    TEN(_)

                    slot, gcnt = load_war(wqk[h])
                    qs = h % 2

                    def _(tn, slot=slot, gcnt=gcnt, t=t):
                        tn.wait_ge(sW[slot].h, gcnt)
                        for dm in range(DC):
                            ins = tn.matmul(pw[3][:], wsl(slot, dm),
                                            xn[t][:, dm * T:(dm + 1) * T],
                                            start=(dm == 0), stop=(dm == DC - 1),
                                            skip_group_check=True)
                        s_t.inc(ins)
                    TEN(_)
                    war_free[slot] = s_t.n
                    t1 = s_t.n

                    def _(v, h=h, t1=t1, qs=qs, qf=qh_free[qs]):
                        v.wait_ge(s_t.h, max(t1, qf))
                        b = qkvb[:, h + 1:h + 2] if L1 else 0.0
                        v.tensor_scalar(out=qhb[:, qs * T:(qs + 1) * T],
                                        in0=pw[3][:], scalar1=b, scalar2=None,
                                        op0=ALU.add)
                        s_v.inc(v.tensor_mul(out=bfs[:, 0:T],
                                             in0=qhb[:, qs * T:(qs + 1) * T],
                                             in1=qhb[:, qs * T:(qs + 1) * T]))
                    VEC(_)
                    v1 = s_v.n

                    def _(tn, v1=v1):
                        tn.wait_ge(s_v.h, v1)
                        s_t.inc(tn.matmul(pr0[:], onescol_bf[:], bfs[:, 0:T],
                                          start=True, stop=True, skip_group_check=True))
                    TEN(_)
                    t2 = s_t.n

                    def _(v, h=h, t2=t2, knall_v=knall_v):
                        v.wait_ge(s_t.h, t2)
                        v.wait_ge(s_v.h, knall_v)
                        c = s_v.inc(v.reduce_max(out=sc1[0:1, 1:2], in_=pr0[:],
                                                 axis=AX.X))
                        v.wait_ge(s_v.h, c)
                        s_v.inc(v.tensor_tensor(out=sc1[0:1, 2:3], in0=sc1[0:1, 1:2],
                                                in1=knm[0:1, 16 + h:17 + h],
                                                op=ALU.mult))
                    VEC(_)
                    v2 = s_v.n

                    def _(sc, v2=v2):
                        sc.wait_ge(s_v.h, v2)
                        s_s.inc(sc.activation(out=sc1[0:1, 3:4], in_=sc1[0:1, 2:3],
                                              func=AF.Sqrt))
                    SCA(_)
                    s2 = s_s.n

                    def _(v, s2=s2):
                        v.wait_ge(s_s.h, s2)
                        s_v.inc(v.tensor_scalar(out=sc1[0:1, 4:5], in0=sc1[0:1, 3:4],
                                                scalar1=-SCALE, scalar2=None,
                                                op0=ALU.mult))
                    VEC(_)
                    v3 = s_v.n

                    def _(tn, v3=v3):
                        tn.wait_ge(s_v.h, v3)
                        s_t.inc(tn.matmul(pw[5][:, 0:1], onesrow_f[:], sc1[0:1, 4:5],
                                          start=True, stop=True))
                    TEN(_)
                    t3 = s_t.n

                    def _(v, t3=t3):
                        v.wait_ge(s_t.h, t3)
                        s_v.inc(v.tensor_copy(out=biascol[:], in_=pw[5][:, 0:1]))
                    VEC(_)
                    bias_v = s_v.n

                    first = [True]
                    for r in range(2):
                        myk = [0]
                        myv_ = [0]

                        def _(sy, h=h, r=r, myk=myk, myv_=myv_, cc_done=cc_done,
                              kf=kv_free[r]):
                            sy.wait_ge(s_cc.h, cc_done)
                            if kf:
                                sy.wait_ge(s_t.h, kf)
                            myk[0] = sKB[r].inc(sy.dma_start(
                                out=kbuf[:, r * TPC:(r + 1) * TPC],
                                in_=cck_o[r * (D + 16) + h * 128:
                                          r * (D + 16) + (h + 1) * 128, :]), 16)
                            inap = ccv_o[r * TPC:(r + 1) * TPC,
                                         h * 128:(h + 1) * 128].rearrange(
                                "(b p) d -> p b d", p=128)
                            outap = vbuf[:, r * TPC:(r + 1) * TPC].rearrange(
                                "p (b d) -> p b d", d=128)
                            myv_[0] = sVB[r].inc(sy.dma_start(out=outap, in_=inap), 16)
                        SYN(_)

                        for kb2 in range(4):
                            gp = r * 4 + kb2
                            ps = gp % 2
                            mym = [0]
                            if L1:
                                def _(sy, t=t, gp=gp, kb2=kb2, r=r, mym=mym,
                                      mf=mask_free[0]):
                                    if mf:
                                        sy.wait_ge(s_v.h, mf)
                                    gkb = (r * 4 + kb2) * 2
                                    sM[0].inc(sy.dma_start(
                                        out=mbuf[:, 0:T],
                                        in_=mask_d[t * 16 + gkb]), 16)
                                    mym[0] = sM[0].inc(sy.dma_start(
                                        out=mbuf[:, T:2 * T],
                                        in_=mask_d[t * 16 + gkb + 1]), 16)
                                SYN(_)

                            def _(tn, kb2=kb2, r=r, myk=myk, qs=qs,
                                  ef=exp_free[0]):
                                tn.wait_ge(sKB[r].h, myk[0])
                                if ef:
                                    tn.wait_ge(s_s.h, ef)
                                for j in range(2):
                                    kb = kb2 * 2 + j
                                    s_t.inc(tn.matmul(
                                        pws[:, j * T:(j + 1) * T],
                                        kbuf[:, r * TPC + kb * 128:
                                             r * TPC + (kb + 1) * 128],
                                        qhb[:, qs * T:(qs + 1) * T],
                                        start=True, stop=True,
                                        skip_group_check=True))
                            TEN(_)
                            t4 = s_t.n

                            def _(sc, ps=ps, t4=t4,
                                  bias_v=bias_v, pf=pv_free[ps]):
                                sc.wait_ge(s_t.h, t4)
                                sc.wait_ge(s_v.h, bias_v)
                                if pf:
                                    sc.wait_ge(s_t.h, pf)
                                ins = sc.activation(
                                    out=pbuf[:, ps * 2 * T:(ps + 1) * 2 * T],
                                    in_=pws[:, 0:2 * T], func=AF.Exp,
                                    bias=biascol[:], scale=sccol[:])
                                exp_free[0] = s_s.inc(ins)
                            SCA(_)
                            s3 = s_s.n

                            if L1:
                                def _(v, ps=ps, s3=s3, mym=mym):
                                    v.wait_ge(s_s.h, s3)
                                    v.wait_ge(sM[0].h, mym[0])
                                    ins = v.tensor_tensor(
                                        out=pbuf[:, ps * 2 * T:(ps + 1) * 2 * T],
                                        in0=pbuf[:, ps * 2 * T:(ps + 1) * 2 * T],
                                        in1=mbuf[:, 0:2 * T],
                                        op=ALU.mult)
                                    mask_free[0] = s_v.inc(ins)
                                VEC(_)
                                pvw = ("v", s_v.n)
                            else:
                                pvw = ("s", s3)

                            def _(tn, kb2=kb2, r=r, ps=ps, myv_=myv_, pvw=pvw,
                                  first=first, gp=gp):
                                if pvw[0] == "v":
                                    tn.wait_ge(s_v.h, pvw[1])
                                else:
                                    tn.wait_ge(s_s.h, pvw[1])
                                tn.wait_ge(sVB[r].h, myv_[0])
                                ins = None
                                for j in range(2):
                                    kb = kb2 * 2 + j
                                    st = first[0] and j == 0
                                    sp = gp == 7 and j == 1
                                    tn.matmul(pr1[:], onescol_bf[:],
                                              pbuf[:, (ps * 2 + j) * T:
                                                   (ps * 2 + j + 1) * T],
                                              start=st, stop=sp,
                                              skip_group_check=True)
                                    ins = tn.matmul(
                                        pw[4][:],
                                        vbuf[:, r * TPC + kb * 128:
                                             r * TPC + (kb + 1) * 128],
                                        pbuf[:, (ps * 2 + j) * T:
                                             (ps * 2 + j + 1) * T],
                                        start=st, stop=sp,
                                        skip_group_check=True)
                                pv_free[ps] = s_t.inc(ins)
                                first[0] = False
                            TEN(_)
                        kv_free[r] = s_t.n
                    qh_free[qs] = s_t.n
                    pvdone = s_t.n

                    def _(v, pvdone=pvdone):
                        v.wait_ge(s_t.h, pvdone)
                        s_v.inc(v.reciprocal(out=rows[2][:], in_=pr1[:]))
                    VEC(_)
                    v4 = s_v.n

                    def _(tn, v4=v4):
                        tn.wait_ge(s_v.h, v4)
                        s_t.inc(tn.matmul(pw[5][:], onesrow_f[:], rows[2][:],
                                          start=True, stop=True))
                    TEN(_)
                    t5 = s_t.n

                    def _(v, h=h, t5=t5):
                        v.wait_ge(s_t.h, t5)
                        v.tensor_copy(out=bct[:, 3 * T:4 * T], in_=pw[5][:])
                        s_v.inc(v.tensor_mul(out=har[:, (16 + h) * T:(17 + h) * T],
                                             in0=pw[4][:], in1=bct[:, 3 * T:4 * T]))
                    VEC(_)
                    pump(2)
                ao_done = s_v.n

                for mo in range(DC):
                    slot, gcnt = load_war(wop[mo])

                    def _(tn, slot=slot, gcnt=gcnt, ao_done=ao_done, mo=mo,
                          spre=s_v.n):
                        tn.wait_ge(sW[slot].h, gcnt)
                        if mo == 0:
                            tn.wait_ge(s_v.h, ao_done)
                        else:
                            tn.wait_ge(s_v.h, spre)
                        for dm in range(DC):
                            ins = tn.matmul(pw[3][:], wsl(slot, dm),
                                            har[:, (16 + dm) * T:(17 + dm) * T],
                                            start=(dm == 0), stop=(dm == DC - 1),
                                            skip_group_check=True)
                        s_t.inc(ins)
                    TEN(_)
                    war_free[slot] = s_t.n
                    t1 = s_t.n

                    def _(v, mo=mo, t1=t1):
                        v.wait_ge(s_t.h, t1)
                        b = obc[:, mo + 1:mo + 2] if L1 else 0.0
                        s_v.inc(v.tensor_scalar(out=har[:, (32 + mo) * T:(33 + mo) * T],
                                                in0=pw[3][:], scalar1=b, scalar2=None,
                                                op0=ALU.add))
                    VEC(_)
                    pump(2)
                s_done = s_v.n
                if stage == "D":
                    return

                for f in range(DC):
                    slot = f % 2
                    my = [0]

                    def _(sy, f=f, slot=slot, my=my, uf=upd_free[slot]):
                        if uf:
                            sy.wait_ge(sUP[slot].h, uf)
                        my[0] = sX[slot].inc(sy.dma_start(
                            out=xck[:, slot * T:(slot + 1) * T],
                            in_=xsrc[f * 128:(f + 1) * 128, t * T:(t + 1) * T]), 16)
                    SYN(_)

                    def _(v, f=f, slot=slot, my=my, s_done=s_done, g_done=g_done):
                        v.wait_ge(s_s.h, g_done)
                        v.wait_ge(s_v.h, s_done)
                        v.wait_ge(sX[slot].h, my[0])
                        if stage == "dumps" and l == n_layers - 1:
                            s_v.inc(v.tensor_copy(out=xck[:, slot * T:(slot + 1) * T],
                                                  in_=har[:, (32 + f) * T:(33 + f) * T]))
                        elif stage == "dumpg" and l == n_layers - 1:
                            s_v.inc(v.tensor_copy(out=xck[:, slot * T:(slot + 1) * T],
                                                  in_=qg[:, f * T:(f + 1) * T]))
                        else:
                            v.tensor_mul(out=ftmp[:, slot * T:(slot + 1) * T],
                                         in0=har[:, (32 + f) * T:(33 + f) * T],
                                         in1=qg[:, f * T:(f + 1) * T])
                            s_v.inc(v.tensor_add(out=xck[:, slot * T:(slot + 1) * T],
                                                 in0=xck[:, slot * T:(slot + 1) * T],
                                                 in1=ftmp[:, slot * T:(slot + 1) * T]))
                    VEC(_)
                    v1 = s_v.n

                    def _(sy, f=f, slot=slot, v1=v1, t=t):
                        sy.wait_ge(s_v.h, v1)
                        upd_free[slot] = sUP[slot].inc(sy.dma_start(
                            out=xdst[f * 128:(f + 1) * 128, t * T:(t + 1) * T],
                            in_=xck[:, slot * T:(slot + 1) * T]), 16)
                    SYN(_)
                    pump(10)
            layer_e_marker[l] = marker([sUP[0], sUP[1]])

        mcol_ctr = [8]
        evac_free = [0, 0]
        ef_s = [0, 0]
        down_evac = [None, None]
        dse_s = [0, 0]

        for l in range(n_layers):
            emit_layer(l)
        pump_flush()

        if stage == "dumpt":
            pre = (s_mk.n, sUP[0].n, sUP[1].n)

            def _(g, pre=pre):
                g.wait_ge(s_mk.h, pre[0])
                g.wait_ge(sUP[0].h, pre[1])
                g.wait_ge(sUP[1].h, pre[2])
                for mo in range(16):
                    sKN.inc(g.dma_start(out=out[mo * 128:(mo + 1) * 128, :],
                                        in_=tg_d[mo, :, 0:TPC]), 16)
                for i in range(3):
                    sKN.inc(g.dma_start(out=out[0:128, 1020 + i:1021 + i],
                                        in_=swsp[i][:],
                                        allow_slow_non_contiguous=True), 16)
                sKN.inc(g.dma_start(out=out[128:129, 1008:1016],
                                    in_=wmean[0:1, :]), 16)
                sKN.inc(g.dma_start(out=out[129:130, 1008:1016],
                                    in_=sc1[0:1, :]), 16)
                sKN.inc(g.dma_start(out=out[130:258, 1016:1017],
                                    in_=acc[:, 0:1],
                                    allow_slow_non_contiguous=True), 16)
            GPS(_)
            marker([sKN])

        marker([sKS[0], sKS[1], sVS[0], sVS[1], sUP[0], sUP[1], sSP, sTS, sKN])
        mk_fin = s_mk.n

        def _(sy):
            sy.wait_ge(s_mk.h, mk_fin)
        SYN(_)

    return nc, sbuf_left


def _tile_lhsT(w):
    K, M = w.shape
    Ko, Mo = K // 128, M // 128
    return np.ascontiguousarray(
        w.reshape(Ko, 128, Mo, 128).transpose(2, 1, 0, 3).reshape(Mo, 128, K))


def _u8(wi32):
    return np.ascontiguousarray(wi32.view(np.uint8)[..., 0::4])


def _padcol(a):
    return np.ascontiguousarray(
        np.concatenate([np.zeros((a.shape[0], 1), a.dtype), a], axis=1))


def make_inputs(kw):
    f32 = np.float32
    x = np.asarray(kw["L0_attn_x"], f32)
    w0qkv = np.asarray(kw["L0_attn_w_qkv"], f32)
    w0o = np.asarray(kw["L0_attn_w_out"], f32)
    g0 = _u8(np.ascontiguousarray(np.asarray(kw["L0_ffn_gate_w"], np.int32)))
    u0 = _u8(np.ascontiguousarray(np.asarray(kw["L0_ffn_up_w"], np.int32)))
    d0 = _u8(np.ascontiguousarray(np.asarray(kw["L0_ffn_down_w"], np.int32)))
    gs = np.asarray(kw["L0_ffn_gate_scale"], f32)
    us = np.asarray(kw["L0_ffn_up_scale"], f32)
    ds = np.asarray(kw["L0_ffn_down_scale"], f32)
    w1qkv = np.asarray(kw["L1_attn_attn_qkvw"], f32)
    qkvb = np.asarray(kw["L1_attn_attn_qkvb"], f32)
    w1o = np.asarray(kw["L1_attn_attn_ow"], f32)
    ob = np.asarray(kw["L1_attn_attn_ob"], f32)
    wg1 = np.asarray(kw["L1_ffn_w_gate"], f32)
    wu1 = np.asarray(kw["L1_ffn_w_up"], f32)
    wd1 = np.asarray(kw["L1_ffn_w_down"], f32)
    gam = np.asarray(kw["L1_ffn_ln_weight"], f32)

    shared = {
        "w0qk": _tile_lhsT(w0qkv[:, 0:4096]),
        "w0v": np.ascontiguousarray(w0qkv[:, 4096:6144]),
        "w0o": _tile_lhsT(w0o),
        "w0g": _tile_lhsT(np.ascontiguousarray(g0.T)),
        "w0u": _tile_lhsT(np.ascontiguousarray(u0.T)),
        "w0d": _tile_lhsT(np.ascontiguousarray(d0.T)),
        "gs_col": _padcol(np.ascontiguousarray(gs.reshape(64, 128).T)),
        "us_col": _padcol(np.ascontiguousarray(us.reshape(64, 128).T)),
        "ds_col": _padcol(np.ascontiguousarray(ds.reshape(16, 128).T)),
        "w1qk": _tile_lhsT(w1qkv[:, 0:4096]),
        "w1v": np.ascontiguousarray(w1qkv[:, 4096:6144]),
        "w1o": _tile_lhsT(w1o),
        "w1g": _tile_lhsT(np.ascontiguousarray(wg1.T)),
        "w1u": _tile_lhsT(np.ascontiguousarray(wu1.T)),
        "w1d": _tile_lhsT(np.ascontiguousarray(wd1.T)),
        "qkvb_col": _padcol(np.ascontiguousarray(qkvb[0:4096].reshape(32, 128).T)),
        "bv_row": np.ascontiguousarray(qkvb[4096:6144].reshape(1, 2048)),
        "ob_col": _padcol(np.ascontiguousarray(ob.reshape(16, 128).T)),
        "gam_col": _padcol(np.ascontiguousarray(gam.reshape(64, 128).T)),
        "ident_in": np.eye(128, dtype=f32),
        "dummy_d": np.zeros((1, 64), f32),
    }
    in_maps = []
    for c in range(8):
        b, half = c // 2, c % 2
        qbase = half * TPC
        m = dict(shared)
        m["x0"] = np.ascontiguousarray(x[b, qbase:qbase + TPC, :].T)
        msk = np.zeros((32, 128, T), dtype=ml_dtypes.bfloat16)
        for t in range(NT):
            for gkb in range(16):
                kpos = gkb * 128 + np.arange(128)[:, None]
                qpos = qbase + t * T + np.arange(T)[None, :]
                msk[t * 16 + gkb] = (kpos <= qpos).astype(ml_dtypes.bfloat16)
        m["mask_d"] = msk
        in_maps.append(m)
    return in_maps


_CACHE = {}


def kernel(**kw):
    if "nc" not in _CACHE:
        _CACHE["nc"] = build()[0]
    nc = _CACHE["nc"]
    in_maps = make_inputs(kw)
    res = run_bass_kernel_spmd(nc, in_maps, core_ids=list(range(8)))
    outp = np.zeros((B, S, D), np.float32)
    for c in range(8):
        b, half = c // 2, c % 2
        outp[b, half * TPC:(half + 1) * TPC, :] = res.results[c]["out"].T
    return outp


# version 06
# speedup vs baseline: 1.1911x; 1.0626x over previous
import numpy as np
import ml_dtypes
from contextlib import ExitStack

import concourse.bass as bass
import concourse.mybir as mybir
from concourse.bass_utils import run_bass_kernel_spmd

F32 = mybir.dt.float32
BF16 = mybir.dt.bfloat16
U8 = mybir.dt.uint8
I8 = mybir.dt.int8
AF = mybir.ActivationFunctionType
ALU = mybir.AluOpType
AX = mybir.AxisListType

B, S, D = 4, 2048, 2048
II = 8192
NH = 16
TPC = 1024
T = 512
NT = TPC // T
DC = D // 128
IC = II // 128
SCALE = 1.0 / float(np.sqrt(128.0))
RG = [[0, 1], [2, 3], [4, 5], [6, 7]]
TCH = 1024


class Sem:
    def __init__(self, es, nc, name):
        self.h = es.enter_context(nc.semaphore(name))
        self.n = 0

    def inc(self, ins, k=1):
        ins.then_inc(self.h, k)
        self.n += k
        return self.n


def build(n_layers=2, l0_out=False, stage="full"):
    nc = bass.Bass("TRN2", target_bir_lowering=False, debug=False,
                   detect_race_conditions=False)

    def din(name, shape, dt):
        return nc.dram_tensor(name, list(shape), dt, kind="ExternalInput").ap()

    def dint(name, shape, dt):
        return nc.dram_tensor(name, list(shape), dt, kind="Internal").ap()

    x0 = din("x0", [D, TPC], F32)
    w0qk = din("w0qk", [32, 128, D], F32)
    w0v = din("w0v", [D, D], F32)
    w0o = din("w0o", [16, 128, D], F32)
    w0g = din("w0g", [64, 128, D], U8)
    w0u = din("w0u", [64, 128, D], U8)
    w0d = din("w0d", [16, 128, II], U8)
    gs_col = din("gs_col", [128, 65], F32)
    us_col = din("us_col", [128, 65], F32)
    ds_col = din("ds_col", [128, 17], F32)
    w1qk = din("w1qk", [32, 128, D], F32)
    w1v = din("w1v", [D, D], F32)
    w1o = din("w1o", [16, 128, D], F32)
    w1g = din("w1g", [64, 128, D], F32)
    w1u = din("w1u", [64, 128, D], F32)
    w1d = din("w1d", [16, 128, II], F32)
    qkvb_col = din("qkvb_col", [128, 33], F32)
    bv_row = din("bv_row", [1, D], F32)
    ob_col = din("ob_col", [128, 17], F32)
    gam_col = din("gam_col", [128, 65], F32)
    mask_d = din("mask_d", [32, 128, T], BF16)
    ident_in = din("ident_in", [128, 128], F32)

    out = nc.dram_tensor("out", [D, TPC], F32, kind="ExternalOutput").ap()
    xres = dint("xres", [D, TPC], F32)
    cck_i = dint("cck_i", [D + 16, TPC], BF16)
    ccv_i = dint("ccv_i", [TPC, D], BF16)
    cck_o = dint("cck_o", [2 * (D + 16), TPC], BF16)
    ccv_o = dint("ccv_o", [2 * TPC, D], BF16)
    tg_d = dint("tg_d", [64, 128, D], I8)
    tu_d = dint("tu_d", [64, 128, D], I8)
    td_d = dint("td_d", [16, 128, II], I8)
    rowspill = dint("rowspill", [64, T], F32)
    dummy_d = dint("dummy_d", [1, 64], F32)

    es = ExitStack()
    with es:
        def sbt(name, shape, dt):
            return es.enter_context(nc.sbuf_tensor(name, list(shape), dt))

        xn = [sbt(f"xn{t}", [128, DC * T], BF16) for t in range(NT)]
        qg = sbt("qg", [128, DC * T], BF16)
        har = sbt("har", [128, IC * T], BF16)
        war = sbt("war", [128, 4 * D], BF16)
        wvb = sbt("wvb", [128, 2 * T], BF16)
        kbuf = sbt("kbuf", [128, 2 * TPC], BF16)
        vbuf = sbt("vbuf", [128, 2 * TPC], BF16)
        pbuf = sbt("pbuf", [128, 4 * T], BF16)
        qhb = sbt("qhb", [128, 2 * T], BF16)
        bct = sbt("bct", [128, 4 * T], F32)
        xck = sbt("xck", [128, 2 * T], F32)
        ftmp = sbt("ftmp", [128, 2 * T], F32)
        mrun = sbt("mrun", [128, 2 * T], F32)
        mcols = sbt("mcols", [128, 48], F32)
        bfs = sbt("bfs", [128, 3 * T], BF16)
        kvst = sbt("kvst", [128, 2 * T], BF16)
        terni = sbt("terni", [128, TCH], F32)
        ternm = sbt("ternm", [128, TCH], F32)
        terno = sbt("terno", [128, TCH], I8)
        acc = sbt("acc", [128, 256], F32)
        qi8 = sbt("qi8", [128, 2 * T], I8)
        mbuf = sbt("mbuf", [128, 2 * T], BF16)
        ident = sbt("ident", [128, 128], F32)
        gsc = sbt("gsc", [128, 65], F32)
        usc = sbt("usc", [128, 65], F32)
        dsc = sbt("dsc", [128, 17], F32)
        gam = sbt("gam", [128, 65], F32)
        qkvb = sbt("qkvb", [128, 33], F32)
        obc = sbt("obc", [128, 17], F32)
        bvr = sbt("bvr", [1, D], BF16)
        onesrow_f = sbt("onesrow_f", [1, 128], F32)
        onesrow_bf = sbt("onesrow_bf", [1, 128], BF16)
        m128row = sbt("m128row", [1, 128], BF16)
        onescol_f = sbt("onescol_f", [128, 1], F32)
        onescol_bf = sbt("onescol_bf", [128, 1], BF16)
        biascol = sbt("biascol", [128, 1], F32)
        sccol = sbt("sccol", [128, 1], F32)
        swsp = [sbt(f"swsp{i}", [128, 1], F32) for i in range(3)]
        rows = [sbt(f"row{i}", [1, T], F32) for i in range(7)]
        srow_bf = [sbt(f"rbf{i}", [1, T], BF16) for i in range(2)]
        ccm_sb = sbt("ccm_sb", [1, 64], BF16)
        knm = sbt("knm", [1, 32], F32)
        sc1 = sbt("sc1", [1, 8], F32)
        wmean = sbt("wmean", [1, 8], F32)
        wrecip = [sbt(f"wrecip{i}", [1, 1], F32) for i in range(3)]
        dummy_sb = sbt("dummy_sb", [1, 64], F32)

        sbuf_left = nc.sbuf_bytes_remaining

        def psm(name, shape):
            return es.enter_context(nc.psum_tensor(name, list(shape), F32))

        class _PV:
            def __init__(self, t, c0):
                self.t, self.c0 = t, c0

            def __getitem__(self, ix):
                if isinstance(ix, tuple):
                    rs, cs = ix
                    a = cs.start or 0
                    b = T if cs.stop is None else cs.stop
                    return self.t[rs, self.c0 + a:self.c0 + b]
                return self.t[:, self.c0:self.c0 + T]

        pws = psm("pws", [128, 2 * T])
        pw = [_PV(pws, 0), _PV(pws, T)] + [psm(f"pw{i}", [128, T])
                                           for i in range(2, 6)]
        pr0 = psm("pr0", [1, T])
        pr1 = psm("pr1", [1, T])

        s_cc = Sem(es, nc, "s_cc")
        s_t = Sem(es, nc, "s_t")
        s_v = Sem(es, nc, "s_v")
        s_s = Sem(es, nc, "s_s")
        s_mk = Sem(es, nc, "s_mk")
        sP = Sem(es, nc, "sP")
        sP2 = Sem(es, nc, "sP2")
        sX = [Sem(es, nc, f"sX{i}") for i in range(2)]
        sT = Sem(es, nc, "sT")
        sM = [Sem(es, nc, f"sM{i}") for i in range(2)]
        sKB = [Sem(es, nc, f"sKB{i}") for i in range(2)]
        sVB = [Sem(es, nc, f"sVB{i}") for i in range(2)]
        sCM = Sem(es, nc, "sCM")
        sU = Sem(es, nc, "sU")
        sKS = [Sem(es, nc, f"sKS{i}") for i in range(2)]
        sVS = [Sem(es, nc, f"sVS{i}") for i in range(2)]
        sUP = [Sem(es, nc, f"sUP{i}") for i in range(2)]
        sSP = Sem(es, nc, "sSP")
        sTS = Sem(es, nc, "sTS")
        sKN = Sem(es, nc, "sKN")
        sW = [Sem(es, nc, f"sW{i}") for i in range(4)]
        sWV = [Sem(es, nc, f"sWV{i}") for i in range(2)]

        blk = es.enter_context(nc.Block())

        def TEN(f): blk.tensor(f)
        def VEC(f): blk.vector(f)
        def SCA(f): blk.scalar(f)
        def SYN(f): blk.sync(f)
        def GPS(f): blk.gpsimd(f)

        def marker(deps):
            vals = [(d.h, d.n) for d in deps]

            def _(sy):
                for h, v in vals:
                    sy.wait_ge(h, v)
                s_mk.inc(sy.dma_start(out=dummy_sb[:], in_=dummy_d), 16)
            SYN(_)
            return s_mk.n

        def barrier(allow_safe=True):
            tv, vv, sv = s_t.n, s_v.n, s_s.n

            def _(tn):
                tn.wait_ge(s_v.h, vv)
                tn.wait_ge(s_s.h, sv)
            TEN(_)

            def _(v):
                v.wait_ge(s_t.h, tv)
                v.wait_ge(s_s.h, sv)
            VEC(_)

            def _(sc):
                sc.wait_ge(s_t.h, tv)
                sc.wait_ge(s_v.h, vv)
            SCA(_)
            if allow_safe and tern_pos[0] < len(tern_steps) \
                    and tern_steps[tern_pos[0]][0] == "safe":
                pump(4, safe=True)

        def _(sy):
            sP.inc(sy.dma_start(out=gsc[:], in_=gs_col), 16)
            sP.inc(sy.dma_start(out=usc[:], in_=us_col), 16)
            sP.inc(sy.dma_start(out=dsc[:], in_=ds_col), 16)
            sP.inc(sy.dma_start(out=gam[:], in_=gam_col), 16)
            sP.inc(sy.dma_start(out=qkvb[:], in_=qkvb_col), 16)
            sP.inc(sy.dma_start(out=obc[:], in_=ob_col), 16)
            sP.inc(sy.dma_start(out=ident[:], in_=ident_in), 16)
        SYN(_)
        setup_sy = sP.n

        def _(g):
            sP2.inc(g.dma_start(out=bvr[:], in_=bv_row), 16)
        GPS(_)
        setup_gp = sP2.n

        def _(v):
            s_v.inc(v.memset(kvst[:], 0.0))
        VEC(_)
        zv = s_v.n

        def _(sy):
            sy.wait_ge(s_v.h, zv)
            for rr in range(16):
                sP.inc(sy.dma_start(out=cck_i[D + rr:D + rr + 1, :],
                                    in_=kvst[0:1, :]), 16)
        SYN(_)

        def _(v):
            v.wait_ge(sP.h, setup_sy)
            v.memset(onesrow_f[:], 1.0)
            v.memset(onesrow_bf[:], 1.0)
            v.memset(m128row[:], -128.0)
            v.memset(onescol_f[:], 1.0)
            v.memset(sccol[:], SCALE)
            s_v.inc(v.memset(onescol_bf[:], 1.0))
        VEC(_)

        tern_steps = []
        tern_abs_cnt = {}
        tern_absred = {}
        tern_p2_cnt = {}
        tern_st_cnt = {}
        tern_marker = {}
        tern_state = {}

        def build_tern():
            if n_layers < 2:
                return
            TC2 = TCH // 2
            specs = [(0, w1g, tg_d), (1, w1u, tu_d), (2, w1d, td_d)]
            for wi, wsrc, wdst in specs:
                Mo, K = wsrc.shape[0], wsrc.shape[2]
                ncols = K // TC2
                nch = Mo * ncols
                NEL = float(Mo * 128 * K)

                def mean_chunk(ci, wi=wi, wsrc=wsrc, ncols=ncols, nch=nch):
                    mo, co = divmod(ci, ncols)
                    bq = ci % 4
                    buf = terni if bq < 2 else ternm
                    hf = bq % 2
                    bsl = buf[:, hf * TC2:(hf + 1) * TC2]
                    my = [0]

                    def _(sy):
                        if ci >= 4:
                            sy.wait_ge(s_v.h, tern_absred[(wi, ci - 4)])
                        elif wi >= 1:
                            sy.wait_ge(s_v.h, tern_p2_cnt[(wi - 1, 256 - 1)])
                        my[0] = sT.inc(sy.dma_start(
                            out=bsl,
                            in_=wsrc[mo, :, co * TC2:(co + 1) * TC2]), 16)
                    SYN(_)

                    def _(v):
                        v.wait_ge(sT.h, my[0])
                        tern_absred[(wi, ci)] = s_v.inc(v.reduce_sum(
                            out=acc[:, ci:ci + 1], in_=bsl,
                            axis=AX.X, apply_absolute_value=True))
                    VEC(_)
                for ci in range(nch):
                    tern_steps.append(("any", wi, lambda ci=ci, f=mean_chunk: f(ci)))

                def finish(wi=wi, nch=nch, NEL=NEL):
                    need_v = tern_absred[(wi, nch - 1)]
                    vpre, tpre = s_v.n, s_t.n

                    def _(v):
                        v.wait_ge(s_v.h, need_v)
                        v.wait_ge(s_t.h, tpre)
                        c = s_v.inc(v.reduce_sum(out=mcols[:, 6:7], in_=acc[:, 0:nch],
                                                 axis=AX.X))
                        v.wait_ge(s_v.h, c)
                        s_v.inc(v.tensor_copy(out=mcols[:, 7:8], in_=mcols[:, 6:7]))
                    VEC(_)
                    v1 = s_v.n

                    def _(tn):
                        tn.wait_ge(s_v.h, v1)
                        s_t.inc(tn.matmul(pr0[0:1, 0:1], onescol_f[:], mcols[:, 7:8],
                                          start=True, stop=True))
                    TEN(_)
                    t1 = s_t.n

                    def _(v):
                        v.wait_ge(s_t.h, t1)
                        v.tensor_scalar(out=wmean[0:1, wi + 1:wi + 2],
                                        in0=pr0[0:1, 0:1],
                                        scalar1=1.0 / (NEL * 127.0),
                                        scalar2=1e-5 / 127.0,
                                        op0=ALU.mult, op1=ALU.max)
                        c = s_v.inc(v.tensor_scalar(out=sc1[0:1, 5:6],
                                                    in0=pr0[0:1, 0:1],
                                                    scalar1=1.0 / NEL, scalar2=1e-5,
                                                    op0=ALU.mult, op1=ALU.max))
                        v.wait_ge(s_v.h, c)
                        s_v.inc(v.reciprocal(out=wrecip[wi][:], in_=sc1[0:1, 5:6]))
                    VEC(_)
                    v2 = s_v.n

                    def _(tn):
                        tn.wait_ge(s_v.h, v2)
                        s_t.inc(tn.matmul(pw[5][:, 0:1], onesrow_f[:], wrecip[wi][:],
                                          start=True, stop=True))
                    TEN(_)
                    t2 = s_t.n

                    def _(v):
                        v.wait_ge(s_t.h, t2)
                        s_v.inc(v.tensor_copy(out=swsp[wi][:], in_=pw[5][:, 0:1]))
                    VEC(_)
                tern_steps.append(("safe", wi, finish))

                def tern_chunk(ci, wi=wi, wsrc=wsrc, wdst=wdst, ncols=ncols, nch=nch):
                    mo, co = divmod(ci, ncols)
                    hf = ci % 2
                    sl = slice(hf * TC2, (hf + 1) * TC2)
                    my = [0]

                    def _(sy):
                        if ci >= 2:
                            sy.wait_ge(s_v.h, tern_p2_cnt[(wi, ci - 2)])
                        else:
                            sy.wait_ge(s_v.h, tern_absred[(wi, nch - 1)])
                        my[0] = sT.inc(sy.dma_start(
                            out=terni[:, sl],
                            in_=wsrc[mo, :, co * TC2:(co + 1) * TC2]), 16)
                    SYN(_)

                    def _(v):
                        v.wait_ge(sT.h, my[0])
                        if ci >= 2:
                            v.wait_ge(sTS.h, tern_st_cnt[(wi, ci - 2)])
                        v.tensor_scalar(out=ternm[:, sl],
                                        in0=terni[:, sl],
                                        scalar1=swsp[wi][:], scalar2=None, op0=ALU.mult)
                        tern_p2_cnt[(wi, ci)] = s_v.inc(v.tensor_scalar(
                            out=terno[:, sl], in0=ternm[:, sl],
                            scalar1=-1.4375, scalar2=1.4375, op0=ALU.max, op1=ALU.min))
                    VEC(_)
                    v1 = s_v.n

                    def _(sy):
                        sy.wait_ge(s_v.h, v1)
                        tern_st_cnt[(wi, ci)] = sTS.inc(sy.dma_start(
                            out=wdst[mo, :, co * TC2:(co + 1) * TC2],
                            in_=terno[:, sl]), 16)
                    SYN(_)
                for ci in range(nch):
                    tern_steps.append(("any", wi, lambda ci=ci, f=tern_chunk: f(ci)))

                def endmark(wi=wi):
                    tern_marker[wi] = marker([sTS])
                tern_steps.append(("any", wi, endmark))

        build_tern()
        tern_pos = [0]

        def pump(n, safe=False):
            k = 0
            while k < n and tern_pos[0] < len(tern_steps):
                tag, swi, fn = tern_steps[tern_pos[0]]
                if tag == "safe" and not safe:
                    break
                fn()
                tern_pos[0] += 1
                k += 1

        def pump_until(wmax):
            while tern_pos[0] < len(tern_steps) \
                    and tern_steps[tern_pos[0]][1] <= wmax:
                tern_steps[tern_pos[0]][2]()
                tern_pos[0] += 1

        def pump_flush():
            pump(len(tern_steps), safe=True)

        spill_idx = {}
        spill_mk = {}

        def spill(row_ap, key, v_cnt):
            idx = spill_idx.setdefault(key, len(spill_idx))
            assert idx < 64

            def _(sy):
                sy.wait_ge(s_v.h, v_cnt)
                sSP.inc(sy.dma_start(out=rowspill[idx:idx + 1, :], in_=row_ap), 16)
            SYN(_)
            spill_mk[key] = None

        def unspill(row_ap, key):
            idx = spill_idx[key]
            pre = sSP.n
            res = [0]

            def _(sy):
                sy.wait_ge(sSP.h, pre)
                res[0] = sU.inc(sy.dma_start(out=row_ap, in_=rowspill[idx:idx + 1, :]), 16)
            SYN(_)
            return res[0]

        war_free = [0, 0, 0, 0]
        warslot = [0]
        kst_gate = [0, 0]
        vst_gate = [0, 0]
        wv_free = [0, 0]
        qh_free = [0, 0]
        exp_free = [0, 0]
        pv_free = [0, 0, 0]
        mask_free = [0, 0]
        kv_free = [0, 0]
        upd_free = [0, 0]
        layer_e_marker = [0, 0]

        def load_war(src_ap):
            slot = warslot[0]
            warslot[0] = (slot + 1) % 4
            free = war_free[slot]
            cnt = [0]

            def _(g):
                if free:
                    g.wait_ge(s_t.h, free)
                cnt[0] = sW[slot].inc(g.dma_start(out=war[:, slot * D:(slot + 1) * D],
                                                  in_=src_ap), 16)
            GPS(_)
            return slot, cnt[0]

        def wsl(slot, dm):
            return war[:, slot * D + dm * 128: slot * D + (dm + 1) * 128]

        xsrcs = [x0, xres]
        xdsts = [xres, out]
        if n_layers == 1 or l0_out:
            xdsts[0] = out

        def emit_layer(l):
            L1 = (l == 1)
            xsrc, xdst = xsrcs[l], xdsts[l]
            wqk = w1qk if L1 else w0qk
            wvp = w1v if L1 else w0v
            wop = w1o if L1 else w0o
            wg_t = tg_d if L1 else w0g
            wu_t = tu_d if L1 else w0u
            wd_t = td_d if L1 else w0d

            if L1:
                barrier()
                xmark = layer_e_marker[0]

                def _(sy):
                    sy.wait_ge(s_mk.h, xmark)
                SYN(_)

            def _(v):
                s_v.inc(v.memset(knm[0:1, 0:32], 0.0))
            VEC(_)

            for t in range(NT):
                barrier()
                for f in range(DC):
                    slot = f % 2
                    my = [0]
                    tpre = s_t.n

                    def _(sy, f=f, slot=slot, my=my, tpre=tpre):
                        sy.wait_ge(s_t.h, tpre)
                        my[0] = sX[slot].inc(sy.dma_start(
                            out=xck[:, slot * T:(slot + 1) * T],
                            in_=xsrc[f * 128:(f + 1) * 128, t * T:(t + 1) * T]), 16)
                    SYN(_)

                    def _(sc, slot=slot, my=my):
                        sc.wait_ge(sX[slot].h, my[0])
                        s_s.inc(sc.activation(out=ftmp[:, slot * T:(slot + 1) * T],
                                              in_=xck[:, slot * T:(slot + 1) * T],
                                              func=AF.Square))
                    SCA(_)
                    mys = s_s.n

                    def _(tn, f=f, slot=slot, my=my, mys=mys):
                        tn.wait_ge(sX[slot].h, my[0])
                        tn.matmul(pr0[:], onescol_f[:], xck[:, slot * T:(slot + 1) * T],
                                  start=(f == 0), stop=(f == DC - 1),
                                  skip_group_check=True)
                        tn.wait_ge(s_s.h, mys)
                        s_t.inc(tn.matmul(pr1[:], onescol_f[:],
                                          ftmp[:, slot * T:(slot + 1) * T],
                                          start=(f == 0), stop=(f == DC - 1),
                                          skip_group_check=True))
                    TEN(_)
                stats_t = s_t.n

                def _(v, stats_t=stats_t):
                    v.wait_ge(s_t.h, stats_t)
                    v.tensor_scalar(out=rows[0][:], in0=pr0[:], scalar1=1.0 / D,
                                    scalar2=None, op0=ALU.mult)
                    v.tensor_mul(out=rows[2][:], in0=rows[0][:], in1=rows[0][:])
                    s_v.inc(v.scalar_tensor_tensor(out=rows[2][:], in0=pr1[:],
                                                   scalar=1.0 / D, in1=rows[2][:],
                                                   op0=ALU.mult, op1=ALU.subtract))
                VEC(_)
                v1 = s_v.n

                def _(v):
                    s_v.inc(v.tensor_scalar(out=rows[3][:], in0=rows[2][:],
                                            scalar1=1e-5, scalar2=None, op0=ALU.add))
                VEC(_)
                v1b = s_v.n

                def _(sc, v1b=v1b):
                    sc.wait_ge(s_v.h, v1b)
                    s_s.inc(sc.activation(out=rows[2][:], in_=rows[3][:], func=AF.Sqrt))
                SCA(_)
                s1 = s_s.n

                def _(v, s1=s1):
                    v.wait_ge(s_s.h, s1)
                    s_v.inc(v.reciprocal(out=rows[1][:], in_=rows[2][:]))
                VEC(_)
                vr = s_v.n

                def _(tn, vr=vr):
                    tn.wait_ge(s_v.h, vr)
                    tn.matmul(pw[4][:], onesrow_f[:], rows[0][:], start=True, stop=True)
                    s_t.inc(tn.matmul(pw[5][:], onesrow_f[:], rows[1][:],
                                      start=True, stop=True))
                TEN(_)
                t1 = s_t.n

                def _(v, t1=t1):
                    v.wait_ge(s_t.h, t1)
                    v.tensor_copy(out=bct[:, 0:T], in_=pw[4][:])
                    s_v.inc(v.tensor_copy(out=bct[:, T:2 * T], in_=pw[5][:]))
                VEC(_)
                bc_ready = s_v.n
                spill(rows[0][:], ("m", l, t), bc_ready)

                if L1:
                    def _(v, bc_ready=bc_ready):
                        v.wait_ge(s_v.h, bc_ready)
                        v.memset(mrun[:, 0:T], -1e30)
                        s_v.inc(v.memset(mrun[:, T:2 * T], 1e30))
                    VEC(_)
                for f in range(DC):
                    slot = f % 2
                    my = [0]
                    vpre = s_v.n

                    def _(sy, f=f, slot=slot, my=my, vpre=vpre):
                        sy.wait_ge(s_v.h, vpre)
                        my[0] = sX[slot].inc(sy.dma_start(
                            out=xck[:, slot * T:(slot + 1) * T],
                            in_=xsrc[f * 128:(f + 1) * 128, t * T:(t + 1) * T]), 16)
                    SYN(_)

                    def _(v, f=f, slot=slot, my=my, t=t, bc_ready=bc_ready):
                        v.wait_ge(sX[slot].h, my[0])
                        v.wait_ge(s_v.h, bc_ready)
                        v.tensor_sub(out=ftmp[:, slot * T:(slot + 1) * T],
                                     in0=xck[:, slot * T:(slot + 1) * T],
                                     in1=bct[:, 0:T])
                        ins = v.tensor_mul(out=xn[t][:, f * T:(f + 1) * T],
                                           in0=ftmp[:, slot * T:(slot + 1) * T],
                                           in1=bct[:, T:2 * T])
                        if L1:
                            v.tensor_tensor(out=mrun[:, 0:T],
                                            in0=ftmp[:, slot * T:(slot + 1) * T],
                                            in1=mrun[:, 0:T], op=ALU.max)
                            ins = v.tensor_tensor(out=mrun[:, T:2 * T],
                                                  in0=ftmp[:, slot * T:(slot + 1) * T],
                                                  in1=mrun[:, T:2 * T], op=ALU.min)
                        s_v.inc(ins)
                    VEC(_)
                xn_ready = s_v.n
                if stage == "LN" + str(t):
                    return

                if L1:
                    def _(v, xn_ready=xn_ready):
                        v.wait_ge(s_v.h, xn_ready)
                        v.tensor_scalar(out=mrun[:, T:2 * T], in0=mrun[:, T:2 * T],
                                        scalar1=-1.0, scalar2=None, op0=ALU.mult)
                        s_v.inc(v.tensor_tensor(out=mrun[:, 0:T], in0=mrun[:, 0:T],
                                                in1=mrun[:, T:2 * T], op=ALU.max))
                    VEC(_)
                    mv = s_v.n
                    for tb in range(4):
                        def _(tn, tb=tb, mv=mv):
                            tn.wait_ge(s_v.h, mv)
                            s_t.inc(tn.transpose(pw[tb][:, 0:128],
                                                 mrun[:, tb * 128:(tb + 1) * 128],
                                                 ident[:]))
                        TEN(_)
                    t2 = s_t.n
                    mcbase = mcol_ctr[0]
                    mcol_ctr[0] += 4

                    def _(v, t2=t2, mcbase=mcbase):
                        v.wait_ge(s_t.h, t2)
                        for tb in range(4):
                            ins = v.reduce_max(out=mcols[:, mcbase + tb:mcbase + tb + 1],
                                               in_=pw[tb][:, 0:128], axis=AX.X)
                        s_v.inc(ins)
                    VEC(_)
                    v2 = s_v.n
                    for tb in range(4):
                        def _(tn, tb=tb, v2=v2, mcbase=mcbase):
                            tn.wait_ge(s_v.h, v2)
                            s_t.inc(tn.matmul(pr0[0:1, tb * 128:(tb + 1) * 128],
                                              mcols[:, mcbase + tb:mcbase + tb + 1],
                                              ident[:], start=True, stop=True,
                                              skip_group_check=True))
                        TEN(_)
                    t3 = s_t.n

                    def _(v, t3=t3):
                        v.wait_ge(s_t.h, t3)
                        v.tensor_copy(out=rows[3][:], in_=pr0[:])
                        v.tensor_mul(out=rows[4][:], in0=rows[3][:], in1=rows[1][:])
                        v.tensor_scalar(out=rows[4][:], in0=rows[4][:], scalar1=1e-5,
                                        scalar2=None, op0=ALU.max)
                        v.reciprocal(out=rows[5][:], in_=rows[4][:])
                        s_v.inc(v.scalar_tensor_tensor(out=rows[5][:], in0=rows[5][:],
                                                       scalar=127.0, in1=rows[1][:],
                                                       op0=ALU.mult, op1=ALU.mult))
                    VEC(_)
                    v3 = s_v.n
                    spill(rows[4][:], ("mx", l, t), v3)
                    spill(rows[5][:], ("qscT", l, t), v3)
                else:
                    for f in range(DC):
                        def _(tn, f=f, t=t, xn_ready=xn_ready):
                            tn.wait_ge(s_v.h, xn_ready)
                            s_t.inc(tn.matmul(pr1[:], onescol_bf[:],
                                              xn[t][:, f * T:(f + 1) * T],
                                              start=(f == 0), stop=(f == DC - 1),
                                              skip_group_check=True))
                        TEN(_)
                    t2 = s_t.n

                    def _(v, t2=t2, t_=t):
                        v.wait_ge(s_t.h, t2)
                        s_v.inc(v.tensor_copy(out=srow_bf[t_][:], in_=pr1[:]))
                    VEC(_)

                for hmo in range(16):
                    slot, gcnt = load_war(wqk[16 + hmo])
                    vpre = s_v.n

                    def _(tn, slot=slot, gcnt=gcnt, t=t, xn_ready=xn_ready, vpre=vpre):
                        tn.wait_ge(sW[slot].h, gcnt)
                        tn.wait_ge(s_v.h, vpre)
                        for dm in range(DC):
                            ins = tn.matmul(pw[0][:], wsl(slot, dm),
                                            xn[t][:, dm * T:(dm + 1) * T],
                                            start=(dm == 0), stop=(dm == DC - 1),
                                            skip_group_check=True)
                        s_t.inc(ins)
                    TEN(_)
                    war_free[slot] = s_t.n
                    t1 = s_t.n
                    ks = hmo % 2

                    def _(v, hmo=hmo, t1=t1, ks=ks):
                        v.wait_ge(s_t.h, t1)
                        v.wait_ge(sKS[ks].h, kst_gate[ks])
                        v.wait_ge(sVS[ks].h, vst_gate[ks])
                        b = qkvb[:, 17 + hmo:18 + hmo] if L1 else 0.0
                        v.tensor_scalar(out=kvst[:, ks * T:(ks + 1) * T],
                                        in0=pw[0][:], scalar1=b, scalar2=None,
                                        op0=ALU.add)
                        s_v.inc(v.tensor_mul(out=bfs[:, 0:T],
                                             in0=kvst[:, ks * T:(ks + 1) * T],
                                             in1=kvst[:, ks * T:(ks + 1) * T]))
                    VEC(_)
                    s2 = s_v.n
                    v2 = s_v.n

                    def _(tn, v2=v2):
                        tn.wait_ge(s_v.h, v2)
                        s_t.inc(tn.matmul(pr1[:], onescol_bf[:], bfs[:, 0:T],
                                          start=True, stop=True, skip_group_check=True))
                    TEN(_)
                    t3 = s_t.n

                    def _(v, hmo=hmo, t3=t3):
                        v.wait_ge(s_t.h, t3)
                        c = s_v.inc(v.reduce_max(out=sc1[0:1, 4:5], in_=pr1[:],
                                                 axis=AX.X))
                        v.wait_ge(s_v.h, c)
                        s_v.inc(v.tensor_tensor(out=knm[0:1, hmo:hmo + 1],
                                                in0=knm[0:1, hmo:hmo + 1],
                                                in1=sc1[0:1, 4:5], op=ALU.max))
                    VEC(_)

                    def _(sy, hmo=hmo, s2=s2, ks=ks, t=t):
                        sy.wait_ge(s_v.h, s2)
                        kst_gate[ks] = sKS[ks].inc(sy.dma_start(
                            out=cck_i[hmo * 128:(hmo + 1) * 128, t * T:(t + 1) * T],
                            in_=kvst[:, ks * T:(ks + 1) * T]), 16)
                    SYN(_)
                    pump(8)
                if stage == "K" + str(t):
                    return

                prev_vevac = [s_v.n]
                for cg in range(4):
                    for dm in range(DC):
                        slot = dm % 2
                        my = [0]
                        free = wv_free[slot]

                        def _(g, dm=dm, cg=cg, slot=slot, my=my, free=free):
                            if free:
                                g.wait_ge(s_t.h, free)
                            my[0] = sWV[slot].inc(g.dma_start(
                                out=wvb[:, slot * T:(slot + 1) * T],
                                in_=wvp[dm * 128:(dm + 1) * 128, cg * T:(cg + 1) * T]), 16)
                        GPS(_)

                        def _(tn, dm=dm, slot=slot, my=my, t=t, xn_ready=xn_ready,
                              pe=prev_vevac[0]):
                            tn.wait_ge(sWV[slot].h, my[0])
                            if dm == 0:
                                tn.wait_ge(s_v.h, max(xn_ready, pe))
                            last = (dm == DC - 1) and (not L1)
                            for tb in range(4):
                                ins = tn.matmul(
                                    pw[tb][:],
                                    xn[t][:, dm * T + tb * 128: dm * T + (tb + 1) * 128],
                                    wvb[:, slot * T:(slot + 1) * T],
                                    start=(dm == 0), stop=last, skip_group_check=True)
                            s_t.inc(ins)
                        TEN(_)
                        wv_free[slot] = s_t.n
                    if L1:
                        def _(tn, cg=cg):
                            tn.wait_ge(sP2.h, setup_gp)
                            for tb in range(4):
                                ins = tn.matmul(pw[tb][:], onesrow_bf[:],
                                                bvr[0:1, cg * T:(cg + 1) * T],
                                                start=False, stop=True,
                                                skip_group_check=True)
                            s_t.inc(ins)
                        TEN(_)
                    t4 = s_t.n
                    for tb in range(4):
                        vs = tb % 2
                        if "x" in stage:
                            continue

                        def _(v, tb=tb, t4=t4, vs=vs):
                            v.wait_ge(s_t.h, t4)
                            v.wait_ge(sVS[vs].h, vst_gate[vs])
                            v.wait_ge(sKS[vs].h, kst_gate[vs])
                            s_v.inc(v.tensor_copy(out=kvst[:, vs * T:(vs + 1) * T],
                                                  in_=pw[tb][:]))
                        VEC(_)
                        v4 = s_v.n
                        if "y" in stage:
                            continue

                        def _(sy, tb=tb, cg=cg, v4=v4, vs=vs, t=t):
                            sy.wait_ge(s_v.h, v4)
                            vst_gate[vs] = sVS[vs].inc(sy.dma_start(
                                out=ccv_i[t * T + tb * 128: t * T + (tb + 1) * 128,
                                          cg * T:(cg + 1) * T],
                                in_=kvst[:, vs * T:(vs + 1) * T]), 16)
                        SYN(_)
                    prev_vevac[0] = s_v.n
                    pump(8)
                if stage.startswith("V") and stage.endswith(str(t)):
                    return

            vpre = s_v.n

            def _(g, vpre=vpre):
                g.wait_ge(s_v.h, vpre)
                sKN.inc(g.dma_start(out=cck_i[D:D + 1, 0:32], in_=knm[0:1, 0:32]), 16)
            GPS(_)
            ccmark = marker([sKS[0], sKS[1], sVS[0], sVS[1], sKN, sP])

            def _(g, ccmark=ccmark, l=l):
                g.wait_ge(s_mk.h, ccmark)
                if l == 1:
                    for sdep in (sKB[0], sKB[1], sVB[0], sVB[1]):
                        g.wait_ge(sdep.h, sdep.n)
                s_cc.inc(g.collective_compute("AllGather", ALU.bypass, ins=[cck_i],
                                              outs=[cck_o], replica_groups=RG), 1)
                s_cc.inc(g.collective_compute("AllGather", ALU.bypass, ins=[ccv_i],
                                              outs=[ccv_o], replica_groups=RG), 1)
                g.wait_ge(s_cc.h, s_cc.n)
            GPS(_)
            cc_done = s_cc.n
            if stage == "A":
                return

            for t in range(NT):
                barrier()
                if L1:
                    u1 = unspill(rows[0][:], ("m", l, t))
                    u2 = unspill(rows[5][:], ("qscT", l, t))
                    u3 = unspill(rows[4][:], ("mx", l, t))
                    vpre = s_v.n

                    def _(tn, u1=u1, u2=u2, u3=u3, vpre=vpre):
                        tn.wait_ge(sU.h, max(u1, u2, u3))
                        tn.wait_ge(s_v.h, vpre)
                        tn.matmul(pw[4][:], onesrow_f[:], rows[0][:], start=True,
                                  stop=True)
                        s_t.inc(tn.matmul(pw[5][:], onesrow_f[:], rows[5][:],
                                          start=True, stop=True))
                    TEN(_)
                    t1 = s_t.n

                    def _(v, t1=t1):
                        v.wait_ge(s_t.h, t1)
                        v.tensor_copy(out=bct[:, 0:T], in_=pw[4][:])
                        s_v.inc(v.tensor_copy(out=bct[:, 2 * T:3 * T], in_=pw[5][:]))
                    VEC(_)
                    bcq = s_v.n
                    for f in range(DC):
                        slot = f % 2
                        my = [0]
                        vpre = s_v.n

                        def _(sy, f=f, slot=slot, my=my, vpre=vpre):
                            sy.wait_ge(s_v.h, vpre)
                            my[0] = sX[slot].inc(sy.dma_start(
                                out=xck[:, slot * T:(slot + 1) * T],
                                in_=xsrc[f * 128:(f + 1) * 128, t * T:(t + 1) * T]), 16)
                        SYN(_)

                        def _(v, f=f, slot=slot, my=my, bcq=bcq):
                            v.wait_ge(sX[slot].h, my[0])
                            v.wait_ge(s_v.h, bcq)
                            v.tensor_sub(out=ftmp[:, slot * T:(slot + 1) * T],
                                         in0=xck[:, slot * T:(slot + 1) * T],
                                         in1=bct[:, 0:T])
                            v.tensor_mul(out=qi8[:, slot * T:(slot + 1) * T],
                                         in0=ftmp[:, slot * T:(slot + 1) * T],
                                         in1=bct[:, 2 * T:3 * T])
                            s_v.inc(v.tensor_copy(out=qg[:, f * T:(f + 1) * T],
                                                  in_=qi8[:, slot * T:(slot + 1) * T]))
                        VEC(_)
                    rhs = qg
                    def _(v):
                        v.tensor_scalar(out=rows[0][:], in0=rows[4][:],
                                        scalar1=wmean[0:1, 1:2], scalar2=None,
                                        op0=ALU.mult)
                        s_v.inc(v.tensor_scalar(out=rows[2][:], in0=rows[4][:],
                                                scalar1=wmean[0:1, 2:3], scalar2=None,
                                                op0=ALU.mult))
                    VEC(_)
                    v2 = s_v.n

                    def _(tn, v2=v2):
                        tn.wait_ge(s_v.h, v2)
                        tn.matmul(pw[4][:], onesrow_f[:], rows[0][:], start=True,
                                  stop=True)
                        s_t.inc(tn.matmul(pw[5][:], onesrow_f[:], rows[2][:],
                                          start=True, stop=True))
                    TEN(_)
                    t2 = s_t.n

                    def _(v, t2=t2):
                        v.wait_ge(s_t.h, t2)
                        v.tensor_copy(out=bct[:, 0:T], in_=pw[4][:])
                        v.tensor_copy(out=bct[:, T:2 * T], in_=pw[5][:])
                        v.memset(mrun[:, 0:T], -1e30)
                        s_v.inc(v.memset(mrun[:, T:2 * T], 1e30))
                    VEC(_)
                    cgu = s_v.n

                    pump_until(1)

                    def _(g):
                        g.wait_ge(s_mk.h, max(tern_marker.get(0, 0),
                                              tern_marker.get(1, 0)))
                    GPS(_)
                else:
                    rhs = xn[t]
                    cgu = s_v.n

                gu_free = [0, 0, 0, 0]
                for sl in range(4):
                    gu_free[sl] = war_free[sl]
                for ic in range(IC):
                    gsl = ic % 2
                    usl = 2 + (ic % 2)
                    mg = [0]
                    mu = [0]

                    def _(g, ic=ic, gsl=gsl, usl=usl, mg=mg, mu=mu,
                          f1=gu_free[gsl], f2=gu_free[usl]):
                        if f1:
                            g.wait_ge(s_t.h, f1)
                        mg[0] = sW[gsl].inc(g.dma_start(out=war[:, gsl * D:(gsl + 1) * D],
                                                        in_=wg_t[ic]), 16)
                        if f2:
                            g.wait_ge(s_t.h, f2)
                        mu[0] = sW[usl].inc(g.dma_start(out=war[:, usl * D:(usl + 1) * D],
                                                        in_=wu_t[ic]), 16)
                    GPS(_)

                    pg = ic % 2
                    pu = ic % 2
                    gb = pw[0] if pg == 0 else pw[2]
                    ub = pw[1] if pu == 0 else pw[3]

                    def _(tn, ic=ic, gsl=gsl, usl=usl, mg=mg, mu=mu, gb=gb, ub=ub,
                          cgu=cgu, t=t, ef=evac_free):
                        tn.wait_ge(sW[gsl].h, mg[0])
                        if ic == 0:
                            tn.wait_ge(s_v.h, cgu)
                        if ic >= 2:
                            tn.wait_ge(s_v.h, ef[ic % 2])
                            tn.wait_ge(s_s.h, ef_s[ic % 2])
                        for dm in range(DC):
                            ins = tn.matmul(gb[:], wsl(gsl, dm), rhs[:, dm * T:(dm + 1) * T],
                                            start=(dm == 0),
                                            stop=(L1 and dm == DC - 1),
                                            skip_group_check=True)
                        if not L1:
                            ins = tn.matmul(gb[:], m128row[:], srow_bf[t][:],
                                            start=False, stop=True, skip_group_check=True)
                        s_t.inc(ins)
                        gu_free[gsl] = s_t.n
                        tn.wait_ge(sW[usl].h, mu[0])
                        for dm in range(DC):
                            ins = tn.matmul(ub[:], wsl(usl, dm), rhs[:, dm * T:(dm + 1) * T],
                                            start=(dm == 0),
                                            stop=(L1 and dm == DC - 1),
                                            skip_group_check=True)
                        if not L1:
                            ins = tn.matmul(ub[:], m128row[:], srow_bf[t][:],
                                            start=False, stop=True, skip_group_check=True)
                        s_t.inc(ins)
                        gu_free[usl] = s_t.n
                    TEN(_)
                    t1 = s_t.n

                    if not L1:
                        def _(sc, ic=ic, t1=t1, gb=gb):
                            sc.wait_ge(s_t.h, t1)
                            s_s.inc(sc.activation(out=bfs[:, 0:T], in_=gb[:],
                                                  func=AF.Silu, bias=0.0,
                                                  scale=gsc[:, ic + 1:ic + 2]))
                        SCA(_)
                        ef_s[ic % 2] = s_s.n
                        s2 = s_s.n

                        def _(v, ic=ic, t1=t1, s2=s2, ub=ub):
                            v.wait_ge(s_t.h, t1)
                            v.tensor_scalar(out=bfs[:, T:2 * T], in0=ub[:],
                                            scalar1=usc[:, ic + 1:ic + 2], scalar2=None,
                                            op0=ALU.mult)
                            v.wait_ge(s_s.h, s2)
                            s_v.inc(v.tensor_mul(out=har[:, ic * T:(ic + 1) * T],
                                                 in0=bfs[:, 0:T], in1=bfs[:, T:2 * T]))
                        VEC(_)
                        evac_free[ic % 2] = s_v.n
                        v2 = s_v.n

                        def _(tn, ic=ic, v2=v2):
                            tn.wait_ge(s_v.h, v2)
                            s_t.inc(tn.matmul(pr1[:], onescol_bf[:],
                                              har[:, ic * T:(ic + 1) * T],
                                              start=(ic == 0), stop=(ic == IC - 1),
                                              skip_group_check=True))
                        TEN(_)
                    else:
                        def _(v, ic=ic, t1=t1, gb=gb):
                            v.wait_ge(s_t.h, t1)
                            s_v.inc(v.tensor_mul(out=ftmp[:, 0:T], in0=gb[:],
                                                 in1=bct[:, 0:T]))
                        VEC(_)
                        v1 = s_v.n

                        def _(sc, v1=v1):
                            sc.wait_ge(s_v.h, v1)
                            s_s.inc(sc.activation(out=bfs[:, 0:T], in_=ftmp[:, 0:T],
                                                  func=AF.Silu))
                        SCA(_)
                        ef_s[ic % 2] = s_s.n
                        s2 = s_s.n

                        def _(v, ic=ic, s2=s2, ub=ub):
                            v.tensor_mul(out=ftmp[:, T:2 * T], in0=ub[:],
                                         in1=bct[:, T:2 * T])
                            v.wait_ge(s_s.h, s2)
                            v.tensor_mul(out=har[:, ic * T:(ic + 1) * T],
                                         in0=ftmp[:, T:2 * T], in1=bfs[:, 0:T])
                            v.tensor_mul(out=bfs[:, 2 * T:3 * T],
                                         in0=har[:, ic * T:(ic + 1) * T],
                                         in1=har[:, ic * T:(ic + 1) * T])
                            v.scalar_tensor_tensor(out=ftmp[:, 0:T],
                                                   in0=har[:, ic * T:(ic + 1) * T],
                                                   scalar=gam[:, ic + 1:ic + 2],
                                                   in1=mrun[:, 0:T],
                                                   op0=ALU.mult, op1=ALU.max)
                            v.tensor_copy(out=mrun[:, 0:T], in_=ftmp[:, 0:T])
                            v.scalar_tensor_tensor(out=ftmp[:, 0:T],
                                                   in0=har[:, ic * T:(ic + 1) * T],
                                                   scalar=gam[:, ic + 1:ic + 2],
                                                   in1=mrun[:, T:2 * T],
                                                   op0=ALU.mult, op1=ALU.min)
                            s_v.inc(v.tensor_copy(out=mrun[:, T:2 * T], in_=ftmp[:, 0:T]))
                        VEC(_)
                        evac_free[ic % 2] = s_v.n
                        v2 = s_v.n

                        def _(tn, ic=ic, v2=v2):
                            tn.wait_ge(s_v.h, v2)
                            s_t.inc(tn.matmul(pr1[:], onescol_bf[:], bfs[:, 2 * T:3 * T],
                                              start=(ic == 0), stop=(ic == IC - 1),
                                              skip_group_check=True))
                        TEN(_)
                    pump(4)
                hs_t = s_t.n
                hs_v = s_v.n

                if not L1:
                    def _(v, hs_t=hs_t, t_=t):
                        v.wait_ge(s_t.h, hs_t)
                        s_v.inc(v.tensor_copy(out=srow_bf[t_][:], in_=pr1[:]))
                    VEC(_)
                else:
                    def _(v, hs_t=hs_t):
                        v.wait_ge(s_t.h, hs_t)
                        s_v.inc(v.tensor_scalar(out=rows[3][:], in0=pr1[:],
                                                scalar1=1.0 / II, scalar2=1e-6,
                                                op0=ALU.mult, op1=ALU.add))
                    VEC(_)
                    rss_v = s_v.n

                    def _(sc, rss_v=rss_v):
                        sc.wait_ge(s_v.h, rss_v)
                        s_s.inc(sc.activation(out=rows[6][:], in_=rows[3][:],
                                              func=AF.Sqrt))
                    SCA(_)
                    rstd_ss = s_s.n

                    def _(v, rstd_ss=rstd_ss):
                        v.wait_ge(s_s.h, rstd_ss)
                        s_v.inc(v.reciprocal(out=rows[2][:], in_=rows[6][:]))
                    VEC(_)
                    rstd_s = s_s.n

                    def _(v, hs_v=hs_v):
                        v.wait_ge(s_v.h, hs_v)
                        v.tensor_scalar(out=mrun[:, T:2 * T], in0=mrun[:, T:2 * T],
                                        scalar1=-1.0, scalar2=None, op0=ALU.mult)
                        s_v.inc(v.tensor_tensor(out=mrun[:, 0:T], in0=mrun[:, 0:T],
                                                in1=mrun[:, T:2 * T], op=ALU.max))
                    VEC(_)
                    mv = s_v.n
                    for tb in range(4):
                        def _(tn, tb=tb, mv=mv):
                            tn.wait_ge(s_v.h, mv)
                            s_t.inc(tn.transpose(pw[tb][:, 0:128],
                                                 mrun[:, tb * 128:(tb + 1) * 128],
                                                 ident[:]))
                        TEN(_)
                    t2 = s_t.n
                    mcbase = mcol_ctr[0]
                    mcol_ctr[0] += 4

                    def _(v, t2=t2, mcbase=mcbase):
                        v.wait_ge(s_t.h, t2)
                        for tb in range(4):
                            ins = v.reduce_max(out=mcols[:, mcbase + tb:mcbase + tb + 1],
                                               in_=pw[tb][:, 0:128], axis=AX.X)
                        s_v.inc(ins)
                    VEC(_)
                    v2 = s_v.n
                    for tb in range(4):
                        def _(tn, tb=tb, v2=v2, mcbase=mcbase):
                            tn.wait_ge(s_v.h, v2)
                            s_t.inc(tn.matmul(pr0[0:1, tb * 128:(tb + 1) * 128],
                                              mcols[:, mcbase + tb:mcbase + tb + 1],
                                              ident[:], start=True, stop=True,
                                              skip_group_check=True))
                        TEN(_)
                    t3 = s_t.n

                    def _(v, t3=t3, rstd_s=rstd_s):
                        v.wait_ge(s_t.h, t3)
                        v.wait_ge(s_s.h, rstd_s)
                        v.tensor_copy(out=rows[3][:], in_=pr0[:])
                        v.tensor_mul(out=rows[4][:], in0=rows[3][:], in1=rows[2][:])
                        v.tensor_scalar(out=rows[4][:], in0=rows[4][:], scalar1=1e-5,
                                        scalar2=None, op0=ALU.max)
                        v.reciprocal(out=rows[5][:], in_=rows[4][:])
                        v.scalar_tensor_tensor(out=rows[5][:], in0=rows[5][:],
                                               scalar=127.0, in1=rows[2][:],
                                               op0=ALU.mult, op1=ALU.mult)
                        s_v.inc(v.tensor_scalar(out=rows[6][:], in0=rows[4][:],
                                                scalar1=wmean[0:1, 3:4], scalar2=None,
                                                op0=ALU.mult))
                    VEC(_)
                    v3 = s_v.n

                    def _(tn, v3=v3):
                        tn.wait_ge(s_v.h, v3)
                        tn.matmul(pw[4][:], onesrow_f[:], rows[5][:], start=True,
                                  stop=True)
                        s_t.inc(tn.matmul(pw[5][:], onesrow_f[:], rows[6][:],
                                          start=True, stop=True))
                    TEN(_)
                    t4 = s_t.n

                    def _(v, t4=t4):
                        v.wait_ge(s_t.h, t4)
                        v.tensor_copy(out=bct[:, 2 * T:3 * T], in_=pw[4][:])
                        s_v.inc(v.tensor_copy(out=bct[:, 3 * T:4 * T], in_=pw[5][:]))
                    VEC(_)
                    qsch_v = s_v.n

                    for ic2 in range(IC):
                        slot = ic2 % 2

                        def _(v, ic2=ic2, slot=slot, qsch_v=qsch_v):
                            if ic2 == 0:
                                v.wait_ge(s_v.h, qsch_v)
                            v.scalar_tensor_tensor(out=qi8[:, slot * T:(slot + 1) * T],
                                                   in0=har[:, ic2 * T:(ic2 + 1) * T],
                                                   scalar=gam[:, ic2 + 1:ic2 + 2],
                                                   in1=bct[:, 2 * T:3 * T],
                                                   op0=ALU.mult, op1=ALU.mult)
                            s_v.inc(v.tensor_copy(out=har[:, ic2 * T:(ic2 + 1) * T],
                                                  in_=qi8[:, slot * T:(slot + 1) * T]))
                        VEC(_)
                    pump_until(2)

                    def _(g):
                        g.wait_ge(s_mk.h, tern_marker.get(2, 0))
                    GPS(_)
                qdone_v = s_v.n

                dn_free = list(gu_free)
                for hf in range(DC):
                    pd = pw[4] if hf % 2 == 0 else pw[5]
                    for sub in range(4):
                        my = [0]
                        free = dn_free[sub]

                        def _(g, hf=hf, sub=sub, my=my, free=free):
                            if free:
                                g.wait_ge(s_t.h, free)
                            my[0] = sW[sub].inc(g.dma_start(
                                out=war[:, sub * D:(sub + 1) * D],
                                in_=wd_t[hf, :, sub * D:(sub + 1) * D]), 16)
                        GPS(_)

                        def _(tn, hf=hf, sub=sub, my=my, pd=pd, qdone_v=qdone_v, t=t,
                              dse=down_evac):
                            tn.wait_ge(sW[sub].h, my[0])
                            if sub == 0:
                                tn.wait_ge(s_v.h, qdone_v)
                                if hf >= 2 and dse[hf % 2] is not None:
                                    tn.wait_ge(dse[hf % 2][0].h, dse[hf % 2][1])
                                    tn.wait_ge(s_s.h, dse_s[hf % 2])
                            for k in range(DC):
                                ic2 = sub * DC + k
                                ins = tn.matmul(pd[:], wsl(sub, k),
                                                har[:, ic2 * T:(ic2 + 1) * T],
                                                start=(ic2 == 0),
                                                stop=(L1 and ic2 == IC - 1),
                                                skip_group_check=True)
                            if sub == 3 and not L1:
                                ins = tn.matmul(pd[:], m128row[:], srow_bf[t][:],
                                                start=False, stop=True,
                                                skip_group_check=True)
                            s_t.inc(ins)
                            dn_free[sub] = s_t.n
                        TEN(_)
                    t1 = s_t.n

                    if not L1:
                        def _(sc, hf=hf, t1=t1, pd=pd):
                            sc.wait_ge(s_t.h, t1)
                            s_s.inc(sc.activation(out=qg[:, hf * T:(hf + 1) * T],
                                                  in_=pd[:], func=AF.Silu, bias=0.0,
                                                  scale=dsc[:, hf + 1:hf + 2]))
                        SCA(_)
                        down_evac[hf % 2] = (s_s, s_s.n)
                        dse_s[hf % 2] = s_s.n
                    else:
                        def _(v, hf=hf, t1=t1, pd=pd):
                            v.wait_ge(s_t.h, t1)
                            s_v.inc(v.tensor_mul(out=ftmp[:, 0:T], in0=pd[:],
                                                 in1=bct[:, 3 * T:4 * T]))
                        VEC(_)
                        down_evac[hf % 2] = (s_v, s_v.n)
                        v1 = s_v.n

                        def _(sc, hf=hf, v1=v1):
                            sc.wait_ge(s_v.h, v1)
                            s_s.inc(sc.activation(out=qg[:, hf * T:(hf + 1) * T],
                                                  in_=ftmp[:, 0:T], func=AF.Silu))
                        SCA(_)
                        dse_s[hf % 2] = s_s.n
                    pump(4)
                for sl in range(4):
                    war_free[sl] = dn_free[sl]
                g_done = s_s.n
                if stage == "C":
                    return

                barrier()
                my = [0]

                def _(sy, cc_done=cc_done, my=my):
                    sy.wait_ge(s_cc.h, cc_done)
                    sCM.inc(sy.dma_start(out=ccm_sb[0:1, 0:16],
                                         in_=cck_o[D:D + 1, 0:16]), 16)
                    my[0] = sCM.inc(sy.dma_start(out=ccm_sb[0:1, 16:32],
                                                 in_=cck_o[2 * D + 16:2 * D + 17, 0:16]), 16)
                SYN(_)
                sy1 = sCM.n

                def _(v, sy1=sy1):
                    v.wait_ge(sCM.h, sy1)
                    s_v.inc(v.tensor_tensor(out=knm[0:1, 16:32], in0=ccm_sb[0:1, 0:16],
                                            in1=ccm_sb[0:1, 16:32], op=ALU.max))
                VEC(_)
                knall_v = s_v.n

                for h in range(NH):
                    vpre, spre = s_v.n, s_s.n

                    def _(tn, vpre=vpre, spre=spre):
                        tn.wait_ge(s_v.h, vpre)
                        tn.wait_ge(s_s.h, spre)
                    TEN(_)

                    slot, gcnt = load_war(wqk[h])
                    qs = h % 2

                    def _(tn, slot=slot, gcnt=gcnt, t=t):
                        tn.wait_ge(sW[slot].h, gcnt)
                        for dm in range(DC):
                            ins = tn.matmul(pw[3][:], wsl(slot, dm),
                                            xn[t][:, dm * T:(dm + 1) * T],
                                            start=(dm == 0), stop=(dm == DC - 1),
                                            skip_group_check=True)
                        s_t.inc(ins)
                    TEN(_)
                    war_free[slot] = s_t.n
                    t1 = s_t.n

                    def _(v, h=h, t1=t1, qs=qs, qf=qh_free[qs]):
                        v.wait_ge(s_t.h, max(t1, qf))
                        b = qkvb[:, h + 1:h + 2] if L1 else 0.0
                        v.tensor_scalar(out=qhb[:, qs * T:(qs + 1) * T],
                                        in0=pw[3][:], scalar1=b, scalar2=None,
                                        op0=ALU.add)
                        s_v.inc(v.tensor_mul(out=bfs[:, 0:T],
                                             in0=qhb[:, qs * T:(qs + 1) * T],
                                             in1=qhb[:, qs * T:(qs + 1) * T]))
                    VEC(_)
                    v1 = s_v.n

                    def _(tn, v1=v1):
                        tn.wait_ge(s_v.h, v1)
                        s_t.inc(tn.matmul(pr0[:], onescol_bf[:], bfs[:, 0:T],
                                          start=True, stop=True, skip_group_check=True))
                    TEN(_)
                    t2 = s_t.n

                    def _(v, h=h, t2=t2, knall_v=knall_v):
                        v.wait_ge(s_t.h, t2)
                        v.wait_ge(s_v.h, knall_v)
                        c = s_v.inc(v.reduce_max(out=sc1[0:1, 1:2], in_=pr0[:],
                                                 axis=AX.X))
                        v.wait_ge(s_v.h, c)
                        s_v.inc(v.tensor_tensor(out=sc1[0:1, 2:3], in0=sc1[0:1, 1:2],
                                                in1=knm[0:1, 16 + h:17 + h],
                                                op=ALU.mult))
                    VEC(_)
                    v2 = s_v.n

                    def _(sc, v2=v2):
                        sc.wait_ge(s_v.h, v2)
                        s_s.inc(sc.activation(out=sc1[0:1, 3:4], in_=sc1[0:1, 2:3],
                                              func=AF.Sqrt))
                    SCA(_)
                    s2 = s_s.n

                    def _(v, s2=s2):
                        v.wait_ge(s_s.h, s2)
                        s_v.inc(v.tensor_scalar(out=sc1[0:1, 4:5], in0=sc1[0:1, 3:4],
                                                scalar1=-SCALE, scalar2=None,
                                                op0=ALU.mult))
                    VEC(_)
                    v3 = s_v.n

                    def _(tn, v3=v3):
                        tn.wait_ge(s_v.h, v3)
                        s_t.inc(tn.matmul(pw[5][:, 0:1], onesrow_f[:], sc1[0:1, 4:5],
                                          start=True, stop=True))
                    TEN(_)
                    t3 = s_t.n

                    def _(v, t3=t3):
                        v.wait_ge(s_t.h, t3)
                        s_v.inc(v.tensor_copy(out=biascol[:], in_=pw[5][:, 0:1]))
                    VEC(_)
                    bias_v = s_v.n

                    first = [True]
                    for r in range(2):
                        myk = [0]
                        myv_ = [0]

                        def _(sy, h=h, r=r, myk=myk, myv_=myv_, cc_done=cc_done,
                              kf=kv_free[r]):
                            sy.wait_ge(s_cc.h, cc_done)
                            if kf:
                                sy.wait_ge(s_t.h, kf)
                            myk[0] = sKB[r].inc(sy.dma_start(
                                out=kbuf[:, r * TPC:(r + 1) * TPC],
                                in_=cck_o[r * (D + 16) + h * 128:
                                          r * (D + 16) + (h + 1) * 128, :]), 16)
                            inap = ccv_o[r * TPC:(r + 1) * TPC,
                                         h * 128:(h + 1) * 128].rearrange(
                                "(b p) d -> p b d", p=128)
                            outap = vbuf[:, r * TPC:(r + 1) * TPC].rearrange(
                                "p (b d) -> p b d", d=128)
                            myv_[0] = sVB[r].inc(sy.dma_start(out=outap, in_=inap), 16)
                        SYN(_)

                        for kb2 in range(4):
                            gp = r * 4 + kb2
                            ps = gp % 2
                            mym = [0]
                            if L1:
                                def _(sy, t=t, gp=gp, kb2=kb2, r=r, mym=mym,
                                      mf=mask_free[0]):
                                    if mf:
                                        sy.wait_ge(s_v.h, mf)
                                    gkb = (r * 4 + kb2) * 2
                                    sM[0].inc(sy.dma_start(
                                        out=mbuf[:, 0:T],
                                        in_=mask_d[t * 16 + gkb]), 16)
                                    mym[0] = sM[0].inc(sy.dma_start(
                                        out=mbuf[:, T:2 * T],
                                        in_=mask_d[t * 16 + gkb + 1]), 16)
                                SYN(_)

                            def _(tn, kb2=kb2, r=r, myk=myk, qs=qs,
                                  ef=exp_free[0]):
                                tn.wait_ge(sKB[r].h, myk[0])
                                if ef:
                                    tn.wait_ge(s_s.h, ef)
                                for j in range(2):
                                    kb = kb2 * 2 + j
                                    s_t.inc(tn.matmul(
                                        pws[:, j * T:(j + 1) * T],
                                        kbuf[:, r * TPC + kb * 128:
                                             r * TPC + (kb + 1) * 128],
                                        qhb[:, qs * T:(qs + 1) * T],
                                        start=True, stop=True,
                                        skip_group_check=True))
                            TEN(_)
                            t4 = s_t.n

                            def _(sc, ps=ps, t4=t4,
                                  bias_v=bias_v, pf=pv_free[ps]):
                                sc.wait_ge(s_t.h, t4)
                                sc.wait_ge(s_v.h, bias_v)
                                if pf:
                                    sc.wait_ge(s_t.h, pf)
                                ins = sc.activation(
                                    out=pbuf[:, ps * 2 * T:(ps + 1) * 2 * T],
                                    in_=pws[:, 0:2 * T], func=AF.Exp,
                                    bias=biascol[:], scale=sccol[:])
                                exp_free[0] = s_s.inc(ins)
                            SCA(_)
                            s3 = s_s.n

                            if L1:
                                def _(v, ps=ps, s3=s3, mym=mym):
                                    v.wait_ge(s_s.h, s3)
                                    v.wait_ge(sM[0].h, mym[0])
                                    ins = v.tensor_tensor(
                                        out=pbuf[:, ps * 2 * T:(ps + 1) * 2 * T],
                                        in0=pbuf[:, ps * 2 * T:(ps + 1) * 2 * T],
                                        in1=mbuf[:, 0:2 * T],
                                        op=ALU.mult)
                                    mask_free[0] = s_v.inc(ins)
                                VEC(_)
                                pvw = ("v", s_v.n)
                            else:
                                pvw = ("s", s3)

                            def _(tn, kb2=kb2, r=r, ps=ps, myv_=myv_, pvw=pvw,
                                  first=first, gp=gp):
                                if pvw[0] == "v":
                                    tn.wait_ge(s_v.h, pvw[1])
                                else:
                                    tn.wait_ge(s_s.h, pvw[1])
                                tn.wait_ge(sVB[r].h, myv_[0])
                                ins = None
                                for j in range(2):
                                    kb = kb2 * 2 + j
                                    st = first[0] and j == 0
                                    sp = gp == 7 and j == 1
                                    tn.matmul(pr1[:], onescol_bf[:],
                                              pbuf[:, (ps * 2 + j) * T:
                                                   (ps * 2 + j + 1) * T],
                                              start=st, stop=sp,
                                              skip_group_check=True)
                                    ins = tn.matmul(
                                        pw[4][:],
                                        vbuf[:, r * TPC + kb * 128:
                                             r * TPC + (kb + 1) * 128],
                                        pbuf[:, (ps * 2 + j) * T:
                                             (ps * 2 + j + 1) * T],
                                        start=st, stop=sp,
                                        skip_group_check=True)
                                pv_free[ps] = s_t.inc(ins)
                                first[0] = False
                            TEN(_)
                        kv_free[r] = s_t.n
                    qh_free[qs] = s_t.n
                    pvdone = s_t.n

                    def _(v, pvdone=pvdone):
                        v.wait_ge(s_t.h, pvdone)
                        s_v.inc(v.reciprocal(out=rows[2][:], in_=pr1[:]))
                    VEC(_)
                    v4 = s_v.n

                    def _(tn, v4=v4):
                        tn.wait_ge(s_v.h, v4)
                        s_t.inc(tn.matmul(pw[5][:], onesrow_f[:], rows[2][:],
                                          start=True, stop=True))
                    TEN(_)
                    t5 = s_t.n

                    def _(v, h=h, t5=t5):
                        v.wait_ge(s_t.h, t5)
                        v.tensor_copy(out=bct[:, 3 * T:4 * T], in_=pw[5][:])
                        s_v.inc(v.tensor_mul(out=har[:, (16 + h) * T:(17 + h) * T],
                                             in0=pw[4][:], in1=bct[:, 3 * T:4 * T]))
                    VEC(_)
                    pump(2)
                ao_done = s_v.n

                for mo in range(DC):
                    slot, gcnt = load_war(wop[mo])

                    def _(tn, slot=slot, gcnt=gcnt, ao_done=ao_done, mo=mo,
                          spre=s_v.n):
                        tn.wait_ge(sW[slot].h, gcnt)
                        if mo == 0:
                            tn.wait_ge(s_v.h, ao_done)
                        else:
                            tn.wait_ge(s_v.h, spre)
                        for dm in range(DC):
                            ins = tn.matmul(pw[3][:], wsl(slot, dm),
                                            har[:, (16 + dm) * T:(17 + dm) * T],
                                            start=(dm == 0), stop=(dm == DC - 1),
                                            skip_group_check=True)
                        s_t.inc(ins)
                    TEN(_)
                    war_free[slot] = s_t.n
                    t1 = s_t.n

                    def _(v, mo=mo, t1=t1):
                        v.wait_ge(s_t.h, t1)
                        b = obc[:, mo + 1:mo + 2] if L1 else 0.0
                        s_v.inc(v.tensor_scalar(out=har[:, (32 + mo) * T:(33 + mo) * T],
                                                in0=pw[3][:], scalar1=b, scalar2=None,
                                                op0=ALU.add))
                    VEC(_)
                    pump(4)
                s_done = s_v.n
                if stage == "D":
                    return

                for f in range(DC):
                    slot = f % 2
                    my = [0]

                    def _(sy, f=f, slot=slot, my=my, uf=upd_free[slot]):
                        if uf:
                            sy.wait_ge(sUP[slot].h, uf)
                        my[0] = sX[slot].inc(sy.dma_start(
                            out=xck[:, slot * T:(slot + 1) * T],
                            in_=xsrc[f * 128:(f + 1) * 128, t * T:(t + 1) * T]), 16)
                    SYN(_)

                    def _(v, f=f, slot=slot, my=my, s_done=s_done, g_done=g_done):
                        v.wait_ge(s_s.h, g_done)
                        v.wait_ge(s_v.h, s_done)
                        v.wait_ge(sX[slot].h, my[0])
                        if stage == "dumps" and l == n_layers - 1:
                            s_v.inc(v.tensor_copy(out=xck[:, slot * T:(slot + 1) * T],
                                                  in_=har[:, (32 + f) * T:(33 + f) * T]))
                        elif stage == "dumpg" and l == n_layers - 1:
                            s_v.inc(v.tensor_copy(out=xck[:, slot * T:(slot + 1) * T],
                                                  in_=qg[:, f * T:(f + 1) * T]))
                        else:
                            v.tensor_mul(out=ftmp[:, slot * T:(slot + 1) * T],
                                         in0=har[:, (32 + f) * T:(33 + f) * T],
                                         in1=qg[:, f * T:(f + 1) * T])
                            s_v.inc(v.tensor_add(out=xck[:, slot * T:(slot + 1) * T],
                                                 in0=xck[:, slot * T:(slot + 1) * T],
                                                 in1=ftmp[:, slot * T:(slot + 1) * T]))
                    VEC(_)
                    v1 = s_v.n

                    def _(sy, f=f, slot=slot, v1=v1, t=t):
                        sy.wait_ge(s_v.h, v1)
                        upd_free[slot] = sUP[slot].inc(sy.dma_start(
                            out=xdst[f * 128:(f + 1) * 128, t * T:(t + 1) * T],
                            in_=xck[:, slot * T:(slot + 1) * T]), 16)
                    SYN(_)
                    pump(12)
            layer_e_marker[l] = marker([sUP[0], sUP[1]])

        mcol_ctr = [8]
        evac_free = [0, 0]
        ef_s = [0, 0]
        down_evac = [None, None]
        dse_s = [0, 0]

        for l in range(n_layers):
            emit_layer(l)
        pump_flush()

        if stage == "dumpt":
            pre = (s_mk.n, sUP[0].n, sUP[1].n)

            def _(g, pre=pre):
                g.wait_ge(s_mk.h, pre[0])
                g.wait_ge(sUP[0].h, pre[1])
                g.wait_ge(sUP[1].h, pre[2])
                for mo in range(16):
                    sKN.inc(g.dma_start(out=out[mo * 128:(mo + 1) * 128, :],
                                        in_=tg_d[mo, :, 0:TPC]), 16)
                for i in range(3):
                    sKN.inc(g.dma_start(out=out[0:128, 1020 + i:1021 + i],
                                        in_=swsp[i][:],
                                        allow_slow_non_contiguous=True), 16)
                sKN.inc(g.dma_start(out=out[128:129, 1008:1016],
                                    in_=wmean[0:1, :]), 16)
                sKN.inc(g.dma_start(out=out[129:130, 1008:1016],
                                    in_=sc1[0:1, :]), 16)
                sKN.inc(g.dma_start(out=out[130:258, 1016:1017],
                                    in_=acc[:, 0:1],
                                    allow_slow_non_contiguous=True), 16)
            GPS(_)
            marker([sKN])

        marker([sKS[0], sKS[1], sVS[0], sVS[1], sUP[0], sUP[1], sSP, sTS, sKN])
        mk_fin = s_mk.n

        def _(sy):
            sy.wait_ge(s_mk.h, mk_fin)
        SYN(_)

    return nc, sbuf_left


def _tile_lhsT(w):
    K, M = w.shape
    Ko, Mo = K // 128, M // 128
    return np.ascontiguousarray(
        w.reshape(Ko, 128, Mo, 128).transpose(2, 1, 0, 3).reshape(Mo, 128, K))


def _u8(wi32):
    return np.ascontiguousarray(wi32.view(np.uint8)[..., 0::4])


def _padcol(a):
    return np.ascontiguousarray(
        np.concatenate([np.zeros((a.shape[0], 1), a.dtype), a], axis=1))


def make_inputs(kw):
    f32 = np.float32
    x = np.asarray(kw["L0_attn_x"], f32)
    w0qkv = np.asarray(kw["L0_attn_w_qkv"], f32)
    w0o = np.asarray(kw["L0_attn_w_out"], f32)
    g0 = _u8(np.ascontiguousarray(np.asarray(kw["L0_ffn_gate_w"], np.int32)))
    u0 = _u8(np.ascontiguousarray(np.asarray(kw["L0_ffn_up_w"], np.int32)))
    d0 = _u8(np.ascontiguousarray(np.asarray(kw["L0_ffn_down_w"], np.int32)))
    gs = np.asarray(kw["L0_ffn_gate_scale"], f32)
    us = np.asarray(kw["L0_ffn_up_scale"], f32)
    ds = np.asarray(kw["L0_ffn_down_scale"], f32)
    w1qkv = np.asarray(kw["L1_attn_attn_qkvw"], f32)
    qkvb = np.asarray(kw["L1_attn_attn_qkvb"], f32)
    w1o = np.asarray(kw["L1_attn_attn_ow"], f32)
    ob = np.asarray(kw["L1_attn_attn_ob"], f32)
    wg1 = np.asarray(kw["L1_ffn_w_gate"], f32)
    wu1 = np.asarray(kw["L1_ffn_w_up"], f32)
    wd1 = np.asarray(kw["L1_ffn_w_down"], f32)
    gam = np.asarray(kw["L1_ffn_ln_weight"], f32)

    shared = {
        "w0qk": _tile_lhsT(w0qkv[:, 0:4096]),
        "w0v": np.ascontiguousarray(w0qkv[:, 4096:6144]),
        "w0o": _tile_lhsT(w0o),
        "w0g": _tile_lhsT(np.ascontiguousarray(g0.T)),
        "w0u": _tile_lhsT(np.ascontiguousarray(u0.T)),
        "w0d": _tile_lhsT(np.ascontiguousarray(d0.T)),
        "gs_col": _padcol(np.ascontiguousarray(gs.reshape(64, 128).T)),
        "us_col": _padcol(np.ascontiguousarray(us.reshape(64, 128).T)),
        "ds_col": _padcol(np.ascontiguousarray(ds.reshape(16, 128).T)),
        "w1qk": _tile_lhsT(w1qkv[:, 0:4096]),
        "w1v": np.ascontiguousarray(w1qkv[:, 4096:6144]),
        "w1o": _tile_lhsT(w1o),
        "w1g": _tile_lhsT(np.ascontiguousarray(wg1.T)),
        "w1u": _tile_lhsT(np.ascontiguousarray(wu1.T)),
        "w1d": _tile_lhsT(np.ascontiguousarray(wd1.T)),
        "qkvb_col": _padcol(np.ascontiguousarray(qkvb[0:4096].reshape(32, 128).T)),
        "bv_row": np.ascontiguousarray(qkvb[4096:6144].reshape(1, 2048)),
        "ob_col": _padcol(np.ascontiguousarray(ob.reshape(16, 128).T)),
        "gam_col": _padcol(np.ascontiguousarray(gam.reshape(64, 128).T)),
        "ident_in": np.eye(128, dtype=f32),
        "dummy_d": np.zeros((1, 64), f32),
    }
    in_maps = []
    for c in range(8):
        b, half = c // 2, c % 2
        qbase = half * TPC
        m = dict(shared)
        m["x0"] = np.ascontiguousarray(x[b, qbase:qbase + TPC, :].T)
        msk = np.zeros((32, 128, T), dtype=ml_dtypes.bfloat16)
        for t in range(NT):
            for gkb in range(16):
                kpos = gkb * 128 + np.arange(128)[:, None]
                qpos = qbase + t * T + np.arange(T)[None, :]
                msk[t * 16 + gkb] = (kpos <= qpos).astype(ml_dtypes.bfloat16)
        m["mask_d"] = msk
        in_maps.append(m)
    return in_maps


_CACHE = {}


def kernel(**kw):
    if "nc" not in _CACHE:
        _CACHE["nc"] = build()[0]
    nc = _CACHE["nc"]
    in_maps = make_inputs(kw)
    res = run_bass_kernel_spmd(nc, in_maps, core_ids=list(range(8)))
    outp = np.zeros((B, S, D), np.float32)
    for c in range(8):
        b, half = c // 2, c % 2
        outp[b, half * TPC:(half + 1) * TPC, :] = res.results[c]["out"].T
    return outp
